# version 29
# baseline (speedup 1.0000x reference)
from collections import defaultdict
from contextlib import ExitStack

import numpy as np
import concourse.bass as bass
import concourse.mybir as mybir
from concourse.bass_utils import run_bass_kernel_spmd

F32 = mybir.dt.float32
BF16 = mybir.dt.bfloat16
AF = mybir.ActivationFunctionType
ALU = mybir.AluOpType
AX = mybir.AxisListType

MAGIC = float(np.float32(1.5 * 2 ** 23))
NCORES = 8
P = 128


class Cfg:
    def __init__(self, TPC, C):
        self.TPC = TPC
        self.C = C
        self.F = 4 * C
        self.NCC = C // P
        self.NFC = self.F // P
        self.ND6 = 6 * C // P
        self.NTC = TPC // P
        self.TT = min(512, TPC)
        self.NTT = TPC // self.TT


CFG_REAL = Cfg(2048, 1024)

DMA_STREAMS = ["lx", "lw", "lw2", "lw3", "lw4", "lq", "lq2", "lq3", "lq4",
               "lxq", "lxq2", "lh", "lh2", "lf", "lf2",
               "ll", "ll2", "lk", "tw2", "lxo",
               "tx", "tx2", "tw", "th", "tf", "tl", "tl2", "tk", "tk2",
               "lg"]
CC_SEMS = ["cc0", "cc1", "cc2", "cc3", "cc4", "cc5", "cc6", "cc7"]
ENG_SEMS = ["sv", "ss", "sp", "st"]


def build_nc(cfg, phases='faf', dbg=''):
    TPC, C, F = cfg.TPC, cfg.C, cfg.F
    NCC, NFC, NTC, TT, NTT = cfg.NCC, cfg.NFC, cfg.NTC, cfg.TT, cfg.NTT

    nc = bass.Bass("TRN2", target_bir_lowering=False)

    def ein(name, shape, dt=F32):
        return nc.dram_tensor(name, shape, dt, kind="ExternalInput").ap()

    SH1 = F // NCORES
    SH2 = C // NCORES
    x_in = ein("x", [TPC, C])
    w1s_in = ein("w1s", [SH1, C])
    w2s_in = ein("w2s", [SH2, F])
    w1Ts_in = ein("w1Ts", [C, SH1])
    w2Ts_in = ein("w2Ts", [F, SH2])
    b1_in = ein("b1", [P, NFC])
    b2_in = ein("b2", [P, NCC])
    rkvT_in = ein("rkvT", [C, 6 * C])
    owT_in = ein("owT", [2 * C, C])
    tdec_in = ein("tdec", [P, NCC])
    tfir_in = ein("tfir", [P, NCC])
    tdecr_in = ein("tdecr", [P, NCC])
    tfirr_in = ein("tfirr", [P, NCC])
    pfx_in = ein("pfx", [P, NCORES * 4 * NCC])

    out_d = nc.dram_tensor("out", [TPC, C], F32, kind="ExternalOutput").ap()

    w1q_i = nc.dram_tensor("w1q_i", [C, SH1], BF16).ap()
    w1q_o = nc.dram_tensor("w1q_o", [NCORES * C, SH1], BF16).ap()
    w2q_i = nc.dram_tensor("w2q_i", [F, SH2], BF16).ap()
    w2q_o = nc.dram_tensor("w2q_o", [NCORES * F, SH2], BF16).ap()
    N1T_g = max(1, SH1 // P)
    sc_i = nc.dram_tensor("sc_i", [P, N1T_g + 1], F32).ap()
    sc_o = nc.dram_tensor("sc_o", [NCORES * P, N1T_g + 1], F32).ap()
    XQ_d = nc.dram_tensor("XQ_d", [TPC, C], BF16).ap()
    if "h" in dbg:
        H_d = nc.dram_tensor("H_d", [F, TPC], BF16, kind="ExternalOutput").ap()
    else:
        H_d = nc.dram_tensor("H_d", [F, TPC], BF16).ap()
    XQ2_dbg = (nc.dram_tensor("XQ2_dbg", [TPC, C], BF16, kind="ExternalOutput").ap()
               if "q" in dbg else None)
    F1_d = nc.dram_tensor("F1_d", [C, TPC], BF16).ap()
    XO0_d = nc.dram_tensor("XO0_d", [TPC, C], F32).ap()
    XO1_d = nc.dram_tensor("XO1_d", [TPC, C], F32).ap()
    LN2_d = nc.dram_tensor("LN2_d", [TPC, C], BF16).ap()
    KVR_d = nc.dram_tensor("KVR_d", [6 * C, TPC], BF16).ap()
    gmax_i = nc.dram_tensor("gmax_i", [P, 1], F32).ap()
    gmax_o = nc.dram_tensor("gmax_o", [P * NCORES, 1], F32).ap()
    NK = 4 * NCC
    scol_i = nc.dram_tensor("scol_i", [P, NK], F32).ap()
    scol_o = nc.dram_tensor("scol_o", [P * NCORES, NK], F32).ap()

    RG = [list(range(NCORES))]

    counts = defaultdict(int)
    sems = {}
    ctx = ExitStack()
    nc._kb_ctx = ctx
    for sname in DMA_STREAMS + CC_SEMS + ENG_SEMS:
        sems[sname] = ctx.enter_context(nc.semaphore(sname))
    cc_used = [0]
    stats_box = [None]
    xo_info = {"lxo": 0}

    def dma(q, stream, out, in_, serial=True, **kw):
        if kw.get("transpose"):
            serial = True
        if serial and counts[stream] > 0:
            q.wait_ge(sems[stream], counts[stream])
        counts[stream] += 16
        q.dma_start(out=out, in_=in_, **kw).then_inc(sems[stream], 16)
        return counts[stream]

    def cinc(instr, name):
        counts[name] += 1
        instr.then_inc(sems[name], 1)
        return counts[name]

    def W(eng, name, val):
        if val > 0:
            eng.wait_ge(sems[name], val)

    sb = lambda name, shape, dt=F32: ctx.enter_context(nc.sbuf_tensor(name, shape, dt))

    X = sb("X", [P, NTC * C])
    ones_row = sb("ones_row", [1, P])
    magic_col = sb("magic_col", [P, 1])
    nmagic_col = sb("nmagic_col", [P, 1])
    eps_col = sb("eps_col", [P, 1])
    epsq_col = sb("epsq_col", [P, 1])
    grow = sb("grow", [1, P * NCORES])
    tcols = sb("tcols", [P, 4 * NCC])
    wf_col = sb("wf_col", [P, NCC])
    wb_col = sb("wb_col", [P, NCC])
    euf_col = sb("euf_col", [P, NCC])
    eub_col = sb("eub_col", [P, NCC])
    ewf_col = sb("ewf_col", [P, NCC])
    ewb_col = sb("ewb_col", [P, NCC])
    W32 = sb("W32", [P, NK])
    pfx_t = sb("pfx_t", [P, NCORES * NK])
    b1c = sb("b1c", [P, NFC])
    b2c = sb("b2c", [P, NCC])
    swcol1 = sb("swcol1", [P, NFC])
    swcol2 = sb("swcol2", [P, NCC])
    sprod1 = sb("sprod1", [P, NFC])
    bq1f = sb("bq1f", [P, NFC])
    sprod2h = sb("sprod2h", [P, NCC])
    bq2fh = sb("bq2fh", [P, NCC])
    scol = sb("scol", [P, NK])
    Ain = sb("Ain", [P, NK])
    mvt = sb("mvt", [P, 2 * NTC])
    rstd_t = sb("rstd_t", [P, NTC])
    rmax_t = sb("rmax_t", [P, NTC])
    gm_t = sb("gm_t", [P, NTC])
    gcol = sb("gcol", [P, 1])
    g1 = sb("g1", [1, 1])
    s_col = sb("s_col", [P, 1])
    invs_col = sb("invs_col", [P, 1])
    acol_t = sb("acol_t", [P, NTC])
    bcol_t = sb("bcol_t", [P, NTC])
    hmcols = sb("hmcols", [P, NFC])
    tmp_col = sb("tmp_col", [P, max(NFC, NK)])
    tmp_col2 = sb("tmp_col2", [P, max(NFC, NK, 12)])
    idtf = sb("idtf", [P, P])
    idn = sb("idn", [P, P], BF16)

    blk = ctx.enter_context(nc.Block())

    def barrier():
        marks = {}
        for name, section in (("sv", blk.vector), ("ss", blk.scalar),
                              ("sp", blk.gpsimd)):
            def mk(eng, name=name):
                cinc(eng.drain(), name)
            section(mk)
        for n in ENG_SEMS + DMA_STREAMS:
            marks[n] = counts[n]

        def wait_all(eng):
            for n, v in marks.items():
                W(eng, n, v)
        blk.vector(wait_all)
        blk.scalar(wait_all)
        blk.gpsimd(wait_all)
        blk.tensor(wait_all)
        blk.sync(wait_all)

    @blk.sync
    def _(sy):
        xr = x_in.rearrange("(k p) c -> p k c", p=P)
        g = max(1, NTC // 4)
        for i in range(0, NTC, g):
            j = min(NTC, i + g)
            dma(sy, "lx", X[:, i * C:j * C].rearrange("p (k c) -> p k c", c=C),
                xr[:, i:j, :])
        dma(sy, "lx", tcols[:, 0 * NCC:1 * NCC], tdec_in)
        dma(sy, "lx", tcols[:, 1 * NCC:2 * NCC], tfir_in)
        dma(sy, "lx", tcols[:, 2 * NCC:3 * NCC], tdecr_in)
        dma(sy, "lx", tcols[:, 3 * NCC:4 * NCC], tfirr_in)
        dma(sy, "lx", pfx_t[:, :], pfx_in)
        dma(sy, "lx", b1c[:, :], b1_in)
        dma(sy, "lx", b2c[:, :], b2_in)
    lx_p0 = counts["lx"]

    @blk.gpsimd
    def _(g):
        cinc(g.iota(idtf[:, :], [[1, P]], base=0, channel_multiplier=-1,
                    allow_small_or_imprecise_dtypes=True), "sp")
    sp_idn = counts["sp"]

    @blk.vector
    def _(v):
        v.memset(ones_row[:, :], 1.0)
        v.memset(magic_col[:, :], MAGIC)
        v.memset(nmagic_col[:, :], -MAGIC)
        v.memset(eps_col[:, :], 1e-5)
        v.memset(epsq_col[:, :], 1e-8)
        W(v, "sp", sp_idn)
        v.drain()
        v.tensor_scalar(idn[:, :], idtf[:, :], 0.0, None, ALU.is_equal)
        cinc(v.drain(), "sv")

    @blk.scalar
    def _(s):
        W(s, "lx", lx_p0)
        s.activation(wf_col[:, :], tcols[:, 0:NCC], AF.Exp)
        s.drain()
        s.activation(euf_col[:, :], tcols[:, NCC:2 * NCC], AF.Exp)
        s.drain()
        s.activation(wb_col[:, :], tcols[:, 2 * NCC:3 * NCC], AF.Exp)
        s.drain()
        cinc(s.activation(eub_col[:, :], tcols[:, 3 * NCC:4 * NCC], AF.Exp), "ss")
    ss_w0 = counts["ss"]

    @blk.vector
    def _(v):
        W(v, "ss", ss_w0)
        v.drain()
        v.tensor_scalar(wf_col[:, :], wf_col[:, :], -1.0, None, ALU.mult)
        v.drain()
        v.tensor_scalar(wb_col[:, :], wb_col[:, :], -1.0, None, ALU.mult)
        v.drain()
        v.tensor_copy(W32[:, 0 * NCC:1 * NCC], wf_col[:, :])
        v.drain()
        v.tensor_copy(W32[:, 1 * NCC:2 * NCC], wf_col[:, :])
        v.drain()
        v.tensor_copy(W32[:, 2 * NCC:3 * NCC], wb_col[:, :])
        v.drain()
        cinc(v.tensor_copy(W32[:, 3 * NCC:4 * NCC], wb_col[:, :]), "sv")
    sv_w = counts["sv"]

    @blk.scalar
    def _(s):
        W(s, "sv", sv_w)
        s.drain()
        s.activation(ewf_col[:, :], wf_col[:, :], AF.Exp)
        s.drain()
        cinc(s.activation(ewb_col[:, :], wb_col[:, :], AF.Exp), "ss")


    lq_prep_box = [(0, 0)]
    prep_tail_box = [None]
    lq2_prep_box = [(0, 0)]
    sc_lg_box = [0]
    sc_sv_box = [0]
    ccW1_box = [None]
    ccW2_box = [None]

    def emit_prep(mid_cb=None):
        N1T = max(1, SH1 // P)
        with ExitStack() as pctx:
            nb = [pctx.enter_context(nc.sbuf_tensor(f"nb{i}", [P, C], F32))
                  for i in range(8)]
            s2t = pctx.enter_context(nc.sbuf_tensor("s2t", [P, F // C], F32))
            wTb = [pctx.enter_context(nc.sbuf_tensor(f"wTb{i}", [P, SH1], F32))
                   for i in range(2)]
            qob = [pctx.enter_context(nc.sbuf_tensor(f"qob{i}", [P, SH1], BF16))
                   for i in range(2)]
            bct1 = pctx.enter_context(nc.sbuf_tensor("bct1", [P, SH1], F32))
            bct2 = pctx.enter_context(nc.sbuf_tensor("bct2", [P, SH2], F32))
            s1c = pctx.enter_context(nc.sbuf_tensor("s1c", [P, N1T], F32))
            inv1c = pctx.enter_context(nc.sbuf_tensor("inv1c", [P, N1T], F32))
            s2c = pctx.enter_context(nc.sbuf_tensor("s2c", [P, 1], F32))
            inv2c = pctx.enter_context(nc.sbuf_tensor("inv2c", [P, 1], F32))
            idt = pctx.enter_context(nc.sbuf_tensor("idt", [P, P], F32))
            invr = pctx.enter_context(nc.sbuf_tensor("invr", [1, SH1], F32))
            scg = pctx.enter_context(
                nc.sbuf_tensor("scg", [P, NCORES * (N1T + 1)], F32))

            @blk.gpsimd
            def _(g):
                cinc(g.iota(idt[:, :], [[1, P]], base=0, channel_multiplier=-1,
                            allow_small_or_imprecise_dtypes=True), "sp")
            sp_id = counts["sp"]

            @blk.vector
            def _(v):
                W(v, "sp", sp_id)
                v.drain()
                cinc(v.tensor_scalar(idt[:, :], idt[:, :], 0.0, None,
                                     ALU.is_equal), "sv")
            sv_id = counts["sv"]

            lwp = ["lw", "lw2", "lw3", "lw4"]
            NW2 = F // C
            lw_ab = []

            @blk.sync
            def _(sy):
                for i in range(N1T):
                    dma(sy, lwp[i % 4], nb[i % 4][:, :],
                        w1s_in[i * P:(i + 1) * P, :], serial=False)
                    lw_ab.append(counts[lwp[i % 4]])

            @blk.vector
            def _(v):
                for i in range(N1T):
                    W(v, lwp[i % 4], lw_ab[i])
                    v.drain()
                    v.tensor_reduce(s1c[:, i:i + 1], nb[i % 4][:, :],
                                    AX.X, ALU.max, apply_absolute_value=True)
                v.drain()
                v.tensor_scalar(s1c[:, :], s1c[:, :], 1e-8, 1.0 / 127.0,
                                ALU.max, ALU.mult)
                v.drain()
                cinc(v.reciprocal(inv1c[:, :], s1c[:, :]), "sv")
            sv_ab1 = counts["sv"]
            lw2_ab = []

            @blk.sync
            def _(sy):
                for j in range(NW2):
                    dma(sy, lwp[j % 4], nb[4 + j % 4][0:SH2, :],
                        w2s_in[:, j * C:(j + 1) * C], serial=False)
                    lw2_ab.append(counts[lwp[j % 4]])

            @blk.vector
            def _(v):
                for j in range(NW2):
                    W(v, lwp[j % 4], lw2_ab[j])
                    v.drain()
                    v.tensor_reduce(s2t[0:SH2, j:j + 1], nb[4 + j % 4][0:SH2, :],
                                    AX.X, ALU.max, apply_absolute_value=True)
                v.drain()
                v.tensor_reduce(s2c[0:SH2, 0:1], s2t[0:SH2, 0:NW2], AX.X,
                                ALU.max)
                v.drain()
                v.tensor_scalar(s2c[0:SH2, 0:1], s2c[0:SH2, 0:1], 1e-8,
                                1.0 / 127.0, ALU.max, ALU.mult)
                v.drain()
                cinc(v.reciprocal(inv2c[0:SH2, 0:1], s2c[0:SH2, 0:1]), "sv")
            sv_ab2 = counts["sv"]

            HW = SH1
            p_row = pctx.enter_context(nc.psum_tensor("p_row", [1, HW], F32))
            p_bct = pctx.enter_context(nc.psum_tensor("p_bct", [P, 512], F32))
            sct = pctx.enter_context(nc.sbuf_tensor("sct", [P, N1T + 1], F32))

            @blk.vector
            def _(v):
                W(v, "sv", max(sv_ab1, sv_ab2))
                v.drain()
                if SH2 < P:
                    v.memset(sct[:, :], 0.0)
                    v.drain()
                v.tensor_copy(sct[:, 0:N1T], s1c[:, :])
                v.drain()
                cinc(v.tensor_copy(sct[0:SH2, N1T:N1T + 1], s2c[0:SH2, 0:1]),
                     "sv")
            sv_sr = counts["sv"]

            ccS = CC_SEMS[cc_used[0]]
            cc_used[0] += 1

            @blk.gpsimd
            def _(g):
                W(g, "sv", sv_sr)
                dma(g, "lg", sc_i, sct[:, :])
                g.wait_ge(sems["lg"], counts["lg"])
                counts[ccS] += 1
                g.collective_compute(
                    "AllGather", ALU.bypass, replica_groups=RG,
                    ins=[sc_i], outs=[sc_o]).then_inc(sems[ccS])
                g.wait_ge(sems[ccS], counts[ccS])

            scpool = ["lx", "lk", "lxo", "tw2", "lh", "lh2"]
            sc_tot = {}

            @blk.sync
            def _(sy):
                W(sy, ccS, 1)
                for m in range(NCORES):
                    st_ = scpool[m % len(scpool)]
                    dma(sy, st_, scg[:, m * (N1T + 1):(m + 1) * (N1T + 1)],
                        sc_o[m * P:(m + 1) * P, :], serial=False)
            for st_ in scpool:
                sc_tot[st_] = counts[st_]

            @blk.vector
            def _(v):
                for st_ in scpool:
                    W(v, st_, sc_tot[st_])
                v.drain()
                for m in range(NCORES):
                    v.tensor_copy(swcol1[:, m * N1T:(m + 1) * N1T],
                                  scg[:, m * (N1T + 1):m * (N1T + 1) + N1T])
                    v.drain()
                    if SH2 >= P:
                        v.tensor_copy(swcol2[:, m:m + 1],
                                      scg[:, m * (N1T + 1) + N1T:
                                          (m + 1) * (N1T + 1)])
                        v.drain()
                cinc(v.nop(), "sv")
            sc_sv_box[0] = counts["sv"]

            if SH2 < P:
                @blk.gpsimd
                def _(g):
                    W(g, "sv", sc_sv_box[0])
                    for m in range(NCORES):
                        cc0_ = (m * SH2) // P
                        pa = (m * SH2) % P
                        dma(g, "lg", swcol2[pa:pa + SH2, cc0_:cc0_ + 1],
                            scg[0:SH2, m * (N1T + 1) + N1T:
                                (m + 1) * (N1T + 1)])
            sc_lg_box[0] = counts["lg"]

            if mid_cb is not None:
                mid_cb()

            def rows_and_bcast(invc, ncol, wid, bct, sv_gate):
                @blk.tensor
                def _(t):
                    W(t, "sv", max(sv_gate, sv_id))
                    last = None
                    for i in range(ncol):
                        kk = P if wid // ncol == P else wid // ncol
                        last = t.matmul(p_row[0:1, i * kk:(i + 1) * kk],
                                        invc[0:kk, i:i + 1], idt[0:kk, 0:kk],
                                        start=True, stop=True)
                    cinc(last, "st")
                st_r = counts["st"]

                @blk.vector
                def _(v):
                    W(v, "st", st_r)
                    v.drain()
                    cinc(v.tensor_copy(invr[0:1, 0:wid], p_row[0:1, 0:wid]),
                         "sv")
                sv_r = counts["sv"]

                ws = min(512, wid)
                for sl in range(wid // ws):
                    @blk.tensor
                    def _(t, sl=sl, ws=ws):
                        W(t, "sv", counts["sv"])
                        cinc(t.matmul(p_bct[:, 0:ws], ones_row[:, :],
                                      invr[0:1, sl * ws:(sl + 1) * ws],
                                      start=True, stop=True), "st")
                    st_b = counts["st"]

                    @blk.vector
                    def _(v, sl=sl, ws=ws, bct=bct):
                        W(v, "st", st_b)
                        v.drain()
                        cinc(v.tensor_copy(bct[:, sl * ws:(sl + 1) * ws],
                                           p_bct[:, 0:ws]), "sv")
                return counts["sv"]

            def quantT(nch, width, srcT, dst, bct, qs, qs2, lsa, lsb, sv_bct,
                       dsec=None):
                quant_done = [0, 0]
                spill_done = [0, 0]
                lq_t0 = {n: counts[n] for n in ("lq", "lq2", "lq3", "lq4")}
                if dsec is None:
                    dsec = blk.sync

                def one(ci):
                    b = ci % 2
                    ls = lsa if b == 0 else lsb
                    qsb = qs if b == 0 else qs2

                    @dsec
                    def _(sy, ci=ci, b=b, ls=ls):
                        W(sy, "sv", quant_done[b])
                        dma(sy, ls, wTb[b][:, 0:width],
                            srcT[ci * P:(ci + 1) * P, :], serial=False)
                    lw_m = counts[ls]

                    @blk.vector
                    def _(v, ci=ci, b=b, ls=ls, lw_m=lw_m, sv_bct=sv_bct,
                          qsb=qsb, width=width, bct=bct):
                        W(v, ls, lw_m)
                        W(v, "sv", sv_bct)
                        for n in ("lq", "lq2", "lq3", "lq4"):
                            W(v, n, lq_t0[n])
                        W(v, qsb, spill_done[b])
                        v.drain()
                        v.tensor_mul(wTb[b][:, 0:width], wTb[b][:, 0:width],
                                     bct[:, 0:width])
                        v.drain()
                        v.tensor_scalar(wTb[b][:, 0:width], wTb[b][:, 0:width],
                                        MAGIC, MAGIC, ALU.add, ALU.subtract)
                        v.drain()
                        cinc(v.tensor_scalar(qob[b][:, 0:width],
                                             wTb[b][:, 0:width],
                                             127.0, -128.0, ALU.min, ALU.max),
                             "sv")
                    quant_done[b] = counts["sv"]
                    sv_qq = counts["sv"]

                    @dsec
                    def _(sy, ci=ci, b=b, sv_qq=sv_qq, qsb=qsb, width=width):
                        W(sy, "sv", sv_qq)
                        dma(sy, qsb, dst[ci * P:(ci + 1) * P, :],
                            qob[b][:, 0:width], serial=False)
                    spill_done[b] = counts[qsb]
                return [lambda ci=ci: one(ci) for ci in range(nch)]

            sv_b1c = rows_and_bcast(inv1c, N1T, SH1, bct1,
                                    max(sv_ab1, sv_sr))
            for f_ in quantT(NCC, SH1, w1Ts_in, w1q_i, bct1, "lq", "lq3",
                             "ll", "ll2", sv_b1c):
                f_()
            lq_prep_box[0] = (counts["lq"], counts["lq3"])

            ccW1 = CC_SEMS[cc_used[0]]
            cc_used[0] += 1
            ccW1_box[0] = ccW1

            @blk.gpsimd
            def _(g):
                W(g, "lq", counts["lq"])
                W(g, "lq3", counts["lq3"])
                counts[ccW1] += 1
                g.collective_compute(
                    "AllGather", ALU.bypass, replica_groups=RG,
                    ins=[w1q_i], outs=[w1q_o]).then_inc(sems[ccW1])

            sv_b2c = rows_and_bcast(inv2c, 1, SH2, bct2, sv_ab2)
            w2_pieces = quantT(NFC, SH2, w2Ts_in, w2q_i, bct2, "lq2", "lq4",
                               "lf", "lf2", sv_b2c, dsec=blk.gpsimd)

            ccW2 = CC_SEMS[cc_used[0]]
            cc_used[0] += 1
            ccW2_box[0] = ccW2

            def emit_w2_ag():
                @blk.gpsimd
                def _(g):
                    W(g, "lq2", counts["lq2"])
                    W(g, "lq4", counts["lq4"])
                    counts[ccW2] += 1
                    g.collective_compute(
                        "AllGather", ALU.bypass, replica_groups=RG,
                        ins=[w2q_i], outs=[w2q_o]).then_inc(sems[ccW2])
            prep_tail_box[0] = (w2_pieces, emit_w2_ag)

    def ln_stats(want_max, xo=None, stats_done=None):
        nh = max(1, C // 512)
        step = C // nh
        if stats_done is not None:
            pass
        elif xo is None:
            @blk.vector
            def _(v):
                W(v, "lx", lx_p0)
                v.drain()
                for k in range(NTC):
                    src = X[:, k * C:(k + 1) * C]
                    for h in range(nh):
                        v.bn_stats(tmp_col2[:, 6 * h:6 * h + 6],
                                   src[:, h * step:(h + 1) * step])
                        v.drain()
                    v.bn_aggr(mvt[:, 2 * k:2 * k + 2], tmp_col2[:, 0:6 * nh])
                    v.drain()
                    if want_max:
                        v.tensor_reduce(rmax_t[:, k:k + 1], src, AX.X, ALU.max)
                        v.drain()
                cinc(v.nop(), "sv")
        else:
            XO0, XO1, lxo_tot = xo
            with ExitStack() as xctx:
                xot = [[xctx.enter_context(
                    nc.sbuf_tensor(f"xot{j}_{i}", [P, C], F32))
                    for i in range(2)] for j in range(2)]
                add_sv = [0, 0]
                tile_done = [0, 0]
                for k in range(NTC):
                    pb = k % 2

                    @blk.sync
                    def _(sy, k=k, pb=pb):
                        W(sy, "sv", add_sv[pb])
                        if counts["lxo"] > 0:
                            sy.wait_ge(sems["lxo"], counts["lxo"])
                        dma(sy, "lxo", xot[0][pb][:, :],
                            XO0[k * P:(k + 1) * P, :], serial=False)
                        dma(sy, "lxo", xot[1][pb][:, :],
                            XO1[k * P:(k + 1) * P, :], serial=False)
                    tile_done[pb] = counts["lxo"]

                    @blk.vector
                    def _(v, k=k, pb=pb):
                        W(v, "lxo", tile_done[pb])
                        v.drain()
                        src = X[:, k * C:(k + 1) * C]
                        v.tensor_add(src, src, xot[0][pb][:, :])
                        v.drain()
                        add_sv[pb] = cinc(
                            v.tensor_add(src, src, xot[1][pb][:, :]), "sv")
                        for h in range(nh):
                            v.bn_stats(tmp_col2[:, 6 * h:6 * h + 6],
                                       src[:, h * step:(h + 1) * step])
                            v.drain()
                        v.bn_aggr(mvt[:, 2 * k:2 * k + 2], tmp_col2[:, 0:6 * nh])
                        v.drain()
                        if want_max:
                            v.tensor_reduce(rmax_t[:, k:k + 1], src, AX.X, ALU.max)
                            v.drain()
                        cinc(v.nop(), "sv")
        sv_stats = counts["sv"] if stats_done is None else stats_done

        @blk.scalar
        def _(s):
            W(s, "sv", sv_stats)
            s.drain()
            cinc(s.activation(rmax_t[:, :] if not want_max else gm_t[:, :],
                              mvt[:, 1::2], AF.Sqrt, bias=eps_col[:, :]), "ss")
        ss_sd = counts["ss"]

        @blk.vector
        def _(v):
            W(v, "ss", ss_sd)
            v.drain()
            v.reciprocal(rstd_t[:, :], gm_t[:, :] if want_max else rmax_t[:, :])
            v.drain()
            if want_max:
                v.tensor_sub(gm_t[:, :], rmax_t[:, :], mvt[:, 0::2])
                v.drain()
                v.tensor_mul(gm_t[:, :], gm_t[:, :], rstd_t[:, :])
                v.drain()
                v.tensor_reduce(gcol[:, :], gm_t[:, :], AX.X, ALU.max)
                v.drain()
            cinc(v.nop(), "sv")
        return counts["sv"]

    def gmax_collective(sv_ready):
        ccn = CC_SEMS[cc_used[0]]
        cc_used[0] += 1

        @blk.gpsimd
        def _(g):
            W(g, "sv", sv_ready)
            dma(g, "lg", gmax_i, gcol[:, :])
            g.wait_ge(sems["lg"], counts["lg"])
            counts[ccn] += 1
            g.collective_compute(
                "AllGather", ALU.bypass, replica_groups=RG,
                ins=[gmax_i], outs=[gmax_o]).then_inc(sems[ccn])
            g.wait_ge(sems[ccn], counts[ccn])
            dma(g, "lg", grow[:, :], gmax_o.rearrange("(a b) c -> b (a c)", b=1))
        lg_row = counts["lg"]

        @blk.vector
        def _(v):
            W(v, "lg", lg_row)
            v.drain()
            cinc(v.reduce_max(g1[:, :], grow[:, :], AX.X), "sv")
        sv_g1 = counts["sv"]

        with nc.psum_tensor(f"p_bc{cc_used[0]}", [P, 1], F32) as p_bc:
            @blk.tensor
            def _(t):
                W(t, "sv", sv_g1)
                cinc(t.matmul(p_bc[:, :], ones_row[:, :], g1[:, :],
                              start=True, stop=True), "st")
            st_bc = counts["st"]

            @blk.vector
            def _(v):
                W(v, "st", st_bc)
                v.drain()
                v.tensor_scalar(s_col[:, :], p_bc[:, :], epsq_col[:, :], 1.0 / 255.0,
                                ALU.max, ALU.mult)
                v.drain()
                cinc(v.reciprocal(invs_col[:, :], s_col[:, :]), "sv")
        return counts["sv"]

    def bias_quant(sv_scales, sw, bc, n, sp_out, bq_out, halve):
        @blk.vector
        def _(v):
            W(v, "sv", max(sv_scales, sc_sv_box[0]))
            W(v, "lg", sc_lg_box[0])
            v.drain()
            v.tensor_scalar(sp_out[:, 0:n], sw[:, 0:n], s_col[:, :], None, ALU.mult)
            v.drain()
            v.reciprocal(tmp_col[:, 0:n], sp_out[:, 0:n])
            v.drain()
            v.tensor_mul(tmp_col2[:, 0:n], bc[:, 0:n], tmp_col[:, 0:n])
            v.drain()
            v.tensor_scalar(tmp_col2[:, 0:n], tmp_col2[:, 0:n], MAGIC, MAGIC,
                            ALU.add, ALU.subtract)
            v.drain()
            v.tensor_mul(bq_out[:, 0:n], tmp_col2[:, 0:n], sp_out[:, 0:n])
            v.drain()
            if halve:
                v.tensor_scalar(bq_out[:, 0:n], bq_out[:, 0:n], 0.5, None, ALU.mult)
                v.drain()
                v.tensor_scalar(sp_out[:, 0:n], sp_out[:, 0:n], 0.5, None, ALU.mult)
                v.drain()
            cinc(v.nop(), "sv")
        return counts["sv"]

    def ffn_app(app):
        xo = ((XO0_d, XO1_d, xo_info["lxo"])
              if app == 1 and xo_info["lxo"] > 0 else None)
        sfx = f"_a{app}"
        ph_st, ph_sv, ph_ss, ph_sp = (counts["st"], counts["sv"],
                                      counts["ss"], counts["sp"])
        sv0 = ln_stats(want_max=True, xo=xo,
                       stats_done=stats_box[0] if app == 1 else None)
        sv1 = gmax_collective(sv0)

        fctx = ExitStack()
        qt = [fctx.enter_context(nc.sbuf_tensor(f"qt_f{i}{sfx}", [P, C], F32))
              for i in range(2)]
        qbf = [fctx.enter_context(nc.sbuf_tensor(f"qbf_f{i}{sfx}", [P, C], BF16))
               for i in range(2)]
        xqa = fctx.enter_context(
            nc.sbuf_tensor(f"xqa{sfx}", [P, NCC * TPC], BF16))
        xq_box = [0, 0]

        def emit_xq():
            @blk.vector
            def _(v):
                W(v, "sv", sv1)
                v.drain()
                v.tensor_scalar(acol_t[:, :], rstd_t[:, :], invs_col[:, :],
                                None, ALU.mult)
                v.drain()
                v.tensor_mul(bcol_t[:, :], mvt[:, 0::2], acol_t[:, :])
                v.drain()
                cinc(v.tensor_scalar(bcol_t[:, :], bcol_t[:, :], -1.0, MAGIC,
                                     ALU.mult, ALU.add), "sv")
            sv_a = counts["sv"]

            tp_st = [0, 0]
            ev_ss = [0, 0]
            mn_sv = [0, 0]
            rl_ss = [0, 0]
            with nc.psum_tensor(f"ptx{sfx}", [P, 2 * C], BF16) as ptx:
                for k in range(NTC):
                    b = k % 2

                    @blk.scalar
                    def _(s, k=k, b=b):
                        W(s, "sv", max(sv_a, mn_sv[b]))
                        W(s, "ss", ph_ss)
                        s.drain()
                        s.activation(qt[b][:, :], X[:, k * C:(k + 1) * C],
                                     AF.Relu, bias=bcol_t[:, k:k + 1],
                                     scale=acol_t[:, k:k + 1])
                        s.drain()
                        rl_ss[b] = cinc(s.activation(qt[b][:, :], qt[b][:, :],
                                                     AF.Relu,
                                                     bias=nmagic_col[:, :]),
                                        "ss")

                    @blk.vector
                    def _(v, k=k, b=b):
                        W(v, "ss", rl_ss[b])
                        W(v, "st", max(ph_st, tp_st[b]))
                        W(v, "sp", ph_sp)
                        v.drain()
                        mn_sv[b] = cinc(v.tensor_scalar(qbf[b][:, :],
                                                        qt[b][:, :], 255.0,
                                                        None, ALU.min), "sv")

                    @blk.tensor
                    def _(t, k=k, b=b):
                        W(t, "sv", max(mn_sv[b], ev_ss[b]))
                        last = None
                        for c in range(NCC):
                            last = t.transpose(
                                ptx[:, b * C + c * P:b * C + (c + 1) * P],
                                qbf[b][:, c * P:(c + 1) * P], idn[:, :])
                        tp_st[b] = cinc(last, "st")

                    @blk.vector
                    def _(v, k=k, b=b):
                        W(v, "st", tp_st[b])
                        v.drain()
                        ev_ss[b] = cinc(v.tensor_copy(
                            xqa[:, :].rearrange("p (c t) -> p c t", t=TPC)
                                     [:, :, k * P:(k + 1) * P],
                            ptx[:, b * C:(b + 1) * C]
                                .rearrange("p (c w) -> p c w", w=P)), "sv")
            xq_box[0] = counts["sv"]
            xq_box[1] = bias_quant(sv1, swcol1, b1c, NFC, sprod1, bq1f,
                                   halve=False)

        if app == 0:
            emit_prep(mid_cb=emit_xq)
        else:
            emit_xq()
        xq_done = xq_box[0]
        sv_b1 = xq_box[1]

        if True:
            w2slab = [fctx.enter_context(nc.sbuf_tensor(f"w2slab{i}{sfx}", [P, C], BF16))
                      for i in range(NFC)]

            tw2_hold = [0]

            with ExitStack() as m1ctx:
                SLAB = min(256, F)
                NSLAB = SLAB // P
                w1slab2 = [m1ctx.enter_context(
                    nc.sbuf_tensor(f"w1slab{j}{sfx}", [P, NCC * SLAB], BF16))
                    for j in range(2)]
                wv1 = [w1slab2[j][:, :].rearrange("p (c w) -> p c w", w=SLAB)
                       for j in range(2)]
                hbf = [m1ctx.enter_context(nc.sbuf_tensor(f"hbf{i}{sfx}", [P, TPC], BF16))
                       for i in range(2)]

                @blk.sync
                def _(sy):
                    W(sy, ccW2_box[0], 1)
                    W(sy, "st", ph_st)
                    W(sy, "sv", ph_sv)
                    W(sy, "ss", ph_ss)
                    W(sy, "sp", ph_sp)
                    if counts["tw2"] > 0:
                        sy.wait_ge(sems["tw2"], counts["tw2"])
                    for fc in range(NFC):
                        for m in range(NCORES):
                            dma(sy, "tw2",
                                w2slab[fc][:, m * SH2:(m + 1) * SH2],
                                w2q_o[m * F + fc * P:m * F + (fc + 1) * P, :],
                                serial=False)
                tw2_hold[0] = counts["tw2"]

                with ExitStack() as mctx:
                    psumH = [mctx.enter_context(
                        nc.psum_tensor(f"psumH{i}{sfx}", [P, TPC], F32))
                        for i in range(2)]
                    tw_slab = [0, 0]
                    st_slab_done = [0, 0]
                    evac_ss = [0, 0]
                    st_mm = [0, 0]
                    hspill = [("lh", 0), ("lh2", 0)]

                    NSLABS = NFC // NSLAB

                    def slab_load(sg):
                        sbf = sg % 2

                        @blk.scalar
                        def _(s, sg=sg, sbf=sbf):
                            W(s, "st", st_slab_done[sbf])
                            W(s, ccW1_box[0], 1)
                            if sg == 0:
                                W(s, "sv", ph_sv)
                                W(s, "sp", ph_sp)
                            if counts["tw"] > 0:
                                s.wait_ge(sems["tw"], counts["tw"])
                            fa = sg * SLAB
                            fb = fa + SLAB
                            for c in range(NCC):
                                for m in range(fa // SH1, (fb - 1) // SH1 + 1):
                                    la = max(fa, m * SH1) - m * SH1
                                    lb = min(fb, (m + 1) * SH1) - m * SH1
                                    off = m * SH1 + la - fa
                                    dma(s, "tw",
                                        w1slab2[sbf][:, c * SLAB + off:
                                                     c * SLAB + off + lb - la],
                                        w1q_o[m * C + c * P:m * C + (c + 1) * P,
                                              la:lb],
                                        serial=False)
                        tw_slab[sbf] = counts["tw"]

                    slab_load(0)
                    for fc in range(NFC):
                        sg = fc // NSLAB
                        sbf = sg % 2
                        if fc % NSLAB == 0 and sg + 1 < NSLABS:
                            slab_load(sg + 1)

                        b = fc % 2
                        hb = fc % 2

                        @blk.tensor
                        def _(t, fc=fc, b=b, sbf=sbf):
                            W(t, "tw", tw_slab[sbf])
                            W(t, "sv", max(xq_done, sv_b1))
                            W(t, "ss", evac_ss[b])
                            off = (fc % NSLAB) * P
                            last = None
                            for c in range(NCC):
                                for tt in range(NTT):
                                    last = t.matmul(
                                        psumH[b][:, tt * TT:(tt + 1) * TT],
                                        w1slab2[sbf][:, c * SLAB + off:
                                                     c * SLAB + off + P],
                                        xqa[:, c * TPC + tt * TT:
                                            c * TPC + (tt + 1) * TT],
                                        start=(c == 0), stop=(c == NCC - 1))
                            st_mm[b] = cinc(last, "st")
                        if fc % NSLAB == NSLAB - 1:
                            st_slab_done[sbf] = counts["st"]

                        @blk.scalar
                        def _(s, fc=fc, b=b, hb=hb):
                            W(s, "st", st_mm[b])
                            W(s, hspill[hb][0], hspill[hb][1])
                            s.drain()
                            evac_ss[b] = cinc(
                                s.activation(hbf[hb][:, :], psumH[b][:, :], AF.Relu,
                                             bias=bq1f[:, fc:fc + 1],
                                             scale=sprod1[:, fc:fc + 1]), "ss")

                        @blk.vector
                        def _(v, fc=fc, b=b, hb=hb):
                            W(v, "ss", evac_ss[b])
                            v.drain()
                            cinc(v.tensor_reduce(hmcols[:, fc:fc + 1], hbf[hb][:, :],
                                                 AX.X, ALU.max), "sv")
                        sv_hm = counts["sv"]

                        hs = "lh" if fc % 2 == 0 else "lh2"

                        @blk.gpsimd
                        def _(g, fc=fc, hb=hb, hs=hs):
                            W(g, "sv", sv_hm)
                            dma(g, hs, H_d[fc * P:(fc + 1) * P, :], hbf[hb][:, :])
                        hspill[hb] = (hs, counts[hs])
                        if prep_tail_box[0] is not None and fc >= 2:
                            pieces, agf = prep_tail_box[0]
                            for _ in range(3):
                                if pieces:
                                    pieces.pop(0)()
                            if not pieces:
                                agf()
                                prep_tail_box[0] = None

                    @blk.vector
                    def _(v):
                        v.drain()
                        cinc(v.tensor_reduce(gcol[:, :], hmcols[:, :], AX.X, ALU.max),
                             "sv")
                sv_hmax = counts["sv"]
                m1_st = counts["st"]
                m1_ss = counts["ss"]
                m1_lh = counts["lh"]
                m1_lh2 = counts["lh2"]

            sv2 = gmax_collective(sv_hmax)
            sv_b2 = bias_quant(sv2, swcol2, b2c, NCC, sprod2h, bq2fh, halve=True)

            with ExitStack() as m2ctx:
                hin = [xqa]
                hq32 = m2ctx.enter_context(
                    nc.sbuf_tensor(f"hq32{sfx}", [P, 2 * TT], F32))
                f1bf = [m2ctx.enter_context(
                    nc.sbuf_tensor(f"f1bf{i}{sfx}", [P, TT], BF16))
                    for i in range(NCC)]
                f1tm = [m2ctx.enter_context(
                    nc.sbuf_tensor(f"f1tm{i}{sfx}", [P, C], BF16)) for i in range(2)]

                with ExitStack() as mctx:
                    psumC = [mctx.enter_context(
                        nc.psum_tensor(f"psumC{i}{sfx}", [P, TT], F32))
                        for i in range(NCC)]
                    NHALF = min(2, NFC)
                    NH = NFC // NHALF
                    half_free_st = [[0, 0]] + [[0, 0] for _ in range(NTT)]
                    evac_sv2 = [0] * NCC
                    spill_done = [("lf" if c % 2 == 0 else "lf2", 0)
                                  for c in range(NCC)]
                    lf_tile = []
                    hb = 0
                    st_half = [[0, 0] for _ in range(NTT)]
                    sv_hq = [[0, 0] for _ in range(NTT)]

                    def emit_load(tt, half):
                        f0, f1_ = half * NH, (half + 1) * NH
                        th_prev = counts["th"]

                        @blk.scalar
                        def _(s, tt=tt, half=half, f0=f0, f1_=f1_,
                              th_prev=th_prev):
                            W(s, "st", max(half_free_st[tt][half], m1_st))
                            W(s, "ss", m1_ss)
                            W(s, "lh", m1_lh)
                            W(s, "lh2", m1_lh2)
                            W(s, "th", th_prev)
                            nsp = max(1, NH // 8)
                            step = NH // nsp
                            for i in range(nsp):
                                a = f0 + i * step
                                b2_ = f0 + (i + 1) * step
                                dma(s, "th",
                                    hin[hb][:, a * TT:b2_ * TT]
                                        .rearrange("p (f t) -> p f t", t=TT),
                                    H_d[a * P:b2_ * P, tt * TT:(tt + 1) * TT]
                                        .rearrange("(f p) t -> p f t", p=P),
                                    serial=False)
                        return counts["th"]

                    def emit_quant(tt, half, th_half):
                        f0, f1_ = half * NH, (half + 1) * NH
                        GQ = 2
                        for fg in range(f0, f1_, GQ):
                            gn = min(GQ, f1_ - fg)

                            @blk.vector
                            def _(v, fg=fg, gn=gn, th_half=th_half):
                                W(v, "th", th_half)
                                W(v, "sv", sv_b2)
                                W(v, "lh", m1_lh)
                                W(v, "lh2", m1_lh2)
                                W(v, "st", m1_st)
                                v.drain()
                                for i in range(gn):
                                    fc = fg + i
                                    v.tensor_scalar(
                                        hq32[:, i * TT:(i + 1) * TT],
                                        hin[hb][:, fc * TT:(fc + 1) * TT],
                                        invs_col[:, :], MAGIC, ALU.mult, ALU.add)
                                v.drain()
                                for i in range(gn):
                                    fc = fg + i
                                    cinc(v.tensor_scalar(
                                        hin[hb][:, fc * TT:(fc + 1) * TT],
                                        hq32[:, i * TT:(i + 1) * TT], MAGIC,
                                        255.0, ALU.subtract, ALU.min), "sv")
                        sv_hq[tt][half] = counts["sv"]

                    def emit_mm(tt, half):
                        f0, f1_ = half * NH, (half + 1) * NH
                        for c in range(NCC):
                            @blk.tensor
                            def _(t, tt=tt, c=c, half=half, f0=f0, f1_=f1_):
                                W(t, "tw2", tw2_hold[0])
                                W(t, "sv", max(sv_hq[tt][half], evac_sv2[c]))
                                last = None
                                for fc in range(f0, f1_):
                                    last = t.matmul(
                                        psumC[c][:, :],
                                        w2slab[fc][:, c * P:(c + 1) * P],
                                        hin[hb][:, fc * TT:(fc + 1) * TT],
                                        start=(fc == 0), stop=(fc == NFC - 1))
                                cinc(last, "st")
                        st_half[tt][half] = counts["st"]
                        if tt + 1 <= NTT - 1:
                            half_free_st[tt + 1][half] = counts["st"]

                    def emit_evacs(tt):
                        for c in range(NCC):
                            @blk.vector
                            def _(v, c=c, tt=tt):
                                W(v, "st", st_half[tt][NHALF - 1])
                                W(v, spill_done[c][0], spill_done[c][1])
                                v.drain()
                                evac_sv2[c] = cinc(
                                    v.tensor_scalar(f1bf[c][:, :], psumC[c][:, :],
                                                    sprod2h[:, c:c + 1],
                                                    bq2fh[:, c:c + 1],
                                                    ALU.mult, ALU.add), "sv")

                            fs = "lf" if c % 2 == 0 else "lf2"

                            @blk.sync
                            def _(sy, tt=tt, c=c, fs=fs):
                                W(sy, "sv", evac_sv2[c])
                                dma(sy, fs,
                                    F1_d[c * P:(c + 1) * P, tt * TT:(tt + 1) * TT],
                                    f1bf[c][:, :])
                            spill_done[c] = (fs, counts[fs])
                        lf_tile.append((counts["lf"], counts["lf2"]))

                    for tt in range(NTT):
                        th0 = emit_load(tt, 0)
                        emit_quant(tt, 0, th0)
                        if tt > 0:
                            emit_evacs(tt - 1)
                        emit_mm(tt, 0)
                        if NHALF > 1:
                            th1 = emit_load(tt, 1)
                            emit_quant(tt, 1, th1)
                            emit_mm(tt, 1)
                    emit_evacs(NTT - 1)

                add_done = [0, 0]
                for k in range(NTC):
                    b = k % 2
                    lfv, lf2v = lf_tile[min(k // (NTC // NTT), NTT - 1)]

                    @blk.scalar
                    def _(s, k=k, b=b, lfv=lfv, lf2v=lf2v):
                        W(s, "lf", lfv)
                        W(s, "lf2", lf2v)
                        W(s, "sv", add_done[b])
                        dma(s, "tf", f1tm[b][:, :], F1_d[:, k * P:(k + 1) * P],
                            transpose=True)
                    tf_k = counts["tf"]

                    @blk.vector
                    def _(v, k=k, b=b):
                        W(v, "tf", tf_k)
                        v.drain()
                        add_done[b] = cinc(
                            v.tensor_add(X[:, k * C:(k + 1) * C],
                                         X[:, k * C:(k + 1) * C],
                                         f1tm[b][:, :]), "sv")
                    if app == 1:
                        @blk.sync
                        def _(sy, k=k, val=add_done[b]):
                            W(sy, "sv", val)
                            dma(sy, "lxo",
                                out_d.rearrange("(j p) c -> p j c", p=P)
                                     [:, k:k + 1, :],
                                X[:, k * C:(k + 1) * C]
                                    .rearrange("p (j c) -> p j c", c=C))
        fctx.close()
        barrier()
        return counts["sv"]

    def attn():
        sv0 = ln_stats(want_max=False)

        with ExitStack() as mctx:
            ln2bf = [mctx.enter_context(nc.sbuf_tensor(f"ln2bf{i}", [P, C], BF16))
                     for i in range(2)]
            lnfa = mctx.enter_context(
                nc.sbuf_tensor("lnfa", [P, NCC * TPC], BF16))

            @blk.vector
            def _(v):
                W(v, "sv", sv0)
                v.drain()
                v.tensor_mul(bcol_t[:, :], mvt[:, 0::2], rstd_t[:, :])
                v.drain()
                cinc(v.tensor_scalar(bcol_t[:, :], bcol_t[:, :], -1.0, None,
                                     ALU.mult), "sv")
            sv_lb = counts["sv"]

            tp6 = [0, 0]
            ev6 = [0, 0]
            with nc.psum_tensor("ptl", [P, 2 * C], BF16) as ptl:
                for k in range(NTC):
                    b = k % 2

                    @blk.scalar
                    def _(s, k=k, b=b):
                        W(s, "sv", sv_lb)
                        W(s, "st", tp6[b])
                        s.drain()
                        cinc(s.activation(ln2bf[b][:, :], X[:, k * C:(k + 1) * C],
                                          AF.Identity, bias=bcol_t[:, k:k + 1],
                                          scale=rstd_t[:, k:k + 1]), "ss")
                    ss_l = counts["ss"]

                    @blk.tensor
                    def _(t, k=k, b=b, ss_l=ss_l):
                        W(t, "ss", ss_l)
                        W(t, "sv", ev6[b])
                        last = None
                        for c in range(NCC):
                            last = t.transpose(
                                ptl[:, b * C + c * P:b * C + (c + 1) * P],
                                ln2bf[b][:, c * P:(c + 1) * P], idn[:, :])
                        tp6[b] = cinc(last, "st")

                    @blk.vector
                    def _(v, k=k, b=b):
                        W(v, "st", tp6[b])
                        v.drain()
                        ev6[b] = cinc(v.tensor_copy(
                            lnfa[:, :].rearrange("p (c t) -> p c t", t=TPC)
                                      [:, :, k * P:(k + 1) * P],
                            ptl[:, b * C:(b + 1) * C]
                                .rearrange("p (c w) -> p c w", w=P)), "sv")
            ln_done = counts["sv"]

            if True:
                SLAB6 = min(512, 6 * C)
                NSL = SLAB6 // P
                rksl = [[mctx.enter_context(
                    nc.sbuf_tensor(f"rksl{j}_{i}", [P, SLAB6], BF16))
                    for i in range(NCC)] for j in range(2)]
                khold = [mctx.enter_context(nc.sbuf_tensor(f"khold{i}", [P, TPC], BF16))
                         for i in range(NCC)]
                vtile = [mctx.enter_context(nc.sbuf_tensor(f"vtile{i}", [P, TPC], BF16))
                         for i in range(2)]
                iota_a = mctx.enter_context(nc.sbuf_tensor("iota_a", [P, TPC], F32))
                revd = mctx.enter_context(nc.sbuf_tensor("revd", [P, TPC], F32))
                et = mctx.enter_context(nc.sbuf_tensor("et", [P, TPC], F32))
                psum6 = [mctx.enter_context(nc.psum_tensor(f"psum6_{i}", [P, TPC], F32))
                         for i in range(2)]

                @blk.gpsimd
                def _(g):
                    cinc(g.iota(iota_a[:, :], [[1, TPC]], channel_multiplier=0,
                                allow_small_or_imprecise_dtypes=True), "sp")
                sp_iota = counts["sp"]

                NC3 = 3 * NCC
                order = []
                for c in range(NCC):
                    order.append(("r", 0, c, c))
                for c in range(NCC):
                    order.append(("r", 1, c, NC3 + c))
                for c in range(NCC):
                    order.append(("k", 0, c, NCC + c))
                    order.append(("v", 0, c, 2 * NCC + c))
                for c in range(NCC):
                    order.append(("k", 1, c, NC3 + NCC + c))
                    order.append(("v", 1, c, NC3 + 2 * NCC + c))

                mm6 = [0, 0]
                evac6 = [{"ss": 0, "sv": 0}, {"ss": 0, "sv": 0}]

                windows = []
                for idx, (kind, dr, c, d6) in enumerate(order):
                    s_ = d6 // NSL
                    if windows and windows[-1][1] == s_:
                        continue
                    windows.append((idx, s_))
                resident = {}
                wbuf, wload = [], []
                prev_b = 1
                for widx, (start, s_) in enumerate(windows):
                    if s_ in resident:
                        wbuf.append(resident[s_])
                        wload.append(False)
                    else:
                        b_ = 1 - prev_b
                        for sid in [k2 for k2, v2 in resident.items() if v2 == b_]:
                            del resident[sid]
                        resident[s_] = b_
                        wbuf.append(b_)
                        wload.append(True)
                    prev_b = wbuf[-1]
                for widx, (start, s_) in enumerate(windows):
                    if not wload[widx]:
                        assert any(windows[j][1] == s_ and wload[j]
                                   for j in range(widx))
                buf_last_st = [0, 0]
                lg_of_buf = [0, 0]

                def load_slab(widx):
                    s_, b_ = windows[widx][1], wbuf[widx]
                    bufs = rksl[b_]

                    @blk.gpsimd
                    def _(g, s_=s_, bufs=bufs, b_=b_):
                        W(g, "st", buf_last_st[b_])
                        if counts["lg"] > 0:
                            g.wait_ge(sems["lg"], counts["lg"])
                        for cc in range(NCC):
                            dma(g, "lg", bufs[cc][:, :],
                                rkvT_in[cc * P:(cc + 1) * P,
                                        s_ * SLAB6:(s_ + 1) * SLAB6],
                                serial=False)
                    lg_of_buf[b_] = counts["lg"]

                load_slab(0)
                cur_w = -1
                for idx, (kind, dr, c, d6) in enumerate(order):
                    slab_id = d6 // NSL
                    if cur_w + 1 < len(windows) and windows[cur_w + 1][0] == idx:
                        cur_w += 1
                        if cur_w + 1 < len(windows) and wload[cur_w + 1]:
                            load_slab(cur_w + 1)

                    pb = idx % 2
                    my_b = wbuf[cur_w]
                    lg_slab, my_slabs = lg_of_buf[my_b], rksl[my_b]
                    off = (d6 % NSL) * P

                    @blk.tensor
                    def _(t, pb=pb, off=off, my_slabs=my_slabs, lg_slab=lg_slab):
                        W(t, "sv", ln_done)
                        W(t, "lg", lg_slab)
                        W(t, "ss", evac6[pb]["ss"])
                        W(t, "sv", evac6[pb]["sv"])
                        last = None
                        for cc in range(NCC):
                            for tt in range(NTT):
                                last = t.matmul(psum6[pb][:, tt * TT:(tt + 1) * TT],
                                                my_slabs[cc][:, off:off + P],
                                                lnfa[:, cc * TPC + tt * TT:
                                                     cc * TPC + (tt + 1) * TT],
                                                start=(cc == 0), stop=(cc == NCC - 1))
                        mm6[pb] = cinc(last, "st")
                    buf_last_st[my_b] = counts["st"]

                    if kind == "r":
                        @blk.scalar
                        def _(s, pb=pb):
                            W(s, "st", mm6[pb])
                            W(s, "lk", counts["lk"])
                            W(s, "sv", counts["sv"])
                            s.drain()
                            evac6[pb]["ss"] = cinc(
                                s.activation(vtile[pb][:, :], psum6[pb][:, :],
                                             AF.Sigmoid), "ss")
                        ss_e = counts["ss"]

                        @blk.sync
                        def _(sy, d6=d6, pb=pb):
                            W(sy, "ss", ss_e)
                            dma(sy, "lk", KVR_d[d6 * P:(d6 + 1) * P, :], vtile[pb][:, :])

                    elif kind == "k":
                        @blk.vector
                        def _(v, c=c, pb=pb):
                            W(v, "st", mm6[pb])
                            W(v, "lk", counts["lk"])
                            v.drain()
                            evac6[pb]["sv"] = cinc(
                                v.tensor_copy(khold[c][:, :], psum6[pb][:, :]), "sv")
                        sv_e = counts["sv"]

                        @blk.sync
                        def _(sy, d6=d6, c=c):
                            W(sy, "sv", sv_e)
                            dma(sy, "lk", KVR_d[d6 * P:(d6 + 1) * P, :], khold[c][:, :])

                    else:
                        @blk.vector
                        def _(v, pb=pb):
                            W(v, "st", mm6[pb])
                            W(v, "lk", counts["lk"])
                            W(v, "ss", counts["ss"])
                            v.drain()
                            evac6[pb]["sv"] = cinc(
                                v.tensor_copy(vtile[pb][:, :], psum6[pb][:, :]), "sv")
                        sv_e = counts["sv"]

                        @blk.sync
                        def _(sy, d6=d6, pb=pb):
                            W(sy, "sv", sv_e)
                            dma(sy, "lk", KVR_d[d6 * P:(d6 + 1) * P, :], vtile[pb][:, :])

                        @blk.vector
                        def _(v, dr=dr, c=c):
                            W(v, "sp", sp_iota)
                            W(v, "ss", counts["ss"])
                            v.drain()
                            wc = wf_col if dr == 0 else wb_col
                            isrc = iota_a[:, ::-1] if dr == 0 else iota_a[:, :]
                            cinc(v.scalar_tensor_tensor(
                                et[:, :], isrc, wc[:, c:c + 1], khold[c][:, :],
                                ALU.mult, ALU.add), "sv")
                        sv_t1 = counts["sv"]

                        @blk.scalar
                        def _(s, dr=dr, c=c):
                            W(s, "sv", sv_t1)
                            s.drain()
                            cinc(s.activation(et[:, :], et[:, :], AF.Exp,
                                              accum_out=scol[:, (2 * dr + 1) * NCC + c:
                                                              (2 * dr + 1) * NCC + c + 1]),
                                 "ss")
                        ss_et = counts["ss"]

                        @blk.vector
                        def _(v, dr=dr, c=c, pb=pb):
                            W(v, "ss", ss_et)
                            v.drain()
                            cinc(v.scalar_tensor_tensor(
                                revd[:, :], et[:, :], 1.0, vtile[pb][:, :],
                                ALU.mult, ALU.mult,
                                accum_out=scol[:, (2 * dr) * NCC + c:
                                               (2 * dr) * NCC + c + 1]), "sv")

                lk_all = counts["lk"]
                sv_sum = counts["sv"]
                ss_sum = counts["ss"]
                m3_st = counts["st"]

        if True:
            ccn = CC_SEMS[cc_used[0]]
            cc_used[0] += 1

            @blk.gpsimd
            def _(g):
                W(g, "sv", sv_sum)
                W(g, "ss", ss_sum)
                dma(g, "lg", scol_i, scol[:, :])
                g.wait_ge(sems["lg"], counts["lg"])
                counts[ccn] += 1
                g.collective_compute(
                    "AllGather", ALU.bypass, replica_groups=RG,
                    ins=[scol_i], outs=[scol_o]).then_inc(sems[ccn])
                g.wait_ge(sems[ccn], counts[ccn])

            with ExitStack() as sctx:
                Gt = sctx.enter_context(nc.sbuf_tensor("Gt", [P, NCORES * NK], F32))
                coefa = sctx.enter_context(nc.sbuf_tensor("coefa", [P, NCORES * NK], F32))
                kbuf = sctx.enter_context(nc.sbuf_tensor("kbuf", [P, TPC], BF16))
                vbuf = sctx.enter_context(nc.sbuf_tensor("vbuf", [P, TPC], BF16))
                sgbuf = [sctx.enter_context(nc.sbuf_tensor(f"sgbuf{i}", [P, TPC], BF16))
                         for i in range(2)]
                ekt = [sctx.enter_context(nc.sbuf_tensor(f"ekt{i}", [P, TPC], F32))
                       for i in range(2)]
                d1t = [sctx.enter_context(nc.sbuf_tensor(f"d1t{i}", [P, TPC], F32))
                       for i in range(2)]
                Apad = [sctx.enter_context(nc.sbuf_tensor(f"Apad{i}", [P, TPC + 1], F32))
                        for i in range(2)]
                Bpad = [sctx.enter_context(nc.sbuf_tensor(f"Bpad{i}", [P, TPC + 1], F32))
                        for i in range(2)]
                obuf = sctx.enter_context(nc.sbuf_tensor("obuf", [P, NCC * TPC], BF16))
                owt = [sctx.enter_context(nc.sbuf_tensor(f"owt{i}", [P, C], BF16))
                       for i in range(NCC)]
                psum4 = [sctx.enter_context(nc.psum_tensor(f"psum4_{i}", [P, C], F32))
                         for i in range(2)]

                @blk.gpsimd
                def _(g):
                    W(g, "st", m3_st)
                    dma(g, "lg", Gt[:, :].rearrange("p (j k) -> p j k", k=NK),
                        scol_o.rearrange("(j p) k -> p j k", p=P))
                lg_g = counts["lg"]

                @blk.vector
                def _(v):
                    W(v, "lg", lg_g)
                    W(v, "ss", ss_sum)
                    v.drain()
                    for j in range(NCORES):
                        v.tensor_mul(coefa[:, j * NK:(j + 1) * NK], W32[:, :],
                                     pfx_t[:, j * NK:(j + 1) * NK])
                    cinc(v.drain(), "sv")
                sv_pc = counts["sv"]

                @blk.scalar
                def _(s):
                    W(s, "sv", sv_pc)
                    s.drain()
                    cinc(s.activation(coefa[:, :], coefa[:, :], AF.Exp), "ss")
                ss_pc = counts["ss"]

                @blk.vector
                def _(v):
                    W(v, "ss", ss_pc)
                    v.drain()
                    v.tensor_mul(coefa[:, :], coefa[:, :], Gt[:, :])
                    v.drain()
                    for j in range(1, NCORES):
                        v.tensor_add(coefa[:, 0:NK], coefa[:, 0:NK],
                                     coefa[:, j * NK:(j + 1) * NK])
                        v.drain()
                    cinc(v.tensor_copy(Ain[:, :], coefa[:, 0:NK]), "sv")
                sv_ain = counts["sv"]

                ek_ss = {}
                d1_sp = {}
                glob_sv = {}
                glob_ss = {}
                o_sv = {}
                kv_tk = {}
                sg_lk = {}
                Ash_v = {}
                Bsh_v = {}
                dirprev_st = {}
                pend = []
                NCH = 2 * NCC
                dirprev_box = [m3_st]

                def chunk_bases(g):
                    dr_, c_ = divmod(g, NCC)
                    return ((3 * dr_ + 0) * C + c_ * P,
                            (3 * dr_ + 1) * C + c_ * P,
                            (3 * dr_ + 2) * C + c_ * P)

                def emit_kvload(g):
                    if g >= NCH:
                        return
                    _, base_k, base_v = chunk_bases(g)

                    @blk.scalar
                    def _(s, g=g, base_k=base_k, base_v=base_v):
                        W(s, "lk", lk_all)
                        W(s, "ss", ek_ss.get(g - 1, 0))
                        W(s, "sp", d1_sp.get(g - 1, 0))
                        if counts["tk"] > 0:
                            s.wait_ge(sems["tk"], counts["tk"])
                        dma(s, "tk", kbuf[:, :], KVR_d[base_k:base_k + P, :],
                            serial=False)
                        dma(s, "tk", vbuf[:, :], KVR_d[base_v:base_v + P, :],
                            serial=False)
                    kv_tk[g] = counts["tk"]

                def emit_sgload(g):
                    if g >= NCH:
                        return
                    base_r, _, _ = chunk_bases(g)
                    w = g % 2

                    @blk.sync
                    def _(sy, g=g, w=w, base_r=base_r):
                        W(sy, "sv", o_sv.get(g - 2, 0))
                        dma(sy, "lk", sgbuf[w][:, :],
                            KVR_d[base_r:base_r + P, :])
                    sg_lk[g] = counts["lk"]

                def emit_ek(g):
                    w = g % 2

                    @blk.scalar
                    def _(s, g=g, w=w):
                        W(s, "tk", kv_tk[g])
                        W(s, "sv", glob_sv.get(g - 2, 0))
                        s.drain()
                        cinc(s.activation(ekt[w][:, :], kbuf[:, :], AF.Exp),
                             "ss")
                    ek_ss[g] = counts["ss"]

                def emit_d1(g):
                    w = g % 2

                    @blk.gpsimd
                    def _(gp, g=g, w=w):
                        W(gp, "ss", ek_ss[g])
                        W(gp, "sv", glob_sv.get(g - 2, 0))
                        gp.drain()
                        cinc(gp.tensor_mul(d1t[w][:, :], ekt[w][:, :],
                                           vbuf[:, :]), "sp")
                    d1_sp[g] = counts["sp"]

                def emit_block(g):
                    dr_, c_ = divmod(g, NCC)
                    w = g % 2
                    ew = ewf_col if dr_ == 0 else ewb_col
                    eu = euf_col if dr_ == 0 else eub_col

                    @blk.vector
                    def _(v, g=g, w=w, dr=dr_, c=c_, ew=ew, eu=eu):
                        W(v, "sp", d1_sp[g])
                        W(v, "sv", sv_ain)
                        v.drain()
                        kA = (2 * dr) * NCC + c
                        kB = (2 * dr + 1) * NCC + c
                        d0ap = ew[:, c:c + 1].broadcast_to([P, TPC])
                        if dr == 0:
                            v.tensor_copy(Apad[w][:, 0:1], Ain[:, kA:kA + 1])
                            v.tensor_copy(Bpad[w][:, 0:1], Ain[:, kB:kB + 1])
                            v.drain()
                            v.tensor_tensor_scan(Apad[w][:, 1:TPC + 1], d0ap,
                                                 d1t[w][:, :], Ain[:, kA:kA + 1],
                                                 ALU.mult, ALU.add)
                            v.drain()
                            v.tensor_tensor_scan(Bpad[w][:, 1:TPC + 1], d0ap,
                                                 ekt[w][:, :], Ain[:, kB:kB + 1],
                                                 ALU.mult, ALU.add)
                            Ash = Apad[w][:, 0:TPC]
                            Bsh = Bpad[w][:, 0:TPC]
                        else:
                            v.tensor_copy(Apad[w][:, TPC:TPC + 1],
                                          Ain[:, kA:kA + 1])
                            v.tensor_copy(Bpad[w][:, TPC:TPC + 1],
                                          Ain[:, kB:kB + 1])
                            v.drain()
                            v.tensor_tensor_scan(Apad[w][:, TPC - 1::-1], d0ap,
                                                 d1t[w][:, ::-1],
                                                 Ain[:, kA:kA + 1],
                                                 ALU.mult, ALU.add)
                            v.drain()
                            v.tensor_tensor_scan(Bpad[w][:, TPC - 1::-1], d0ap,
                                                 ekt[w][:, ::-1],
                                                 Ain[:, kB:kB + 1],
                                                 ALU.mult, ALU.add)
                            Ash = Apad[w][:, 1:TPC + 1]
                            Bsh = Bpad[w][:, 1:TPC + 1]
                        Ash_v[g] = Ash
                        Bsh_v[g] = Bsh
                        v.drain()
                        v.scalar_tensor_tensor(Ash, d1t[w][:, :],
                                               eu[:, c:c + 1], Ash,
                                               ALU.mult, ALU.add)
                        cinc(v.scalar_tensor_tensor(Bsh, ekt[w][:, :],
                                                    eu[:, c:c + 1], Bsh,
                                                    ALU.mult, ALU.add), "sv")
                    glob_sv[g] = counts["sv"]

                def emit_tail(g, w):
                    @blk.scalar
                    def _(s, g=g, w=w):
                        W(s, "sv", glob_sv[g])
                        s.drain()
                        s.activation(Bsh_v[g], Bsh_v[g], AF.Ln)
                        s.drain()
                        cinc(s.activation(Bsh_v[g], Bsh_v[g], AF.Exp,
                                          scale=-1.0), "ss")
                    glob_ss[g] = counts["ss"]

                    @blk.vector
                    def _(v, g=g, w=w):
                        W(v, "ss", glob_ss[g])
                        W(v, "lk", sg_lk[g])
                        W(v, "st", dirprev_st[g])
                        v.drain()
                        v.tensor_mul(Ash_v[g], Ash_v[g], Bsh_v[g])
                        v.drain()
                        cinc(v.tensor_mul(
                            obuf[:, (g % NCC) * TPC:(g % NCC + 1) * TPC],
                            Ash_v[g], sgbuf[w][:, :]), "sv")
                    o_sv[g] = counts["sv"]

                m4_lxo = [0, 0]
                for dr in range(2):
                    if dr == 0:
                        emit_kvload(0)
                        emit_sgload(0)
                        emit_sgload(1)

                    for c in range(NCC):
                        g_ = dr * NCC + c
                        dirprev_st[g_] = dirprev_box[0]
                        emit_ek(g_)
                        emit_d1(g_)
                        emit_kvload(g_ + 1)
                        emit_block(g_)
                        pend.append((g_, g_ % 2))
                        if len(pend) > 1:
                            emit_tail(*pend.pop(0))
                            emit_sgload(g_ + 1)
                    while pend:
                        gl, wl = pend.pop(0)
                        emit_tail(gl, wl)
                        emit_sgload(gl + 2)

                    @blk.gpsimd
                    def _(g, dr=dr):
                        W(g, "st", dirprev_box[0])
                        W(g, "ss", ss_sum)
                        W(g, "sv", sv_ain)
                        if counts["lg"] > 0:
                            g.wait_ge(sems["lg"], counts["lg"])
                        for cc in range(NCC):
                            r0 = dr * C + cc * P
                            dma(g, "lg", owt[cc][:, :], owT_in[r0:r0 + P, :],
                                serial=False)
                    lg_ow = counts["lg"]

                    m4_evac = [0, 0]
                    for k in range(NTC):
                        pb = k % 2

                        @blk.tensor
                        def _(t, dr=dr, k=k, pb=pb, lg_ow=lg_ow):
                            W(t, "sv", max(o_sv[dr * NCC + NCC - 1],
                                           m4_evac[pb]))
                            W(t, "lg", lg_ow)
                            last = None
                            w4 = min(TT, C)
                            for cc in range(NCC):
                                lhs = obuf[:, cc * TPC + k * P:
                                           cc * TPC + (k + 1) * P]
                                for h in range(C // w4):
                                    last = t.matmul(
                                        psum4[pb][:, h * w4:(h + 1) * w4],
                                        lhs, owt[cc][:, h * w4:(h + 1) * w4],
                                        start=(cc == 0), stop=(cc == NCC - 1))
                            cinc(last, "st")
                        st_m4 = counts["st"]

                        @blk.vector
                        def _(v, dr=dr, k=k, pb=pb, st_m4=st_m4):
                            W(v, "st", st_m4)
                            v.drain()
                            src_ = X[:, k * C:(k + 1) * C]
                            m4_evac[pb] = cinc(
                                v.tensor_add(src_, src_, psum4[pb][:, :]), "sv")
                            if dr == 1:
                                nh = max(1, C // 512)
                                step = C // nh
                                for h in range(nh):
                                    v.bn_stats(tmp_col2[:, 6 * h:6 * h + 6],
                                               src_[:, h * step:(h + 1) * step])
                                    v.drain()
                                v.bn_aggr(mvt[:, 2 * k:2 * k + 2],
                                          tmp_col2[:, 0:6 * nh])
                                v.drain()
                                v.tensor_reduce(rmax_t[:, k:k + 1], src_,
                                                AX.X, ALU.max)
                                cinc(v.drain(), "sv")
                    dirprev_box[0] = counts["st"]
        stats_box[0] = counts["sv"]

        barrier()
        return counts["sv"]

    if 'f' in phases:
        ffn_app(0)
    if 'a' in phases:
        attn()
    if phases == 'faf':
        ffn_app(1)

    sv_final = counts["sv"]

    if phases == 'faf':
        lxo_out = counts["lxo"]

        @blk.gpsimd
        def _(g):
            g.wait_ge(sems["lxo"], lxo_out)
    else:
        @blk.sync
        def _(sy):
            W(sy, "sv", sv_final)
            xr = out_d.rearrange("(k p) c -> p k c", p=P)
            g = max(1, NTC // 4)
            for i in range(0, NTC, g):
                j = min(NTC, i + g)
                dma(sy, "lx", xr[:, i:j, :],
                    X[:, i * C:j * C].rearrange("p (k c) -> p k c", c=C))
        lx_out = counts["lx"]

        @blk.gpsimd
        def _(g):
            g.wait_ge(sems["lx"], lx_out)

    return nc


def _prep_inputs(cfg, L0_attn_x, L0_attn_rkv_w, L0_attn_out_w, L0_attn_time_decay,
                 L0_attn_time_first, L0_attn_time_decay_rev, L0_attn_time_first_rev,
                 L0_ffn_w1, L0_ffn_b1, L0_ffn_w2, L0_ffn_b2):
    TPC, C, NCC = cfg.TPC, cfg.C, cfg.NCC
    NK = 4 * NCC
    B, T, _ = L0_attn_x.shape
    cpb = NCORES // B
    assert T == cpb * TPC

    xf = np.ascontiguousarray(np.asarray(L0_attn_x, np.float32).reshape(B * T, C))
    rkvT = np.ascontiguousarray(np.asarray(L0_attn_rkv_w, np.float32).T)
    owT = np.ascontiguousarray(np.asarray(L0_attn_out_w, np.float32).T)

    def colform(a, n):
        return np.ascontiguousarray(np.asarray(a, np.float32).reshape(n, P).T)

    tdec = colform(L0_attn_time_decay, NCC)
    tfir = colform(L0_attn_time_first, NCC)
    tdecr = colform(L0_attn_time_decay_rev, NCC)
    tfirr = colform(L0_attn_time_first_rev, NCC)
    b1c = colform(L0_ffn_b1, cfg.NFC)
    b2c = colform(L0_ffn_b2, NCC)
    w1 = np.ascontiguousarray(np.asarray(L0_ffn_w1, np.float32))
    w2 = np.ascontiguousarray(np.asarray(L0_ffn_w2, np.float32))
    w1T = np.ascontiguousarray(w1.T)
    w2T = np.ascontiguousarray(w2.T)
    F = 4 * C
    SH1 = F // NCORES
    SH2 = C // NCORES

    in_maps = []
    for m in range(NCORES):
        mb = m % cpb
        bix = m // cpb
        pfx = np.full((P, NCORES * NK), 1e35, np.float32)
        for j in range(NCORES):
            jb = j // cpb
            jmb = j % cpb
            if jb != bix:
                continue
            if jmb < mb:
                val = float(TPC * (mb - 1 - jmb))
                pfx[:, j * NK + 0 * NCC:j * NK + 2 * NCC] = val
            if jmb > mb:
                val = float(TPC * (jmb - mb - 1))
                pfx[:, j * NK + 2 * NCC:j * NK + 4 * NCC] = val
        in_maps.append(dict(
            x=np.ascontiguousarray(xf[m * TPC:(m + 1) * TPC]),
            w1s=np.ascontiguousarray(w1[m * SH1:(m + 1) * SH1, :]),
            w2s=np.ascontiguousarray(w2[m * SH2:(m + 1) * SH2, :]),
            w1Ts=np.ascontiguousarray(w1T[:, m * SH1:(m + 1) * SH1]),
            w2Ts=np.ascontiguousarray(w2T[:, m * SH2:(m + 1) * SH2]),
            b1=b1c, b2=b2c, rkvT=rkvT, owT=owT,
            tdec=tdec, tfir=tfir, tdecr=tdecr, tfirr=tfirr, pfx=pfx,
        ))
    return in_maps


def kernel(L0_attn_x, L0_attn_rkv_w, L0_attn_out_w, L0_attn_time_decay,
           L0_attn_time_first, L0_attn_time_decay_rev, L0_attn_time_first_rev,
           L0_ffn_w1, L0_ffn_b1, L0_ffn_w2, L0_ffn_b2):
    cfg = CFG_REAL
    in_maps = _prep_inputs(cfg, L0_attn_x, L0_attn_rkv_w, L0_attn_out_w,
                           L0_attn_time_decay, L0_attn_time_first,
                           L0_attn_time_decay_rev, L0_attn_time_first_rev,
                           L0_ffn_w1, L0_ffn_b1, L0_ffn_w2, L0_ffn_b2)
    nc = build_nc(cfg)
    res = run_bass_kernel_spmd(nc, in_maps, core_ids=list(range(NCORES)))
    B, T, C = L0_attn_x.shape
    out = np.concatenate([res.results[m]["out"] for m in range(NCORES)], axis=0)
    return out.reshape(B, T, C).astype(np.float32)


# version 30
# speedup vs baseline: 1.0168x; 1.0168x over previous
from collections import defaultdict
from contextlib import ExitStack

import numpy as np
import concourse.bass as bass
import concourse.mybir as mybir
from concourse.bass_utils import run_bass_kernel_spmd

F32 = mybir.dt.float32
BF16 = mybir.dt.bfloat16
AF = mybir.ActivationFunctionType
ALU = mybir.AluOpType
AX = mybir.AxisListType

MAGIC = float(np.float32(1.5 * 2 ** 23))
NCORES = 8
P = 128


class Cfg:
    def __init__(self, TPC, C):
        self.TPC = TPC
        self.C = C
        self.F = 4 * C
        self.NCC = C // P
        self.NFC = self.F // P
        self.ND6 = 6 * C // P
        self.NTC = TPC // P
        self.TT = min(512, TPC)
        self.NTT = TPC // self.TT


CFG_REAL = Cfg(2048, 1024)

DMA_STREAMS = ["lx", "lw", "lw2", "lw3", "lw4", "lq", "lq2", "lq3", "lq4",
               "lxq", "lxq2", "lh", "lh2", "lf", "lf2",
               "ll", "ll2", "lk", "tw2", "lxo",
               "tx", "tx2", "tw", "th", "tf", "tl", "tl2", "tk", "tk2",
               "lg"]
CC_SEMS = ["cc0", "cc1", "cc2", "cc3", "cc4", "cc5", "cc6", "cc7"]
ENG_SEMS = ["sv", "ss", "sp", "st"]


def build_nc(cfg, phases='faf', dbg=''):
    TPC, C, F = cfg.TPC, cfg.C, cfg.F
    NCC, NFC, NTC, TT, NTT = cfg.NCC, cfg.NFC, cfg.NTC, cfg.TT, cfg.NTT

    nc = bass.Bass("TRN2", target_bir_lowering=False)

    def ein(name, shape, dt=F32):
        return nc.dram_tensor(name, shape, dt, kind="ExternalInput").ap()

    SH1 = F // NCORES
    SH2 = C // NCORES
    x_in = ein("x", [TPC, C])
    w1s_in = ein("w1s", [SH1, C])
    w2s_in = ein("w2s", [SH2, F])
    w1Ts_in = ein("w1Ts", [C, SH1])
    w2Ts_in = ein("w2Ts", [F, SH2])
    b1_in = ein("b1", [P, NFC])
    b2_in = ein("b2", [P, NCC])
    rkvT_in = ein("rkvT", [C, 6 * C])
    owT_in = ein("owT", [2 * C, C])
    tdec_in = ein("tdec", [P, NCC])
    tfir_in = ein("tfir", [P, NCC])
    tdecr_in = ein("tdecr", [P, NCC])
    tfirr_in = ein("tfirr", [P, NCC])
    pfx_in = ein("pfx", [P, NCORES * 4 * NCC])

    out_d = nc.dram_tensor("out", [TPC, C], F32, kind="ExternalOutput").ap()

    w1q_i = nc.dram_tensor("w1q_i", [C, SH1], BF16).ap()
    w1q_o = nc.dram_tensor("w1q_o", [NCORES * C, SH1], BF16).ap()
    w2q_i = nc.dram_tensor("w2q_i", [F, SH2], BF16).ap()
    w2q_o = nc.dram_tensor("w2q_o", [NCORES * F, SH2], BF16).ap()
    N1T_g = max(1, SH1 // P)
    sc_i = nc.dram_tensor("sc_i", [P, N1T_g + 1], F32).ap()
    sc_o = nc.dram_tensor("sc_o", [NCORES * P, N1T_g + 1], F32).ap()
    XQ_d = nc.dram_tensor("XQ_d", [TPC, C], BF16).ap()
    if "h" in dbg:
        H_d = nc.dram_tensor("H_d", [F, TPC], BF16, kind="ExternalOutput").ap()
    else:
        H_d = nc.dram_tensor("H_d", [F, TPC], BF16).ap()
    XQ2_dbg = (nc.dram_tensor("XQ2_dbg", [TPC, C], BF16, kind="ExternalOutput").ap()
               if "q" in dbg else None)
    F1_d = nc.dram_tensor("F1_d", [C, TPC], BF16).ap()
    XO0_d = nc.dram_tensor("XO0_d", [TPC, C], F32).ap()
    XO1_d = nc.dram_tensor("XO1_d", [TPC, C], F32).ap()
    LN2_d = nc.dram_tensor("LN2_d", [TPC, C], BF16).ap()
    KVR_d = nc.dram_tensor("KVR_d", [6 * C, TPC], BF16).ap()
    gmax_i = nc.dram_tensor("gmax_i", [P, 1], F32).ap()
    gmax_o = nc.dram_tensor("gmax_o", [P * NCORES, 1], F32).ap()
    NK = 4 * NCC
    scol_i = nc.dram_tensor("scol_i", [P, NK], F32).ap()
    scol_o = nc.dram_tensor("scol_o", [P * NCORES, NK], F32).ap()

    RG = [list(range(NCORES))]

    counts = defaultdict(int)
    sems = {}
    ctx = ExitStack()
    nc._kb_ctx = ctx
    for sname in DMA_STREAMS + CC_SEMS + ENG_SEMS:
        sems[sname] = ctx.enter_context(nc.semaphore(sname))
    cc_used = [0]
    stats_box = [None]
    xo_info = {"lxo": 0}

    def dma(q, stream, out, in_, serial=True, **kw):
        if kw.get("transpose"):
            serial = True
        if serial and counts[stream] > 0:
            q.wait_ge(sems[stream], counts[stream])
        counts[stream] += 16
        q.dma_start(out=out, in_=in_, **kw).then_inc(sems[stream], 16)
        return counts[stream]

    def cinc(instr, name):
        counts[name] += 1
        instr.then_inc(sems[name], 1)
        return counts[name]

    def W(eng, name, val):
        if val > 0:
            eng.wait_ge(sems[name], val)

    sb = lambda name, shape, dt=F32: ctx.enter_context(nc.sbuf_tensor(name, shape, dt))

    X = sb("X", [P, NTC * C])
    ones_row = sb("ones_row", [1, P])
    magic_col = sb("magic_col", [P, 1])
    nmagic_col = sb("nmagic_col", [P, 1])
    eps_col = sb("eps_col", [P, 1])
    epsq_col = sb("epsq_col", [P, 1])
    grow = sb("grow", [1, P * NCORES])
    tcols = sb("tcols", [P, 4 * NCC])
    wf_col = sb("wf_col", [P, NCC])
    wb_col = sb("wb_col", [P, NCC])
    euf_col = sb("euf_col", [P, NCC])
    eub_col = sb("eub_col", [P, NCC])
    ewf_col = sb("ewf_col", [P, NCC])
    ewb_col = sb("ewb_col", [P, NCC])
    W32 = sb("W32", [P, NK])
    pfx_t = sb("pfx_t", [P, NCORES * NK])
    b1c = sb("b1c", [P, NFC])
    b2c = sb("b2c", [P, NCC])
    swcol1 = sb("swcol1", [P, NFC])
    swcol2 = sb("swcol2", [P, NCC])
    sprod1 = sb("sprod1", [P, NFC])
    bq1f = sb("bq1f", [P, NFC])
    sprod2h = sb("sprod2h", [P, NCC])
    bq2fh = sb("bq2fh", [P, NCC])
    scol = sb("scol", [P, NK])
    Ain = sb("Ain", [P, NK])
    mvt = sb("mvt", [P, 2 * NTC])
    rstd_t = sb("rstd_t", [P, NTC])
    rmax_t = sb("rmax_t", [P, NTC])
    gm_t = sb("gm_t", [P, NTC])
    gcol = sb("gcol", [P, 1])
    g1 = sb("g1", [1, 1])
    s_col = sb("s_col", [P, 1])
    invs_col = sb("invs_col", [P, 1])
    acol_t = sb("acol_t", [P, NTC])
    bcol_t = sb("bcol_t", [P, NTC])
    hmcols = sb("hmcols", [P, NFC])
    tmp_col = sb("tmp_col", [P, max(NFC, NK)])
    tmp_col2 = sb("tmp_col2", [P, max(NFC, NK, 12)])
    idtf = sb("idtf", [P, P])
    idn = sb("idn", [P, P], BF16)

    blk = ctx.enter_context(nc.Block())

    def barrier():
        marks = {}
        for name, section in (("sv", blk.vector), ("ss", blk.scalar),
                              ("sp", blk.gpsimd)):
            def mk(eng, name=name):
                cinc(eng.drain(), name)
            section(mk)
        for n in ENG_SEMS + DMA_STREAMS:
            marks[n] = counts[n]

        def wait_all(eng):
            for n, v in marks.items():
                W(eng, n, v)
        blk.vector(wait_all)
        blk.scalar(wait_all)
        blk.gpsimd(wait_all)
        blk.tensor(wait_all)
        blk.sync(wait_all)

    @blk.sync
    def _(sy):
        xr = x_in.rearrange("(k p) c -> p k c", p=P)
        g = max(1, NTC // 4)
        for i in range(0, NTC, g):
            j = min(NTC, i + g)
            dma(sy, "lx", X[:, i * C:j * C].rearrange("p (k c) -> p k c", c=C),
                xr[:, i:j, :])
        dma(sy, "lx", tcols[:, 0 * NCC:1 * NCC], tdec_in)
        dma(sy, "lx", tcols[:, 1 * NCC:2 * NCC], tfir_in)
        dma(sy, "lx", tcols[:, 2 * NCC:3 * NCC], tdecr_in)
        dma(sy, "lx", tcols[:, 3 * NCC:4 * NCC], tfirr_in)
        dma(sy, "lx", pfx_t[:, :], pfx_in)
        dma(sy, "lx", b1c[:, :], b1_in)
        dma(sy, "lx", b2c[:, :], b2_in)
    lx_p0 = counts["lx"]

    @blk.gpsimd
    def _(g):
        cinc(g.iota(idtf[:, :], [[1, P]], base=0, channel_multiplier=-1,
                    allow_small_or_imprecise_dtypes=True), "sp")
    sp_idn = counts["sp"]

    @blk.vector
    def _(v):
        v.memset(ones_row[:, :], 1.0)
        v.memset(magic_col[:, :], MAGIC)
        v.memset(nmagic_col[:, :], -MAGIC)
        v.memset(eps_col[:, :], 1e-5)
        v.memset(epsq_col[:, :], 1e-8)
        W(v, "sp", sp_idn)
        v.drain()
        v.tensor_scalar(idn[:, :], idtf[:, :], 0.0, None, ALU.is_equal)
        cinc(v.drain(), "sv")

    @blk.scalar
    def _(s):
        W(s, "lx", lx_p0)
        s.activation(wf_col[:, :], tcols[:, 0:NCC], AF.Exp)
        s.drain()
        s.activation(euf_col[:, :], tcols[:, NCC:2 * NCC], AF.Exp)
        s.drain()
        s.activation(wb_col[:, :], tcols[:, 2 * NCC:3 * NCC], AF.Exp)
        s.drain()
        cinc(s.activation(eub_col[:, :], tcols[:, 3 * NCC:4 * NCC], AF.Exp), "ss")
    ss_w0 = counts["ss"]

    @blk.vector
    def _(v):
        W(v, "ss", ss_w0)
        v.drain()
        v.tensor_scalar(wf_col[:, :], wf_col[:, :], -1.0, None, ALU.mult)
        v.drain()
        v.tensor_scalar(wb_col[:, :], wb_col[:, :], -1.0, None, ALU.mult)
        v.drain()
        v.tensor_copy(W32[:, 0 * NCC:1 * NCC], wf_col[:, :])
        v.drain()
        v.tensor_copy(W32[:, 1 * NCC:2 * NCC], wf_col[:, :])
        v.drain()
        v.tensor_copy(W32[:, 2 * NCC:3 * NCC], wb_col[:, :])
        v.drain()
        cinc(v.tensor_copy(W32[:, 3 * NCC:4 * NCC], wb_col[:, :]), "sv")
    sv_w = counts["sv"]

    @blk.scalar
    def _(s):
        W(s, "sv", sv_w)
        s.drain()
        s.activation(ewf_col[:, :], wf_col[:, :], AF.Exp)
        s.drain()
        cinc(s.activation(ewb_col[:, :], wb_col[:, :], AF.Exp), "ss")


    lq_prep_box = [(0, 0)]
    prep_tail_box = [None]
    lq2_prep_box = [(0, 0)]
    sc_lg_box = [0]
    sc_sv_box = [0]
    ccW1_box = [None]
    ccW2_box = [None]

    def emit_prep(mid_cb=None):
        N1T = max(1, SH1 // P)
        with ExitStack() as pctx:
            nb = [pctx.enter_context(nc.sbuf_tensor(f"nb{i}", [P, C], F32))
                  for i in range(8)]
            s2t = pctx.enter_context(nc.sbuf_tensor("s2t", [P, F // C], F32))
            wTb = [pctx.enter_context(nc.sbuf_tensor(f"wTb{i}", [P, SH1], F32))
                   for i in range(2)]
            qob = [pctx.enter_context(nc.sbuf_tensor(f"qob{i}", [P, SH1], BF16))
                   for i in range(2)]
            bct1 = pctx.enter_context(nc.sbuf_tensor("bct1", [P, SH1], F32))
            bct2 = pctx.enter_context(nc.sbuf_tensor("bct2", [P, SH2], F32))
            s1c = pctx.enter_context(nc.sbuf_tensor("s1c", [P, N1T], F32))
            inv1c = pctx.enter_context(nc.sbuf_tensor("inv1c", [P, N1T], F32))
            s2c = pctx.enter_context(nc.sbuf_tensor("s2c", [P, 1], F32))
            inv2c = pctx.enter_context(nc.sbuf_tensor("inv2c", [P, 1], F32))
            idt = pctx.enter_context(nc.sbuf_tensor("idt", [P, P], F32))
            invr = pctx.enter_context(nc.sbuf_tensor("invr", [1, SH1], F32))
            scg = pctx.enter_context(
                nc.sbuf_tensor("scg", [P, NCORES * (N1T + 1)], F32))

            @blk.gpsimd
            def _(g):
                cinc(g.iota(idt[:, :], [[1, P]], base=0, channel_multiplier=-1,
                            allow_small_or_imprecise_dtypes=True), "sp")
            sp_id = counts["sp"]

            @blk.vector
            def _(v):
                W(v, "sp", sp_id)
                v.drain()
                cinc(v.tensor_scalar(idt[:, :], idt[:, :], 0.0, None,
                                     ALU.is_equal), "sv")
            sv_id = counts["sv"]

            lwp = ["lw", "lw2", "lw3", "lw4"]
            NW2 = F // C
            lw_ab = []

            @blk.sync
            def _(sy):
                for i in range(N1T):
                    dma(sy, lwp[i % 4], nb[i % 4][:, :],
                        w1s_in[i * P:(i + 1) * P, :], serial=False)
                    lw_ab.append(counts[lwp[i % 4]])

            @blk.vector
            def _(v):
                for i in range(N1T):
                    W(v, lwp[i % 4], lw_ab[i])
                    v.drain()
                    v.tensor_reduce(s1c[:, i:i + 1], nb[i % 4][:, :],
                                    AX.X, ALU.max, apply_absolute_value=True)
                v.drain()
                v.tensor_scalar(s1c[:, :], s1c[:, :], 1e-8, 1.0 / 127.0,
                                ALU.max, ALU.mult)
                v.drain()
                cinc(v.reciprocal(inv1c[:, :], s1c[:, :]), "sv")
            sv_ab1 = counts["sv"]
            lw2_ab = []

            @blk.sync
            def _(sy):
                for j in range(NW2):
                    dma(sy, lwp[j % 4], nb[4 + j % 4][0:SH2, :],
                        w2s_in[:, j * C:(j + 1) * C], serial=False)
                    lw2_ab.append(counts[lwp[j % 4]])

            @blk.vector
            def _(v):
                for j in range(NW2):
                    W(v, lwp[j % 4], lw2_ab[j])
                    v.drain()
                    v.tensor_reduce(s2t[0:SH2, j:j + 1], nb[4 + j % 4][0:SH2, :],
                                    AX.X, ALU.max, apply_absolute_value=True)
                v.drain()
                v.tensor_reduce(s2c[0:SH2, 0:1], s2t[0:SH2, 0:NW2], AX.X,
                                ALU.max)
                v.drain()
                v.tensor_scalar(s2c[0:SH2, 0:1], s2c[0:SH2, 0:1], 1e-8,
                                1.0 / 127.0, ALU.max, ALU.mult)
                v.drain()
                cinc(v.reciprocal(inv2c[0:SH2, 0:1], s2c[0:SH2, 0:1]), "sv")
            sv_ab2 = counts["sv"]

            HW = SH1
            p_row = pctx.enter_context(nc.psum_tensor("p_row", [1, HW], F32))
            p_bct = pctx.enter_context(nc.psum_tensor("p_bct", [P, 512], F32))
            sct = pctx.enter_context(nc.sbuf_tensor("sct", [P, N1T + 1], F32))

            @blk.vector
            def _(v):
                W(v, "sv", max(sv_ab1, sv_ab2))
                v.drain()
                if SH2 < P:
                    v.memset(sct[:, :], 0.0)
                    v.drain()
                v.tensor_copy(sct[:, 0:N1T], s1c[:, :])
                v.drain()
                cinc(v.tensor_copy(sct[0:SH2, N1T:N1T + 1], s2c[0:SH2, 0:1]),
                     "sv")
            sv_sr = counts["sv"]

            ccS = CC_SEMS[cc_used[0]]
            cc_used[0] += 1

            @blk.gpsimd
            def _(g):
                W(g, "sv", sv_sr)
                dma(g, "lg", sc_i, sct[:, :])
                g.wait_ge(sems["lg"], counts["lg"])
                counts[ccS] += 1
                g.collective_compute(
                    "AllGather", ALU.bypass, replica_groups=RG,
                    ins=[sc_i], outs=[sc_o]).then_inc(sems[ccS])
                g.wait_ge(sems[ccS], counts[ccS])

            scpool = ["lx", "lk", "lxo", "tw2", "lh", "lh2"]
            sc_tot = {}

            @blk.sync
            def _(sy):
                W(sy, ccS, 1)
                for m in range(NCORES):
                    st_ = scpool[m % len(scpool)]
                    dma(sy, st_, scg[:, m * (N1T + 1):(m + 1) * (N1T + 1)],
                        sc_o[m * P:(m + 1) * P, :], serial=False)
            for st_ in scpool:
                sc_tot[st_] = counts[st_]

            @blk.vector
            def _(v):
                for st_ in scpool:
                    W(v, st_, sc_tot[st_])
                v.drain()
                for m in range(NCORES):
                    v.tensor_copy(swcol1[:, m * N1T:(m + 1) * N1T],
                                  scg[:, m * (N1T + 1):m * (N1T + 1) + N1T])
                    v.drain()
                    if SH2 >= P:
                        v.tensor_copy(swcol2[:, m:m + 1],
                                      scg[:, m * (N1T + 1) + N1T:
                                          (m + 1) * (N1T + 1)])
                        v.drain()
                cinc(v.nop(), "sv")
            sc_sv_box[0] = counts["sv"]

            if SH2 < P:
                @blk.gpsimd
                def _(g):
                    W(g, "sv", sc_sv_box[0])
                    for m in range(NCORES):
                        cc0_ = (m * SH2) // P
                        pa = (m * SH2) % P
                        dma(g, "lg", swcol2[pa:pa + SH2, cc0_:cc0_ + 1],
                            scg[0:SH2, m * (N1T + 1) + N1T:
                                (m + 1) * (N1T + 1)])
            sc_lg_box[0] = counts["lg"]

            if mid_cb is not None:
                mid_cb()

            def rows_and_bcast(invc, ncol, wid, bct, sv_gate):
                @blk.tensor
                def _(t):
                    W(t, "sv", max(sv_gate, sv_id))
                    last = None
                    for i in range(ncol):
                        kk = P if wid // ncol == P else wid // ncol
                        last = t.matmul(p_row[0:1, i * kk:(i + 1) * kk],
                                        invc[0:kk, i:i + 1], idt[0:kk, 0:kk],
                                        start=True, stop=True)
                    cinc(last, "st")
                st_r = counts["st"]

                @blk.vector
                def _(v):
                    W(v, "st", st_r)
                    v.drain()
                    cinc(v.tensor_copy(invr[0:1, 0:wid], p_row[0:1, 0:wid]),
                         "sv")
                sv_r = counts["sv"]

                ws = min(512, wid)
                for sl in range(wid // ws):
                    @blk.tensor
                    def _(t, sl=sl, ws=ws):
                        W(t, "sv", counts["sv"])
                        cinc(t.matmul(p_bct[:, 0:ws], ones_row[:, :],
                                      invr[0:1, sl * ws:(sl + 1) * ws],
                                      start=True, stop=True), "st")
                    st_b = counts["st"]

                    @blk.vector
                    def _(v, sl=sl, ws=ws, bct=bct):
                        W(v, "st", st_b)
                        v.drain()
                        cinc(v.tensor_copy(bct[:, sl * ws:(sl + 1) * ws],
                                           p_bct[:, 0:ws]), "sv")
                return counts["sv"]

            def quantT(nch, width, srcT, dst, bct, qs, qs2, lsa, lsb, sv_bct,
                       dsec=None):
                quant_done = [0, 0]
                spill_done = [0, 0]
                lq_t0 = {n: counts[n] for n in ("lq", "lq2", "lq3", "lq4")}
                if dsec is None:
                    dsec = blk.sync

                def one(ci):
                    b = ci % 2
                    ls = lsa if b == 0 else lsb
                    qsb = qs if b == 0 else qs2

                    @dsec
                    def _(sy, ci=ci, b=b, ls=ls):
                        W(sy, "sv", quant_done[b])
                        dma(sy, ls, wTb[b][:, 0:width],
                            srcT[ci * P:(ci + 1) * P, :], serial=False)
                    lw_m = counts[ls]

                    @blk.vector
                    def _(v, ci=ci, b=b, ls=ls, lw_m=lw_m, sv_bct=sv_bct,
                          qsb=qsb, width=width, bct=bct):
                        W(v, ls, lw_m)
                        W(v, "sv", sv_bct)
                        for n in ("lq", "lq2", "lq3", "lq4"):
                            W(v, n, lq_t0[n])
                        W(v, qsb, spill_done[b])
                        v.drain()
                        v.tensor_mul(wTb[b][:, 0:width], wTb[b][:, 0:width],
                                     bct[:, 0:width])
                        v.drain()
                        v.tensor_scalar(wTb[b][:, 0:width], wTb[b][:, 0:width],
                                        MAGIC, MAGIC, ALU.add, ALU.subtract)
                        v.drain()
                        cinc(v.tensor_scalar(qob[b][:, 0:width],
                                             wTb[b][:, 0:width],
                                             127.0, -128.0, ALU.min, ALU.max),
                             "sv")
                    quant_done[b] = counts["sv"]
                    sv_qq = counts["sv"]

                    @dsec
                    def _(sy, ci=ci, b=b, sv_qq=sv_qq, qsb=qsb, width=width):
                        W(sy, "sv", sv_qq)
                        dma(sy, qsb, dst[ci * P:(ci + 1) * P, :],
                            qob[b][:, 0:width], serial=False)
                    spill_done[b] = counts[qsb]
                return [lambda ci=ci: one(ci) for ci in range(nch)]

            sv_b1c = rows_and_bcast(inv1c, N1T, SH1, bct1,
                                    max(sv_ab1, sv_sr))
            for f_ in quantT(NCC, SH1, w1Ts_in, w1q_i, bct1, "lq", "lq3",
                             "ll", "ll2", sv_b1c):
                f_()
            lq_prep_box[0] = (counts["lq"], counts["lq3"])

            ccW1 = CC_SEMS[cc_used[0]]
            cc_used[0] += 1
            ccW1_box[0] = ccW1

            @blk.gpsimd
            def _(g):
                W(g, "lq", counts["lq"])
                W(g, "lq3", counts["lq3"])
                counts[ccW1] += 1
                g.collective_compute(
                    "AllGather", ALU.bypass, replica_groups=RG,
                    ins=[w1q_i], outs=[w1q_o]).then_inc(sems[ccW1])

            sv_b2c = rows_and_bcast(inv2c, 1, SH2, bct2, sv_ab2)
            w2_pieces = quantT(NFC, SH2, w2Ts_in, w2q_i, bct2, "lq2", "lq4",
                               "lf", "lf2", sv_b2c)

            ccW2 = CC_SEMS[cc_used[0]]
            cc_used[0] += 1
            ccW2_box[0] = ccW2

            def emit_w2_ag():
                @blk.gpsimd
                def _(g):
                    W(g, "lq2", counts["lq2"])
                    W(g, "lq4", counts["lq4"])
                    counts[ccW2] += 1
                    g.collective_compute(
                        "AllGather", ALU.bypass, replica_groups=RG,
                        ins=[w2q_i], outs=[w2q_o]).then_inc(sems[ccW2])
            prep_tail_box[0] = (w2_pieces, emit_w2_ag)

    def ln_stats(want_max, xo=None, stats_done=None):
        nh = max(1, C // 512)
        step = C // nh
        if stats_done is not None:
            pass
        elif xo is None:
            @blk.vector
            def _(v):
                W(v, "lx", lx_p0)
                v.drain()
                for k in range(NTC):
                    src = X[:, k * C:(k + 1) * C]
                    for h in range(nh):
                        v.bn_stats(tmp_col2[:, 6 * h:6 * h + 6],
                                   src[:, h * step:(h + 1) * step])
                        v.drain()
                    v.bn_aggr(mvt[:, 2 * k:2 * k + 2], tmp_col2[:, 0:6 * nh])
                    v.drain()
                    if want_max:
                        v.tensor_reduce(rmax_t[:, k:k + 1], src, AX.X, ALU.max)
                        v.drain()
                cinc(v.nop(), "sv")
        else:
            XO0, XO1, lxo_tot = xo
            with ExitStack() as xctx:
                xot = [[xctx.enter_context(
                    nc.sbuf_tensor(f"xot{j}_{i}", [P, C], F32))
                    for i in range(2)] for j in range(2)]
                add_sv = [0, 0]
                tile_done = [0, 0]
                for k in range(NTC):
                    pb = k % 2

                    @blk.sync
                    def _(sy, k=k, pb=pb):
                        W(sy, "sv", add_sv[pb])
                        if counts["lxo"] > 0:
                            sy.wait_ge(sems["lxo"], counts["lxo"])
                        dma(sy, "lxo", xot[0][pb][:, :],
                            XO0[k * P:(k + 1) * P, :], serial=False)
                        dma(sy, "lxo", xot[1][pb][:, :],
                            XO1[k * P:(k + 1) * P, :], serial=False)
                    tile_done[pb] = counts["lxo"]

                    @blk.vector
                    def _(v, k=k, pb=pb):
                        W(v, "lxo", tile_done[pb])
                        v.drain()
                        src = X[:, k * C:(k + 1) * C]
                        v.tensor_add(src, src, xot[0][pb][:, :])
                        v.drain()
                        add_sv[pb] = cinc(
                            v.tensor_add(src, src, xot[1][pb][:, :]), "sv")
                        for h in range(nh):
                            v.bn_stats(tmp_col2[:, 6 * h:6 * h + 6],
                                       src[:, h * step:(h + 1) * step])
                            v.drain()
                        v.bn_aggr(mvt[:, 2 * k:2 * k + 2], tmp_col2[:, 0:6 * nh])
                        v.drain()
                        if want_max:
                            v.tensor_reduce(rmax_t[:, k:k + 1], src, AX.X, ALU.max)
                            v.drain()
                        cinc(v.nop(), "sv")
        sv_stats = counts["sv"] if stats_done is None else stats_done

        @blk.scalar
        def _(s):
            W(s, "sv", sv_stats)
            s.drain()
            cinc(s.activation(rmax_t[:, :] if not want_max else gm_t[:, :],
                              mvt[:, 1::2], AF.Sqrt, bias=eps_col[:, :]), "ss")
        ss_sd = counts["ss"]

        @blk.vector
        def _(v):
            W(v, "ss", ss_sd)
            v.drain()
            v.reciprocal(rstd_t[:, :], gm_t[:, :] if want_max else rmax_t[:, :])
            v.drain()
            if want_max:
                v.tensor_sub(gm_t[:, :], rmax_t[:, :], mvt[:, 0::2])
                v.drain()
                v.tensor_mul(gm_t[:, :], gm_t[:, :], rstd_t[:, :])
                v.drain()
                v.tensor_reduce(gcol[:, :], gm_t[:, :], AX.X, ALU.max)
                v.drain()
            cinc(v.nop(), "sv")
        return counts["sv"]

    def gmax_collective(sv_ready):
        ccn = CC_SEMS[cc_used[0]]
        cc_used[0] += 1

        @blk.gpsimd
        def _(g):
            W(g, "sv", sv_ready)
            dma(g, "lg", gmax_i, gcol[:, :])
            g.wait_ge(sems["lg"], counts["lg"])
            counts[ccn] += 1
            g.collective_compute(
                "AllGather", ALU.bypass, replica_groups=RG,
                ins=[gmax_i], outs=[gmax_o]).then_inc(sems[ccn])
            g.wait_ge(sems[ccn], counts[ccn])
            dma(g, "lg", grow[:, :], gmax_o.rearrange("(a b) c -> b (a c)", b=1))
        lg_row = counts["lg"]

        @blk.vector
        def _(v):
            W(v, "lg", lg_row)
            v.drain()
            cinc(v.reduce_max(g1[:, :], grow[:, :], AX.X), "sv")
        sv_g1 = counts["sv"]

        with nc.psum_tensor(f"p_bc{cc_used[0]}", [P, 1], F32) as p_bc:
            @blk.tensor
            def _(t):
                W(t, "sv", sv_g1)
                cinc(t.matmul(p_bc[:, :], ones_row[:, :], g1[:, :],
                              start=True, stop=True), "st")
            st_bc = counts["st"]

            @blk.vector
            def _(v):
                W(v, "st", st_bc)
                v.drain()
                v.tensor_scalar(s_col[:, :], p_bc[:, :], epsq_col[:, :], 1.0 / 255.0,
                                ALU.max, ALU.mult)
                v.drain()
                cinc(v.reciprocal(invs_col[:, :], s_col[:, :]), "sv")
        return counts["sv"]

    def bias_quant(sv_scales, sw, bc, n, sp_out, bq_out, halve):
        @blk.vector
        def _(v):
            W(v, "sv", max(sv_scales, sc_sv_box[0]))
            W(v, "lg", sc_lg_box[0])
            v.drain()
            v.tensor_scalar(sp_out[:, 0:n], sw[:, 0:n], s_col[:, :], None, ALU.mult)
            v.drain()
            v.reciprocal(tmp_col[:, 0:n], sp_out[:, 0:n])
            v.drain()
            v.tensor_mul(tmp_col2[:, 0:n], bc[:, 0:n], tmp_col[:, 0:n])
            v.drain()
            v.tensor_scalar(tmp_col2[:, 0:n], tmp_col2[:, 0:n], MAGIC, MAGIC,
                            ALU.add, ALU.subtract)
            v.drain()
            v.tensor_mul(bq_out[:, 0:n], tmp_col2[:, 0:n], sp_out[:, 0:n])
            v.drain()
            if halve:
                v.tensor_scalar(bq_out[:, 0:n], bq_out[:, 0:n], 0.5, None, ALU.mult)
                v.drain()
                v.tensor_scalar(sp_out[:, 0:n], sp_out[:, 0:n], 0.5, None, ALU.mult)
                v.drain()
            cinc(v.nop(), "sv")
        return counts["sv"]

    def ffn_app(app):
        xo = ((XO0_d, XO1_d, xo_info["lxo"])
              if app == 1 and xo_info["lxo"] > 0 else None)
        sfx = f"_a{app}"
        ph_st, ph_sv, ph_ss, ph_sp = (counts["st"], counts["sv"],
                                      counts["ss"], counts["sp"])
        sv0 = ln_stats(want_max=True, xo=xo,
                       stats_done=stats_box[0] if app == 1 else None)
        sv1 = gmax_collective(sv0)

        fctx = ExitStack()
        qt = [fctx.enter_context(nc.sbuf_tensor(f"qt_f{i}{sfx}", [P, C], F32))
              for i in range(2)]
        qbf = [fctx.enter_context(nc.sbuf_tensor(f"qbf_f{i}{sfx}", [P, C], BF16))
               for i in range(2)]
        xqa = fctx.enter_context(
            nc.sbuf_tensor(f"xqa{sfx}", [P, NCC * TPC], BF16))
        xq_box = [0, 0]

        def emit_xq():
            @blk.vector
            def _(v):
                W(v, "sv", sv1)
                v.drain()
                v.tensor_scalar(acol_t[:, :], rstd_t[:, :], invs_col[:, :],
                                None, ALU.mult)
                v.drain()
                v.tensor_mul(bcol_t[:, :], mvt[:, 0::2], acol_t[:, :])
                v.drain()
                cinc(v.tensor_scalar(bcol_t[:, :], bcol_t[:, :], -1.0, MAGIC,
                                     ALU.mult, ALU.add), "sv")
            sv_a = counts["sv"]

            tp_st = [0, 0]
            ev_ss = [0, 0]
            mn_sv = [0, 0]
            rl_ss = [0, 0]
            with nc.psum_tensor(f"ptx{sfx}", [P, 2 * C], BF16) as ptx:
                for k in range(NTC):
                    b = k % 2

                    @blk.scalar
                    def _(s, k=k, b=b):
                        W(s, "sv", max(sv_a, mn_sv[b]))
                        W(s, "ss", ph_ss)
                        s.drain()
                        s.activation(qt[b][:, :], X[:, k * C:(k + 1) * C],
                                     AF.Relu, bias=bcol_t[:, k:k + 1],
                                     scale=acol_t[:, k:k + 1])
                        s.drain()
                        rl_ss[b] = cinc(s.activation(qt[b][:, :], qt[b][:, :],
                                                     AF.Relu,
                                                     bias=nmagic_col[:, :]),
                                        "ss")

                    @blk.vector
                    def _(v, k=k, b=b):
                        W(v, "ss", rl_ss[b])
                        W(v, "st", max(ph_st, tp_st[b]))
                        W(v, "sp", ph_sp)
                        v.drain()
                        mn_sv[b] = cinc(v.tensor_scalar(qbf[b][:, :],
                                                        qt[b][:, :], 255.0,
                                                        None, ALU.min), "sv")

                    @blk.tensor
                    def _(t, k=k, b=b):
                        W(t, "sv", max(mn_sv[b], ev_ss[b]))
                        last = None
                        for c in range(NCC):
                            last = t.transpose(
                                ptx[:, b * C + c * P:b * C + (c + 1) * P],
                                qbf[b][:, c * P:(c + 1) * P], idn[:, :])
                        tp_st[b] = cinc(last, "st")

                    @blk.vector
                    def _(v, k=k, b=b):
                        W(v, "st", tp_st[b])
                        v.drain()
                        ev_ss[b] = cinc(v.tensor_copy(
                            xqa[:, :].rearrange("p (c t) -> p c t", t=TPC)
                                     [:, :, k * P:(k + 1) * P],
                            ptx[:, b * C:(b + 1) * C]
                                .rearrange("p (c w) -> p c w", w=P)), "sv")
            xq_box[0] = counts["sv"]
            xq_box[1] = bias_quant(sv1, swcol1, b1c, NFC, sprod1, bq1f,
                                   halve=False)

        if app == 0:
            emit_prep(mid_cb=emit_xq)
        else:
            emit_xq()
        xq_done = xq_box[0]
        sv_b1 = xq_box[1]

        if True:
            w2slab = [fctx.enter_context(nc.sbuf_tensor(f"w2slab{i}{sfx}", [P, C], BF16))
                      for i in range(NFC)]

            tw2_hold = [0]

            with ExitStack() as m1ctx:
                SLAB = min(256, F)
                NSLAB = SLAB // P
                w1slab2 = [m1ctx.enter_context(
                    nc.sbuf_tensor(f"w1slab{j}{sfx}", [P, NCC * SLAB], BF16))
                    for j in range(2)]
                wv1 = [w1slab2[j][:, :].rearrange("p (c w) -> p c w", w=SLAB)
                       for j in range(2)]
                hbf = [m1ctx.enter_context(nc.sbuf_tensor(f"hbf{i}{sfx}", [P, TPC], BF16))
                       for i in range(2)]

                def emit_tw2():
                    @blk.sync
                    def _(sy):
                        W(sy, ccW2_box[0], 1)
                        W(sy, "st", ph_st)
                        W(sy, "sv", ph_sv)
                        W(sy, "ss", ph_ss)
                        W(sy, "sp", ph_sp)
                        if counts["tw2"] > 0:
                            sy.wait_ge(sems["tw2"], counts["tw2"])
                        for fc in range(NFC):
                            for m in range(NCORES):
                                dma(sy, "tw2",
                                    w2slab[fc][:, m * SH2:(m + 1) * SH2],
                                    w2q_o[m * F + fc * P:m * F + (fc + 1) * P, :],
                                    serial=False)
                    tw2_hold[0] = counts["tw2"]
                if app != 0:
                    emit_tw2()

                with ExitStack() as mctx:
                    psumH = [mctx.enter_context(
                        nc.psum_tensor(f"psumH{i}{sfx}", [P, TPC], F32))
                        for i in range(2)]
                    tw_slab = [0, 0]
                    st_slab_done = [0, 0]
                    evac_ss = [0, 0]
                    st_mm = [0, 0]
                    hspill = [("lh", 0), ("lh2", 0)]

                    NSLABS = NFC // NSLAB

                    def slab_load(sg):
                        sbf = sg % 2

                        @blk.scalar
                        def _(s, sg=sg, sbf=sbf):
                            W(s, "st", st_slab_done[sbf])
                            W(s, ccW1_box[0], 1)
                            if sg == 0:
                                W(s, "sv", ph_sv)
                                W(s, "sp", ph_sp)
                            if counts["tw"] > 0:
                                s.wait_ge(sems["tw"], counts["tw"])
                            fa = sg * SLAB
                            fb = fa + SLAB
                            for c in range(NCC):
                                for m in range(fa // SH1, (fb - 1) // SH1 + 1):
                                    la = max(fa, m * SH1) - m * SH1
                                    lb = min(fb, (m + 1) * SH1) - m * SH1
                                    off = m * SH1 + la - fa
                                    dma(s, "tw",
                                        w1slab2[sbf][:, c * SLAB + off:
                                                     c * SLAB + off + lb - la],
                                        w1q_o[m * C + c * P:m * C + (c + 1) * P,
                                              la:lb],
                                        serial=False)
                        tw_slab[sbf] = counts["tw"]

                    slab_load(0)
                    for fc in range(NFC):
                        sg = fc // NSLAB
                        sbf = sg % 2
                        if fc % NSLAB == 0 and sg + 1 < NSLABS:
                            slab_load(sg + 1)

                        b = fc % 2
                        hb = fc % 2

                        @blk.tensor
                        def _(t, fc=fc, b=b, sbf=sbf):
                            W(t, "tw", tw_slab[sbf])
                            W(t, "sv", max(xq_done, sv_b1))
                            W(t, "ss", evac_ss[b])
                            off = (fc % NSLAB) * P
                            last = None
                            for c in range(NCC):
                                for tt in range(NTT):
                                    last = t.matmul(
                                        psumH[b][:, tt * TT:(tt + 1) * TT],
                                        w1slab2[sbf][:, c * SLAB + off:
                                                     c * SLAB + off + P],
                                        xqa[:, c * TPC + tt * TT:
                                            c * TPC + (tt + 1) * TT],
                                        start=(c == 0), stop=(c == NCC - 1))
                            st_mm[b] = cinc(last, "st")
                        if fc % NSLAB == NSLAB - 1:
                            st_slab_done[sbf] = counts["st"]

                        @blk.scalar
                        def _(s, fc=fc, b=b, hb=hb):
                            W(s, "st", st_mm[b])
                            W(s, hspill[hb][0], hspill[hb][1])
                            s.drain()
                            evac_ss[b] = cinc(
                                s.activation(hbf[hb][:, :], psumH[b][:, :], AF.Relu,
                                             bias=bq1f[:, fc:fc + 1],
                                             scale=sprod1[:, fc:fc + 1]), "ss")

                        @blk.vector
                        def _(v, fc=fc, b=b, hb=hb):
                            W(v, "ss", evac_ss[b])
                            v.drain()
                            cinc(v.tensor_reduce(hmcols[:, fc:fc + 1], hbf[hb][:, :],
                                                 AX.X, ALU.max), "sv")
                        sv_hm = counts["sv"]

                        hs = "lh" if fc % 2 == 0 else "lh2"

                        @blk.gpsimd
                        def _(g, fc=fc, hb=hb, hs=hs):
                            W(g, "sv", sv_hm)
                            dma(g, hs, H_d[fc * P:(fc + 1) * P, :], hbf[hb][:, :])
                        hspill[hb] = (hs, counts[hs])
                        if prep_tail_box[0] is not None and fc >= 2:
                            pieces, agf = prep_tail_box[0]
                            for _ in range(3):
                                if pieces:
                                    pieces.pop(0)()
                            if not pieces:
                                agf()
                                emit_tw2()
                                prep_tail_box[0] = None

                    @blk.vector
                    def _(v):
                        v.drain()
                        cinc(v.tensor_reduce(gcol[:, :], hmcols[:, :], AX.X, ALU.max),
                             "sv")
                sv_hmax = counts["sv"]
                m1_st = counts["st"]
                m1_ss = counts["ss"]
                m1_lh = counts["lh"]
                m1_lh2 = counts["lh2"]

            sv2 = gmax_collective(sv_hmax)
            sv_b2 = bias_quant(sv2, swcol2, b2c, NCC, sprod2h, bq2fh, halve=True)

            with ExitStack() as m2ctx:
                hin = [xqa]
                hq32 = m2ctx.enter_context(
                    nc.sbuf_tensor(f"hq32{sfx}", [P, 2 * TT], F32))
                f1bf = [m2ctx.enter_context(
                    nc.sbuf_tensor(f"f1bf{i}{sfx}", [P, TT], BF16))
                    for i in range(NCC)]
                f1tm = [m2ctx.enter_context(
                    nc.sbuf_tensor(f"f1tm{i}{sfx}", [P, C], BF16)) for i in range(2)]

                with ExitStack() as mctx:
                    psumC = [mctx.enter_context(
                        nc.psum_tensor(f"psumC{i}{sfx}", [P, TT], F32))
                        for i in range(NCC)]
                    NHALF = min(2, NFC)
                    NH = NFC // NHALF
                    half_free_st = [[0, 0]] + [[0, 0] for _ in range(NTT)]
                    evac_sv2 = [0] * NCC
                    spill_done = [("lf" if c % 2 == 0 else "lf2", 0)
                                  for c in range(NCC)]
                    lf_tile = []
                    hb = 0
                    st_half = [[0, 0] for _ in range(NTT)]
                    sv_hq = [[0, 0] for _ in range(NTT)]

                    def emit_load(tt, half):
                        f0, f1_ = half * NH, (half + 1) * NH
                        th_prev = counts["th"]

                        @blk.scalar
                        def _(s, tt=tt, half=half, f0=f0, f1_=f1_,
                              th_prev=th_prev):
                            W(s, "st", max(half_free_st[tt][half], m1_st))
                            W(s, "ss", m1_ss)
                            W(s, "lh", m1_lh)
                            W(s, "lh2", m1_lh2)
                            W(s, "th", th_prev)
                            nsp = max(1, NH // 8)
                            step = NH // nsp
                            for i in range(nsp):
                                a = f0 + i * step
                                b2_ = f0 + (i + 1) * step
                                dma(s, "th",
                                    hin[hb][:, a * TT:b2_ * TT]
                                        .rearrange("p (f t) -> p f t", t=TT),
                                    H_d[a * P:b2_ * P, tt * TT:(tt + 1) * TT]
                                        .rearrange("(f p) t -> p f t", p=P),
                                    serial=False)
                        return counts["th"]

                    def emit_quant(tt, half, th_half):
                        f0, f1_ = half * NH, (half + 1) * NH
                        GQ = 2
                        for fg in range(f0, f1_, GQ):
                            gn = min(GQ, f1_ - fg)

                            @blk.vector
                            def _(v, fg=fg, gn=gn, th_half=th_half):
                                W(v, "th", th_half)
                                W(v, "sv", sv_b2)
                                W(v, "lh", m1_lh)
                                W(v, "lh2", m1_lh2)
                                W(v, "st", m1_st)
                                v.drain()
                                for i in range(gn):
                                    fc = fg + i
                                    v.tensor_scalar(
                                        hq32[:, i * TT:(i + 1) * TT],
                                        hin[hb][:, fc * TT:(fc + 1) * TT],
                                        invs_col[:, :], MAGIC, ALU.mult, ALU.add)
                                v.drain()
                                for i in range(gn):
                                    fc = fg + i
                                    cinc(v.tensor_scalar(
                                        hin[hb][:, fc * TT:(fc + 1) * TT],
                                        hq32[:, i * TT:(i + 1) * TT], MAGIC,
                                        255.0, ALU.subtract, ALU.min), "sv")
                        sv_hq[tt][half] = counts["sv"]

                    def emit_mm(tt, half):
                        f0, f1_ = half * NH, (half + 1) * NH
                        for c in range(NCC):
                            @blk.tensor
                            def _(t, tt=tt, c=c, half=half, f0=f0, f1_=f1_):
                                W(t, "tw2", tw2_hold[0])
                                W(t, "sv", max(sv_hq[tt][half], evac_sv2[c]))
                                last = None
                                for fc in range(f0, f1_):
                                    last = t.matmul(
                                        psumC[c][:, :],
                                        w2slab[fc][:, c * P:(c + 1) * P],
                                        hin[hb][:, fc * TT:(fc + 1) * TT],
                                        start=(fc == 0), stop=(fc == NFC - 1))
                                cinc(last, "st")
                        st_half[tt][half] = counts["st"]
                        if tt + 1 <= NTT - 1:
                            half_free_st[tt + 1][half] = counts["st"]

                    def emit_evacs(tt):
                        for c in range(NCC):
                            @blk.vector
                            def _(v, c=c, tt=tt):
                                W(v, "st", st_half[tt][NHALF - 1])
                                W(v, spill_done[c][0], spill_done[c][1])
                                v.drain()
                                evac_sv2[c] = cinc(
                                    v.tensor_scalar(f1bf[c][:, :], psumC[c][:, :],
                                                    sprod2h[:, c:c + 1],
                                                    bq2fh[:, c:c + 1],
                                                    ALU.mult, ALU.add), "sv")

                            fs = "lf" if c % 2 == 0 else "lf2"

                            @blk.sync
                            def _(sy, tt=tt, c=c, fs=fs):
                                W(sy, "sv", evac_sv2[c])
                                dma(sy, fs,
                                    F1_d[c * P:(c + 1) * P, tt * TT:(tt + 1) * TT],
                                    f1bf[c][:, :])
                            spill_done[c] = (fs, counts[fs])
                        lf_tile.append((counts["lf"], counts["lf2"]))

                    for tt in range(NTT):
                        th0 = emit_load(tt, 0)
                        emit_quant(tt, 0, th0)
                        if tt > 0:
                            emit_evacs(tt - 1)
                        emit_mm(tt, 0)
                        if NHALF > 1:
                            th1 = emit_load(tt, 1)
                            emit_quant(tt, 1, th1)
                            emit_mm(tt, 1)
                    emit_evacs(NTT - 1)

                add_done = [0, 0]
                for k in range(NTC):
                    b = k % 2
                    lfv, lf2v = lf_tile[min(k // (NTC // NTT), NTT - 1)]

                    @blk.scalar
                    def _(s, k=k, b=b, lfv=lfv, lf2v=lf2v):
                        W(s, "lf", lfv)
                        W(s, "lf2", lf2v)
                        W(s, "sv", add_done[b])
                        dma(s, "tf", f1tm[b][:, :], F1_d[:, k * P:(k + 1) * P],
                            transpose=True)
                    tf_k = counts["tf"]

                    @blk.vector
                    def _(v, k=k, b=b):
                        W(v, "tf", tf_k)
                        v.drain()
                        add_done[b] = cinc(
                            v.tensor_add(X[:, k * C:(k + 1) * C],
                                         X[:, k * C:(k + 1) * C],
                                         f1tm[b][:, :]), "sv")
                    if app == 1:
                        @blk.sync
                        def _(sy, k=k, val=add_done[b]):
                            W(sy, "sv", val)
                            dma(sy, "lxo",
                                out_d.rearrange("(j p) c -> p j c", p=P)
                                     [:, k:k + 1, :],
                                X[:, k * C:(k + 1) * C]
                                    .rearrange("p (j c) -> p j c", c=C))
        fctx.close()
        barrier()
        return counts["sv"]

    def attn():
        sv0 = ln_stats(want_max=False)

        with ExitStack() as mctx:
            ln2bf = [mctx.enter_context(nc.sbuf_tensor(f"ln2bf{i}", [P, C], BF16))
                     for i in range(2)]
            lnfa = mctx.enter_context(
                nc.sbuf_tensor("lnfa", [P, NCC * TPC], BF16))

            @blk.vector
            def _(v):
                W(v, "sv", sv0)
                v.drain()
                v.tensor_mul(bcol_t[:, :], mvt[:, 0::2], rstd_t[:, :])
                v.drain()
                cinc(v.tensor_scalar(bcol_t[:, :], bcol_t[:, :], -1.0, None,
                                     ALU.mult), "sv")
            sv_lb = counts["sv"]

            tp6 = [0, 0]
            ev6 = [0, 0]
            with nc.psum_tensor("ptl", [P, 2 * C], BF16) as ptl:
                for k in range(NTC):
                    b = k % 2

                    @blk.scalar
                    def _(s, k=k, b=b):
                        W(s, "sv", sv_lb)
                        W(s, "st", tp6[b])
                        s.drain()
                        cinc(s.activation(ln2bf[b][:, :], X[:, k * C:(k + 1) * C],
                                          AF.Identity, bias=bcol_t[:, k:k + 1],
                                          scale=rstd_t[:, k:k + 1]), "ss")
                    ss_l = counts["ss"]

                    @blk.tensor
                    def _(t, k=k, b=b, ss_l=ss_l):
                        W(t, "ss", ss_l)
                        W(t, "sv", ev6[b])
                        last = None
                        for c in range(NCC):
                            last = t.transpose(
                                ptl[:, b * C + c * P:b * C + (c + 1) * P],
                                ln2bf[b][:, c * P:(c + 1) * P], idn[:, :])
                        tp6[b] = cinc(last, "st")

                    @blk.vector
                    def _(v, k=k, b=b):
                        W(v, "st", tp6[b])
                        v.drain()
                        ev6[b] = cinc(v.tensor_copy(
                            lnfa[:, :].rearrange("p (c t) -> p c t", t=TPC)
                                      [:, :, k * P:(k + 1) * P],
                            ptl[:, b * C:(b + 1) * C]
                                .rearrange("p (c w) -> p c w", w=P)), "sv")
            ln_done = counts["sv"]

            if True:
                SLAB6 = min(512, 6 * C)
                NSL = SLAB6 // P
                rksl = [[mctx.enter_context(
                    nc.sbuf_tensor(f"rksl{j}_{i}", [P, SLAB6], BF16))
                    for i in range(NCC)] for j in range(2)]
                khold = [mctx.enter_context(nc.sbuf_tensor(f"khold{i}", [P, TPC], BF16))
                         for i in range(NCC)]
                vtile = [mctx.enter_context(nc.sbuf_tensor(f"vtile{i}", [P, TPC], BF16))
                         for i in range(2)]
                iota_a = mctx.enter_context(nc.sbuf_tensor("iota_a", [P, TPC], F32))
                revd = mctx.enter_context(nc.sbuf_tensor("revd", [P, TPC], F32))
                et = mctx.enter_context(nc.sbuf_tensor("et", [P, TPC], F32))
                psum6 = [mctx.enter_context(nc.psum_tensor(f"psum6_{i}", [P, TPC], F32))
                         for i in range(2)]

                @blk.gpsimd
                def _(g):
                    cinc(g.iota(iota_a[:, :], [[1, TPC]], channel_multiplier=0,
                                allow_small_or_imprecise_dtypes=True), "sp")
                sp_iota = counts["sp"]

                NC3 = 3 * NCC
                order = []
                for c in range(NCC):
                    order.append(("r", 0, c, c))
                for c in range(NCC):
                    order.append(("r", 1, c, NC3 + c))
                for c in range(NCC):
                    order.append(("k", 0, c, NCC + c))
                    order.append(("v", 0, c, 2 * NCC + c))
                for c in range(NCC):
                    order.append(("k", 1, c, NC3 + NCC + c))
                    order.append(("v", 1, c, NC3 + 2 * NCC + c))

                mm6 = [0, 0]
                evac6 = [{"ss": 0, "sv": 0}, {"ss": 0, "sv": 0}]

                windows = []
                for idx, (kind, dr, c, d6) in enumerate(order):
                    s_ = d6 // NSL
                    if windows and windows[-1][1] == s_:
                        continue
                    windows.append((idx, s_))
                resident = {}
                wbuf, wload = [], []
                prev_b = 1
                for widx, (start, s_) in enumerate(windows):
                    if s_ in resident:
                        wbuf.append(resident[s_])
                        wload.append(False)
                    else:
                        b_ = 1 - prev_b
                        for sid in [k2 for k2, v2 in resident.items() if v2 == b_]:
                            del resident[sid]
                        resident[s_] = b_
                        wbuf.append(b_)
                        wload.append(True)
                    prev_b = wbuf[-1]
                for widx, (start, s_) in enumerate(windows):
                    if not wload[widx]:
                        assert any(windows[j][1] == s_ and wload[j]
                                   for j in range(widx))
                buf_last_st = [0, 0]
                lg_of_buf = [0, 0]

                def load_slab(widx):
                    s_, b_ = windows[widx][1], wbuf[widx]
                    bufs = rksl[b_]

                    @blk.gpsimd
                    def _(g, s_=s_, bufs=bufs, b_=b_):
                        W(g, "st", buf_last_st[b_])
                        if counts["lg"] > 0:
                            g.wait_ge(sems["lg"], counts["lg"])
                        for cc in range(NCC):
                            dma(g, "lg", bufs[cc][:, :],
                                rkvT_in[cc * P:(cc + 1) * P,
                                        s_ * SLAB6:(s_ + 1) * SLAB6],
                                serial=False)
                    lg_of_buf[b_] = counts["lg"]

                load_slab(0)
                cur_w = -1
                for idx, (kind, dr, c, d6) in enumerate(order):
                    slab_id = d6 // NSL
                    if cur_w + 1 < len(windows) and windows[cur_w + 1][0] == idx:
                        cur_w += 1
                        if cur_w + 1 < len(windows) and wload[cur_w + 1]:
                            load_slab(cur_w + 1)

                    pb = idx % 2
                    my_b = wbuf[cur_w]
                    lg_slab, my_slabs = lg_of_buf[my_b], rksl[my_b]
                    off = (d6 % NSL) * P

                    @blk.tensor
                    def _(t, pb=pb, off=off, my_slabs=my_slabs, lg_slab=lg_slab):
                        W(t, "sv", ln_done)
                        W(t, "lg", lg_slab)
                        W(t, "ss", evac6[pb]["ss"])
                        W(t, "sv", evac6[pb]["sv"])
                        last = None
                        for cc in range(NCC):
                            for tt in range(NTT):
                                last = t.matmul(psum6[pb][:, tt * TT:(tt + 1) * TT],
                                                my_slabs[cc][:, off:off + P],
                                                lnfa[:, cc * TPC + tt * TT:
                                                     cc * TPC + (tt + 1) * TT],
                                                start=(cc == 0), stop=(cc == NCC - 1))
                        mm6[pb] = cinc(last, "st")
                    buf_last_st[my_b] = counts["st"]

                    if kind == "r":
                        @blk.scalar
                        def _(s, pb=pb):
                            W(s, "st", mm6[pb])
                            W(s, "lk", counts["lk"])
                            W(s, "sv", counts["sv"])
                            s.drain()
                            evac6[pb]["ss"] = cinc(
                                s.activation(vtile[pb][:, :], psum6[pb][:, :],
                                             AF.Sigmoid), "ss")
                        ss_e = counts["ss"]

                        @blk.sync
                        def _(sy, d6=d6, pb=pb):
                            W(sy, "ss", ss_e)
                            dma(sy, "lk", KVR_d[d6 * P:(d6 + 1) * P, :], vtile[pb][:, :])

                    elif kind == "k":
                        @blk.vector
                        def _(v, c=c, pb=pb):
                            W(v, "st", mm6[pb])
                            W(v, "lk", counts["lk"])
                            v.drain()
                            evac6[pb]["sv"] = cinc(
                                v.tensor_copy(khold[c][:, :], psum6[pb][:, :]), "sv")
                        sv_e = counts["sv"]

                        @blk.sync
                        def _(sy, d6=d6, c=c):
                            W(sy, "sv", sv_e)
                            dma(sy, "lk", KVR_d[d6 * P:(d6 + 1) * P, :], khold[c][:, :])

                    else:
                        @blk.vector
                        def _(v, pb=pb):
                            W(v, "st", mm6[pb])
                            W(v, "lk", counts["lk"])
                            W(v, "ss", counts["ss"])
                            v.drain()
                            evac6[pb]["sv"] = cinc(
                                v.tensor_copy(vtile[pb][:, :], psum6[pb][:, :]), "sv")
                        sv_e = counts["sv"]

                        @blk.sync
                        def _(sy, d6=d6, pb=pb):
                            W(sy, "sv", sv_e)
                            dma(sy, "lk", KVR_d[d6 * P:(d6 + 1) * P, :], vtile[pb][:, :])

                        @blk.vector
                        def _(v, dr=dr, c=c):
                            W(v, "sp", sp_iota)
                            W(v, "ss", counts["ss"])
                            v.drain()
                            wc = wf_col if dr == 0 else wb_col
                            isrc = iota_a[:, ::-1] if dr == 0 else iota_a[:, :]
                            cinc(v.scalar_tensor_tensor(
                                et[:, :], isrc, wc[:, c:c + 1], khold[c][:, :],
                                ALU.mult, ALU.add), "sv")
                        sv_t1 = counts["sv"]

                        @blk.scalar
                        def _(s, dr=dr, c=c):
                            W(s, "sv", sv_t1)
                            s.drain()
                            cinc(s.activation(et[:, :], et[:, :], AF.Exp,
                                              accum_out=scol[:, (2 * dr + 1) * NCC + c:
                                                              (2 * dr + 1) * NCC + c + 1]),
                                 "ss")
                        ss_et = counts["ss"]

                        @blk.vector
                        def _(v, dr=dr, c=c, pb=pb):
                            W(v, "ss", ss_et)
                            v.drain()
                            cinc(v.scalar_tensor_tensor(
                                revd[:, :], et[:, :], 1.0, vtile[pb][:, :],
                                ALU.mult, ALU.mult,
                                accum_out=scol[:, (2 * dr) * NCC + c:
                                               (2 * dr) * NCC + c + 1]), "sv")

                lk_all = counts["lk"]
                sv_sum = counts["sv"]
                ss_sum = counts["ss"]
                m3_st = counts["st"]

        if True:
            ccn = CC_SEMS[cc_used[0]]
            cc_used[0] += 1

            @blk.gpsimd
            def _(g):
                W(g, "sv", sv_sum)
                W(g, "ss", ss_sum)
                dma(g, "lg", scol_i, scol[:, :])
                g.wait_ge(sems["lg"], counts["lg"])
                counts[ccn] += 1
                g.collective_compute(
                    "AllGather", ALU.bypass, replica_groups=RG,
                    ins=[scol_i], outs=[scol_o]).then_inc(sems[ccn])
                g.wait_ge(sems[ccn], counts[ccn])

            with ExitStack() as sctx:
                Gt = sctx.enter_context(nc.sbuf_tensor("Gt", [P, NCORES * NK], F32))
                coefa = sctx.enter_context(nc.sbuf_tensor("coefa", [P, NCORES * NK], F32))
                kbuf = sctx.enter_context(nc.sbuf_tensor("kbuf", [P, TPC], BF16))
                vbuf = sctx.enter_context(nc.sbuf_tensor("vbuf", [P, TPC], BF16))
                sgbuf = [sctx.enter_context(nc.sbuf_tensor(f"sgbuf{i}", [P, TPC], BF16))
                         for i in range(2)]
                ekt = [sctx.enter_context(nc.sbuf_tensor(f"ekt{i}", [P, TPC], F32))
                       for i in range(2)]
                d1t = [sctx.enter_context(nc.sbuf_tensor(f"d1t{i}", [P, TPC], F32))
                       for i in range(2)]
                Apad = [sctx.enter_context(nc.sbuf_tensor(f"Apad{i}", [P, TPC + 1], F32))
                        for i in range(2)]
                Bpad = [sctx.enter_context(nc.sbuf_tensor(f"Bpad{i}", [P, TPC + 1], F32))
                        for i in range(2)]
                obuf = sctx.enter_context(nc.sbuf_tensor("obuf", [P, NCC * TPC], BF16))
                owt = [sctx.enter_context(nc.sbuf_tensor(f"owt{i}", [P, C], BF16))
                       for i in range(NCC)]
                psum4 = [sctx.enter_context(nc.psum_tensor(f"psum4_{i}", [P, C], F32))
                         for i in range(2)]

                @blk.gpsimd
                def _(g):
                    W(g, "st", m3_st)
                    dma(g, "lg", Gt[:, :].rearrange("p (j k) -> p j k", k=NK),
                        scol_o.rearrange("(j p) k -> p j k", p=P))
                lg_g = counts["lg"]

                @blk.vector
                def _(v):
                    W(v, "lg", lg_g)
                    W(v, "ss", ss_sum)
                    v.drain()
                    for j in range(NCORES):
                        v.tensor_mul(coefa[:, j * NK:(j + 1) * NK], W32[:, :],
                                     pfx_t[:, j * NK:(j + 1) * NK])
                    cinc(v.drain(), "sv")
                sv_pc = counts["sv"]

                @blk.scalar
                def _(s):
                    W(s, "sv", sv_pc)
                    s.drain()
                    cinc(s.activation(coefa[:, :], coefa[:, :], AF.Exp), "ss")
                ss_pc = counts["ss"]

                @blk.vector
                def _(v):
                    W(v, "ss", ss_pc)
                    v.drain()
                    v.tensor_mul(coefa[:, :], coefa[:, :], Gt[:, :])
                    v.drain()
                    for j in range(1, NCORES):
                        v.tensor_add(coefa[:, 0:NK], coefa[:, 0:NK],
                                     coefa[:, j * NK:(j + 1) * NK])
                        v.drain()
                    cinc(v.tensor_copy(Ain[:, :], coefa[:, 0:NK]), "sv")
                sv_ain = counts["sv"]

                ek_ss = {}
                d1_sp = {}
                glob_sv = {}
                glob_ss = {}
                o_sv = {}
                kv_tk = {}
                sg_lk = {}
                Ash_v = {}
                Bsh_v = {}
                dirprev_st = {}
                pend = []
                NCH = 2 * NCC
                dirprev_box = [m3_st]

                def chunk_bases(g):
                    dr_, c_ = divmod(g, NCC)
                    return ((3 * dr_ + 0) * C + c_ * P,
                            (3 * dr_ + 1) * C + c_ * P,
                            (3 * dr_ + 2) * C + c_ * P)

                def emit_kvload(g):
                    if g >= NCH:
                        return
                    _, base_k, base_v = chunk_bases(g)

                    @blk.scalar
                    def _(s, g=g, base_k=base_k, base_v=base_v):
                        W(s, "lk", lk_all)
                        W(s, "ss", ek_ss.get(g - 1, 0))
                        W(s, "sp", d1_sp.get(g - 1, 0))
                        if counts["tk"] > 0:
                            s.wait_ge(sems["tk"], counts["tk"])
                        dma(s, "tk", kbuf[:, :], KVR_d[base_k:base_k + P, :],
                            serial=False)
                        dma(s, "tk", vbuf[:, :], KVR_d[base_v:base_v + P, :],
                            serial=False)
                    kv_tk[g] = counts["tk"]

                def emit_sgload(g):
                    if g >= NCH:
                        return
                    base_r, _, _ = chunk_bases(g)
                    w = g % 2

                    @blk.sync
                    def _(sy, g=g, w=w, base_r=base_r):
                        W(sy, "sv", o_sv.get(g - 2, 0))
                        dma(sy, "lk", sgbuf[w][:, :],
                            KVR_d[base_r:base_r + P, :])
                    sg_lk[g] = counts["lk"]

                def emit_ek(g):
                    w = g % 2

                    @blk.scalar
                    def _(s, g=g, w=w):
                        W(s, "tk", kv_tk[g])
                        W(s, "sv", glob_sv.get(g - 2, 0))
                        s.drain()
                        cinc(s.activation(ekt[w][:, :], kbuf[:, :], AF.Exp),
                             "ss")
                    ek_ss[g] = counts["ss"]

                def emit_d1(g):
                    w = g % 2

                    @blk.gpsimd
                    def _(gp, g=g, w=w):
                        W(gp, "ss", ek_ss[g])
                        W(gp, "sv", glob_sv.get(g - 2, 0))
                        gp.drain()
                        cinc(gp.tensor_mul(d1t[w][:, :], ekt[w][:, :],
                                           vbuf[:, :]), "sp")
                    d1_sp[g] = counts["sp"]

                def emit_block(g):
                    dr_, c_ = divmod(g, NCC)
                    w = g % 2
                    ew = ewf_col if dr_ == 0 else ewb_col
                    eu = euf_col if dr_ == 0 else eub_col

                    @blk.vector
                    def _(v, g=g, w=w, dr=dr_, c=c_, ew=ew, eu=eu):
                        W(v, "sp", d1_sp[g])
                        W(v, "sv", sv_ain)
                        v.drain()
                        kA = (2 * dr) * NCC + c
                        kB = (2 * dr + 1) * NCC + c
                        d0ap = ew[:, c:c + 1].broadcast_to([P, TPC])
                        if dr == 0:
                            v.tensor_copy(Apad[w][:, 0:1], Ain[:, kA:kA + 1])
                            v.tensor_copy(Bpad[w][:, 0:1], Ain[:, kB:kB + 1])
                            v.drain()
                            v.tensor_tensor_scan(Apad[w][:, 1:TPC + 1], d0ap,
                                                 d1t[w][:, :], Ain[:, kA:kA + 1],
                                                 ALU.mult, ALU.add)
                            v.drain()
                            v.tensor_tensor_scan(Bpad[w][:, 1:TPC + 1], d0ap,
                                                 ekt[w][:, :], Ain[:, kB:kB + 1],
                                                 ALU.mult, ALU.add)
                            Ash = Apad[w][:, 0:TPC]
                            Bsh = Bpad[w][:, 0:TPC]
                        else:
                            v.tensor_copy(Apad[w][:, TPC:TPC + 1],
                                          Ain[:, kA:kA + 1])
                            v.tensor_copy(Bpad[w][:, TPC:TPC + 1],
                                          Ain[:, kB:kB + 1])
                            v.drain()
                            v.tensor_tensor_scan(Apad[w][:, TPC - 1::-1], d0ap,
                                                 d1t[w][:, ::-1],
                                                 Ain[:, kA:kA + 1],
                                                 ALU.mult, ALU.add)
                            v.drain()
                            v.tensor_tensor_scan(Bpad[w][:, TPC - 1::-1], d0ap,
                                                 ekt[w][:, ::-1],
                                                 Ain[:, kB:kB + 1],
                                                 ALU.mult, ALU.add)
                            Ash = Apad[w][:, 1:TPC + 1]
                            Bsh = Bpad[w][:, 1:TPC + 1]
                        Ash_v[g] = Ash
                        Bsh_v[g] = Bsh
                        v.drain()
                        v.scalar_tensor_tensor(Ash, d1t[w][:, :],
                                               eu[:, c:c + 1], Ash,
                                               ALU.mult, ALU.add)
                        cinc(v.scalar_tensor_tensor(Bsh, ekt[w][:, :],
                                                    eu[:, c:c + 1], Bsh,
                                                    ALU.mult, ALU.add), "sv")
                    glob_sv[g] = counts["sv"]

                def emit_tail(g, w):
                    @blk.scalar
                    def _(s, g=g, w=w):
                        W(s, "sv", glob_sv[g])
                        s.drain()
                        s.activation(Bsh_v[g], Bsh_v[g], AF.Ln)
                        s.drain()
                        cinc(s.activation(Bsh_v[g], Bsh_v[g], AF.Exp,
                                          scale=-1.0), "ss")
                    glob_ss[g] = counts["ss"]

                    @blk.vector
                    def _(v, g=g, w=w):
                        W(v, "ss", glob_ss[g])
                        W(v, "lk", sg_lk[g])
                        W(v, "st", dirprev_st[g])
                        v.drain()
                        v.tensor_mul(Ash_v[g], Ash_v[g], Bsh_v[g])
                        v.drain()
                        cinc(v.tensor_mul(
                            obuf[:, (g % NCC) * TPC:(g % NCC + 1) * TPC],
                            Ash_v[g], sgbuf[w][:, :]), "sv")
                    o_sv[g] = counts["sv"]

                m4_lxo = [0, 0]
                for dr in range(2):
                    if dr == 0:
                        emit_kvload(0)
                        emit_sgload(0)
                        emit_sgload(1)

                    for c in range(NCC):
                        g_ = dr * NCC + c
                        dirprev_st[g_] = dirprev_box[0]
                        emit_ek(g_)
                        emit_d1(g_)
                        emit_kvload(g_ + 1)
                        emit_block(g_)
                        pend.append((g_, g_ % 2))
                        if len(pend) > 1:
                            emit_tail(*pend.pop(0))
                            emit_sgload(g_ + 1)
                    while pend:
                        gl, wl = pend.pop(0)
                        emit_tail(gl, wl)
                        emit_sgload(gl + 2)

                    @blk.gpsimd
                    def _(g, dr=dr):
                        W(g, "st", dirprev_box[0])
                        W(g, "ss", ss_sum)
                        W(g, "sv", sv_ain)
                        if counts["lg"] > 0:
                            g.wait_ge(sems["lg"], counts["lg"])
                        for cc in range(NCC):
                            r0 = dr * C + cc * P
                            dma(g, "lg", owt[cc][:, :], owT_in[r0:r0 + P, :],
                                serial=False)
                    lg_ow = counts["lg"]

                    m4_evac = [0, 0]
                    for k in range(NTC):
                        pb = k % 2

                        @blk.tensor
                        def _(t, dr=dr, k=k, pb=pb, lg_ow=lg_ow):
                            W(t, "sv", max(o_sv[dr * NCC + NCC - 1],
                                           m4_evac[pb]))
                            W(t, "lg", lg_ow)
                            last = None
                            w4 = min(TT, C)
                            for cc in range(NCC):
                                lhs = obuf[:, cc * TPC + k * P:
                                           cc * TPC + (k + 1) * P]
                                for h in range(C // w4):
                                    last = t.matmul(
                                        psum4[pb][:, h * w4:(h + 1) * w4],
                                        lhs, owt[cc][:, h * w4:(h + 1) * w4],
                                        start=(cc == 0), stop=(cc == NCC - 1))
                            cinc(last, "st")
                        st_m4 = counts["st"]

                        @blk.vector
                        def _(v, dr=dr, k=k, pb=pb, st_m4=st_m4):
                            W(v, "st", st_m4)
                            v.drain()
                            src_ = X[:, k * C:(k + 1) * C]
                            m4_evac[pb] = cinc(
                                v.tensor_add(src_, src_, psum4[pb][:, :]), "sv")
                            if dr == 1:
                                nh = max(1, C // 512)
                                step = C // nh
                                for h in range(nh):
                                    v.bn_stats(tmp_col2[:, 6 * h:6 * h + 6],
                                               src_[:, h * step:(h + 1) * step])
                                    v.drain()
                                v.bn_aggr(mvt[:, 2 * k:2 * k + 2],
                                          tmp_col2[:, 0:6 * nh])
                                v.drain()
                                v.tensor_reduce(rmax_t[:, k:k + 1], src_,
                                                AX.X, ALU.max)
                                cinc(v.drain(), "sv")
                    dirprev_box[0] = counts["st"]
        stats_box[0] = counts["sv"]

        barrier()
        return counts["sv"]

    if 'f' in phases:
        ffn_app(0)
    if 'a' in phases:
        attn()
    if phases == 'faf':
        ffn_app(1)

    sv_final = counts["sv"]

    if phases == 'faf':
        lxo_out = counts["lxo"]

        @blk.gpsimd
        def _(g):
            g.wait_ge(sems["lxo"], lxo_out)
    else:
        @blk.sync
        def _(sy):
            W(sy, "sv", sv_final)
            xr = out_d.rearrange("(k p) c -> p k c", p=P)
            g = max(1, NTC // 4)
            for i in range(0, NTC, g):
                j = min(NTC, i + g)
                dma(sy, "lx", xr[:, i:j, :],
                    X[:, i * C:j * C].rearrange("p (k c) -> p k c", c=C))
        lx_out = counts["lx"]

        @blk.gpsimd
        def _(g):
            g.wait_ge(sems["lx"], lx_out)

    return nc


def _prep_inputs(cfg, L0_attn_x, L0_attn_rkv_w, L0_attn_out_w, L0_attn_time_decay,
                 L0_attn_time_first, L0_attn_time_decay_rev, L0_attn_time_first_rev,
                 L0_ffn_w1, L0_ffn_b1, L0_ffn_w2, L0_ffn_b2):
    TPC, C, NCC = cfg.TPC, cfg.C, cfg.NCC
    NK = 4 * NCC
    B, T, _ = L0_attn_x.shape
    cpb = NCORES // B
    assert T == cpb * TPC

    xf = np.ascontiguousarray(np.asarray(L0_attn_x, np.float32).reshape(B * T, C))
    rkvT = np.ascontiguousarray(np.asarray(L0_attn_rkv_w, np.float32).T)
    owT = np.ascontiguousarray(np.asarray(L0_attn_out_w, np.float32).T)

    def colform(a, n):
        return np.ascontiguousarray(np.asarray(a, np.float32).reshape(n, P).T)

    tdec = colform(L0_attn_time_decay, NCC)
    tfir = colform(L0_attn_time_first, NCC)
    tdecr = colform(L0_attn_time_decay_rev, NCC)
    tfirr = colform(L0_attn_time_first_rev, NCC)
    b1c = colform(L0_ffn_b1, cfg.NFC)
    b2c = colform(L0_ffn_b2, NCC)
    w1 = np.ascontiguousarray(np.asarray(L0_ffn_w1, np.float32))
    w2 = np.ascontiguousarray(np.asarray(L0_ffn_w2, np.float32))
    w1T = np.ascontiguousarray(w1.T)
    w2T = np.ascontiguousarray(w2.T)
    F = 4 * C
    SH1 = F // NCORES
    SH2 = C // NCORES

    in_maps = []
    for m in range(NCORES):
        mb = m % cpb
        bix = m // cpb
        pfx = np.full((P, NCORES * NK), 1e35, np.float32)
        for j in range(NCORES):
            jb = j // cpb
            jmb = j % cpb
            if jb != bix:
                continue
            if jmb < mb:
                val = float(TPC * (mb - 1 - jmb))
                pfx[:, j * NK + 0 * NCC:j * NK + 2 * NCC] = val
            if jmb > mb:
                val = float(TPC * (jmb - mb - 1))
                pfx[:, j * NK + 2 * NCC:j * NK + 4 * NCC] = val
        in_maps.append(dict(
            x=np.ascontiguousarray(xf[m * TPC:(m + 1) * TPC]),
            w1s=np.ascontiguousarray(w1[m * SH1:(m + 1) * SH1, :]),
            w2s=np.ascontiguousarray(w2[m * SH2:(m + 1) * SH2, :]),
            w1Ts=np.ascontiguousarray(w1T[:, m * SH1:(m + 1) * SH1]),
            w2Ts=np.ascontiguousarray(w2T[:, m * SH2:(m + 1) * SH2]),
            b1=b1c, b2=b2c, rkvT=rkvT, owT=owT,
            tdec=tdec, tfir=tfir, tdecr=tdecr, tfirr=tfirr, pfx=pfx,
        ))
    return in_maps


def kernel(L0_attn_x, L0_attn_rkv_w, L0_attn_out_w, L0_attn_time_decay,
           L0_attn_time_first, L0_attn_time_decay_rev, L0_attn_time_first_rev,
           L0_ffn_w1, L0_ffn_b1, L0_ffn_w2, L0_ffn_b2):
    cfg = CFG_REAL
    in_maps = _prep_inputs(cfg, L0_attn_x, L0_attn_rkv_w, L0_attn_out_w,
                           L0_attn_time_decay, L0_attn_time_first,
                           L0_attn_time_decay_rev, L0_attn_time_first_rev,
                           L0_ffn_w1, L0_ffn_b1, L0_ffn_w2, L0_ffn_b2)
    nc = build_nc(cfg)
    res = run_bass_kernel_spmd(nc, in_maps, core_ids=list(range(NCORES)))
    B, T, C = L0_attn_x.shape
    out = np.concatenate([res.results[m]["out"] for m in range(NCORES)], axis=0)
    return out.reshape(B, T, C).astype(np.float32)


# version 31
# speedup vs baseline: 1.0198x; 1.0030x over previous
from collections import defaultdict
from contextlib import ExitStack

import numpy as np
import concourse.bass as bass
import concourse.mybir as mybir
from concourse.bass_utils import run_bass_kernel_spmd

F32 = mybir.dt.float32
BF16 = mybir.dt.bfloat16
AF = mybir.ActivationFunctionType
ALU = mybir.AluOpType
AX = mybir.AxisListType

MAGIC = float(np.float32(1.5 * 2 ** 23))
NCORES = 8
P = 128


class Cfg:
    def __init__(self, TPC, C):
        self.TPC = TPC
        self.C = C
        self.F = 4 * C
        self.NCC = C // P
        self.NFC = self.F // P
        self.ND6 = 6 * C // P
        self.NTC = TPC // P
        self.TT = min(512, TPC)
        self.NTT = TPC // self.TT


CFG_REAL = Cfg(2048, 1024)

DMA_STREAMS = ["lx", "lw", "lw2", "lw3", "lw4", "lq", "lq2", "lq3", "lq4",
               "lxq", "lxq2", "lh", "lh2", "lf", "lf2",
               "ll", "ll2", "lk", "tw2", "lxo",
               "tx", "tx2", "tw", "th", "tf", "tl", "tl2", "tk", "tk2",
               "lg"]
CC_SEMS = ["cc0", "cc1", "cc2", "cc3", "cc4", "cc5", "cc6", "cc7"]
ENG_SEMS = ["sv", "ss", "sp", "st"]


def build_nc(cfg, phases='faf', dbg=''):
    TPC, C, F = cfg.TPC, cfg.C, cfg.F
    NCC, NFC, NTC, TT, NTT = cfg.NCC, cfg.NFC, cfg.NTC, cfg.TT, cfg.NTT

    nc = bass.Bass("TRN2", target_bir_lowering=False)

    def ein(name, shape, dt=F32):
        return nc.dram_tensor(name, shape, dt, kind="ExternalInput").ap()

    SH1 = F // NCORES
    SH2 = C // NCORES
    x_in = ein("x", [TPC, C])
    w1s_in = ein("w1s", [SH1, C])
    w2s_in = ein("w2s", [SH2, F])
    w1Ts_in = ein("w1Ts", [C, SH1])
    w2Ts_in = ein("w2Ts", [F, SH2])
    b1_in = ein("b1", [P, NFC])
    b2_in = ein("b2", [P, NCC])
    rkvT_in = ein("rkvT", [C, 6 * C])
    owT_in = ein("owT", [2 * C, C])
    tdec_in = ein("tdec", [P, NCC])
    tfir_in = ein("tfir", [P, NCC])
    tdecr_in = ein("tdecr", [P, NCC])
    tfirr_in = ein("tfirr", [P, NCC])
    pfx_in = ein("pfx", [P, NCORES * 4 * NCC])

    out_d = nc.dram_tensor("out", [TPC, C], F32, kind="ExternalOutput").ap()

    w1q_i = nc.dram_tensor("w1q_i", [C, SH1], BF16).ap()
    w1q_o = nc.dram_tensor("w1q_o", [NCORES * C, SH1], BF16).ap()
    w2q_i = nc.dram_tensor("w2q_i", [F, SH2], BF16).ap()
    w2q_o = nc.dram_tensor("w2q_o", [NCORES * F, SH2], BF16).ap()
    N1T_g = max(1, SH1 // P)
    sc_i = nc.dram_tensor("sc_i", [P, N1T_g + 1], F32).ap()
    sc_o = nc.dram_tensor("sc_o", [NCORES * P, N1T_g + 1], F32).ap()
    XQ_d = nc.dram_tensor("XQ_d", [TPC, C], BF16).ap()
    if "h" in dbg:
        H_d = nc.dram_tensor("H_d", [F, TPC], BF16, kind="ExternalOutput").ap()
    else:
        H_d = nc.dram_tensor("H_d", [F, TPC], BF16).ap()
    XQ2_dbg = (nc.dram_tensor("XQ2_dbg", [TPC, C], BF16, kind="ExternalOutput").ap()
               if "q" in dbg else None)
    F1_d = nc.dram_tensor("F1_d", [C, TPC], BF16).ap()
    XO0_d = nc.dram_tensor("XO0_d", [TPC, C], F32).ap()
    XO1_d = nc.dram_tensor("XO1_d", [TPC, C], F32).ap()
    LN2_d = nc.dram_tensor("LN2_d", [TPC, C], BF16).ap()
    KVR_d = nc.dram_tensor("KVR_d", [6 * C, TPC], BF16).ap()
    gmax_i = nc.dram_tensor("gmax_i", [P, 1], F32).ap()
    gmax_o = nc.dram_tensor("gmax_o", [P * NCORES, 1], F32).ap()
    NK = 4 * NCC
    scol_i = nc.dram_tensor("scol_i", [P, NK], F32).ap()
    scol_o = nc.dram_tensor("scol_o", [P * NCORES, NK], F32).ap()

    RG = [list(range(NCORES))]

    counts = defaultdict(int)
    sems = {}
    ctx = ExitStack()
    nc._kb_ctx = ctx
    for sname in DMA_STREAMS + CC_SEMS + ENG_SEMS:
        sems[sname] = ctx.enter_context(nc.semaphore(sname))
    cc_used = [0]
    stats_box = [None]
    xo_info = {"lxo": 0}

    def dma(q, stream, out, in_, serial=True, **kw):
        if kw.get("transpose"):
            serial = True
        if serial and counts[stream] > 0:
            q.wait_ge(sems[stream], counts[stream])
        counts[stream] += 16
        q.dma_start(out=out, in_=in_, **kw).then_inc(sems[stream], 16)
        return counts[stream]

    def cinc(instr, name):
        counts[name] += 1
        instr.then_inc(sems[name], 1)
        return counts[name]

    def W(eng, name, val):
        if val > 0:
            eng.wait_ge(sems[name], val)

    sb = lambda name, shape, dt=F32: ctx.enter_context(nc.sbuf_tensor(name, shape, dt))

    X = sb("X", [P, NTC * C])
    ones_row = sb("ones_row", [1, P])
    magic_col = sb("magic_col", [P, 1])
    nmagic_col = sb("nmagic_col", [P, 1])
    eps_col = sb("eps_col", [P, 1])
    epsq_col = sb("epsq_col", [P, 1])
    grow = sb("grow", [1, P * NCORES])
    tcols = sb("tcols", [P, 4 * NCC])
    wf_col = sb("wf_col", [P, NCC])
    wb_col = sb("wb_col", [P, NCC])
    euf_col = sb("euf_col", [P, NCC])
    eub_col = sb("eub_col", [P, NCC])
    ewf_col = sb("ewf_col", [P, NCC])
    ewb_col = sb("ewb_col", [P, NCC])
    W32 = sb("W32", [P, NK])
    pfx_t = sb("pfx_t", [P, NCORES * NK])
    b1c = sb("b1c", [P, NFC])
    b2c = sb("b2c", [P, NCC])
    swcol1 = sb("swcol1", [P, NFC])
    swcol2 = sb("swcol2", [P, NCC])
    sprod1 = sb("sprod1", [P, NFC])
    bq1f = sb("bq1f", [P, NFC])
    sprod2h = sb("sprod2h", [P, NCC])
    bq2fh = sb("bq2fh", [P, NCC])
    scol = sb("scol", [P, NK])
    Ain = sb("Ain", [P, NK])
    mvt = sb("mvt", [P, 2 * NTC])
    rstd_t = sb("rstd_t", [P, NTC])
    rmax_t = sb("rmax_t", [P, NTC])
    gm_t = sb("gm_t", [P, NTC])
    gcol = sb("gcol", [P, 1])
    g1 = sb("g1", [1, 1])
    s_col = sb("s_col", [P, 1])
    invs_col = sb("invs_col", [P, 1])
    acol_t = sb("acol_t", [P, NTC])
    bcol_t = sb("bcol_t", [P, NTC])
    hmcols = sb("hmcols", [P, NFC])
    tmp_col = sb("tmp_col", [P, max(NFC, NK)])
    tmp_col2 = sb("tmp_col2", [P, max(NFC, NK, 12)])
    idtf = sb("idtf", [P, P])
    idn = sb("idn", [P, P], BF16)

    blk = ctx.enter_context(nc.Block())

    def barrier():
        marks = {}
        for name, section in (("sv", blk.vector), ("ss", blk.scalar),
                              ("sp", blk.gpsimd)):
            def mk(eng, name=name):
                cinc(eng.drain(), name)
            section(mk)
        for n in ENG_SEMS + DMA_STREAMS:
            marks[n] = counts[n]

        def wait_all(eng):
            for n, v in marks.items():
                W(eng, n, v)
        blk.vector(wait_all)
        blk.scalar(wait_all)
        blk.gpsimd(wait_all)
        blk.tensor(wait_all)
        blk.sync(wait_all)

    @blk.sync
    def _(sy):
        xr = x_in.rearrange("(k p) c -> p k c", p=P)
        g = max(1, NTC // 4)
        for i in range(0, NTC, g):
            j = min(NTC, i + g)
            dma(sy, "lx", X[:, i * C:j * C].rearrange("p (k c) -> p k c", c=C),
                xr[:, i:j, :])
        dma(sy, "lx", tcols[:, 0 * NCC:1 * NCC], tdec_in)
        dma(sy, "lx", tcols[:, 1 * NCC:2 * NCC], tfir_in)
        dma(sy, "lx", tcols[:, 2 * NCC:3 * NCC], tdecr_in)
        dma(sy, "lx", tcols[:, 3 * NCC:4 * NCC], tfirr_in)
        dma(sy, "lx", pfx_t[:, :], pfx_in)
        dma(sy, "lx", b1c[:, :], b1_in)
        dma(sy, "lx", b2c[:, :], b2_in)
    lx_p0 = counts["lx"]

    @blk.gpsimd
    def _(g):
        cinc(g.iota(idtf[:, :], [[1, P]], base=0, channel_multiplier=-1,
                    allow_small_or_imprecise_dtypes=True), "sp")
    sp_idn = counts["sp"]

    @blk.vector
    def _(v):
        v.memset(ones_row[:, :], 1.0)
        v.memset(magic_col[:, :], MAGIC)
        v.memset(nmagic_col[:, :], -MAGIC)
        v.memset(eps_col[:, :], 1e-5)
        v.memset(epsq_col[:, :], 1e-8)
        W(v, "sp", sp_idn)
        v.drain()
        v.tensor_scalar(idn[:, :], idtf[:, :], 0.0, None, ALU.is_equal)
        cinc(v.drain(), "sv")

    @blk.scalar
    def _(s):
        W(s, "lx", lx_p0)
        s.activation(wf_col[:, :], tcols[:, 0:NCC], AF.Exp)
        s.drain()
        s.activation(euf_col[:, :], tcols[:, NCC:2 * NCC], AF.Exp)
        s.drain()
        s.activation(wb_col[:, :], tcols[:, 2 * NCC:3 * NCC], AF.Exp)
        s.drain()
        cinc(s.activation(eub_col[:, :], tcols[:, 3 * NCC:4 * NCC], AF.Exp), "ss")
    ss_w0 = counts["ss"]

    @blk.vector
    def _(v):
        W(v, "ss", ss_w0)
        v.drain()
        v.tensor_scalar(wf_col[:, :], wf_col[:, :], -1.0, None, ALU.mult)
        v.drain()
        v.tensor_scalar(wb_col[:, :], wb_col[:, :], -1.0, None, ALU.mult)
        v.drain()
        v.tensor_copy(W32[:, 0 * NCC:1 * NCC], wf_col[:, :])
        v.drain()
        v.tensor_copy(W32[:, 1 * NCC:2 * NCC], wf_col[:, :])
        v.drain()
        v.tensor_copy(W32[:, 2 * NCC:3 * NCC], wb_col[:, :])
        v.drain()
        cinc(v.tensor_copy(W32[:, 3 * NCC:4 * NCC], wb_col[:, :]), "sv")
    sv_w = counts["sv"]

    @blk.scalar
    def _(s):
        W(s, "sv", sv_w)
        s.drain()
        s.activation(ewf_col[:, :], wf_col[:, :], AF.Exp)
        s.drain()
        cinc(s.activation(ewb_col[:, :], wb_col[:, :], AF.Exp), "ss")


    lq_prep_box = [(0, 0)]
    prep_tail_box = [None]
    lq2_prep_box = [(0, 0)]
    sc_lg_box = [0]
    sc_sv_box = [0]
    ccW1_box = [None]
    ccW2_box = [None]

    def emit_prep(mid_cb=None):
        N1T = max(1, SH1 // P)
        with ExitStack() as pctx:
            nb = [pctx.enter_context(nc.sbuf_tensor(f"nb{i}", [P, C], F32))
                  for i in range(8)]
            s2t = pctx.enter_context(nc.sbuf_tensor("s2t", [P, F // C], F32))
            wTb = [pctx.enter_context(nc.sbuf_tensor(f"wTb{i}", [P, SH1], F32))
                   for i in range(2)]
            qob = [pctx.enter_context(nc.sbuf_tensor(f"qob{i}", [P, SH1], BF16))
                   for i in range(2)]
            bct1 = pctx.enter_context(nc.sbuf_tensor("bct1", [P, SH1], F32))
            bct2 = pctx.enter_context(nc.sbuf_tensor("bct2", [P, SH2], F32))
            s1c = pctx.enter_context(nc.sbuf_tensor("s1c", [P, N1T], F32))
            inv1c = pctx.enter_context(nc.sbuf_tensor("inv1c", [P, N1T], F32))
            s2c = pctx.enter_context(nc.sbuf_tensor("s2c", [P, 1], F32))
            inv2c = pctx.enter_context(nc.sbuf_tensor("inv2c", [P, 1], F32))
            idt = pctx.enter_context(nc.sbuf_tensor("idt", [P, P], F32))
            invr = pctx.enter_context(nc.sbuf_tensor("invr", [1, SH1], F32))
            scg = pctx.enter_context(
                nc.sbuf_tensor("scg", [P, NCORES * (N1T + 1)], F32))

            @blk.gpsimd
            def _(g):
                cinc(g.iota(idt[:, :], [[1, P]], base=0, channel_multiplier=-1,
                            allow_small_or_imprecise_dtypes=True), "sp")
            sp_id = counts["sp"]

            @blk.vector
            def _(v):
                W(v, "sp", sp_id)
                v.drain()
                cinc(v.tensor_scalar(idt[:, :], idt[:, :], 0.0, None,
                                     ALU.is_equal), "sv")
            sv_id = counts["sv"]

            lwp = ["lw", "lw2", "lw3", "lw4"]
            NW2 = F // C
            lw_ab = []

            @blk.sync
            def _(sy):
                for i in range(N1T):
                    dma(sy, lwp[i % 4], nb[i % 4][:, :],
                        w1s_in[i * P:(i + 1) * P, :], serial=False)
                    lw_ab.append(counts[lwp[i % 4]])

            @blk.vector
            def _(v):
                for i in range(N1T):
                    W(v, lwp[i % 4], lw_ab[i])
                    v.drain()
                    v.tensor_reduce(s1c[:, i:i + 1], nb[i % 4][:, :],
                                    AX.X, ALU.max, apply_absolute_value=True)
                v.drain()
                v.tensor_scalar(s1c[:, :], s1c[:, :], 1e-8, 1.0 / 127.0,
                                ALU.max, ALU.mult)
                v.drain()
                cinc(v.reciprocal(inv1c[:, :], s1c[:, :]), "sv")
            sv_ab1 = counts["sv"]
            lw2_ab = []

            @blk.sync
            def _(sy):
                for j in range(NW2):
                    dma(sy, lwp[j % 4], nb[4 + j % 4][0:SH2, :],
                        w2s_in[:, j * C:(j + 1) * C], serial=False)
                    lw2_ab.append(counts[lwp[j % 4]])

            @blk.vector
            def _(v):
                for j in range(NW2):
                    W(v, lwp[j % 4], lw2_ab[j])
                    v.drain()
                    v.tensor_reduce(s2t[0:SH2, j:j + 1], nb[4 + j % 4][0:SH2, :],
                                    AX.X, ALU.max, apply_absolute_value=True)
                v.drain()
                v.tensor_reduce(s2c[0:SH2, 0:1], s2t[0:SH2, 0:NW2], AX.X,
                                ALU.max)
                v.drain()
                v.tensor_scalar(s2c[0:SH2, 0:1], s2c[0:SH2, 0:1], 1e-8,
                                1.0 / 127.0, ALU.max, ALU.mult)
                v.drain()
                cinc(v.reciprocal(inv2c[0:SH2, 0:1], s2c[0:SH2, 0:1]), "sv")
            sv_ab2 = counts["sv"]

            HW = SH1
            p_row = pctx.enter_context(nc.psum_tensor("p_row", [1, HW], F32))
            p_bct = pctx.enter_context(nc.psum_tensor("p_bct", [P, 512], F32))
            sct = pctx.enter_context(nc.sbuf_tensor("sct", [P, N1T + 1], F32))

            @blk.vector
            def _(v):
                W(v, "sv", max(sv_ab1, sv_ab2))
                v.drain()
                if SH2 < P:
                    v.memset(sct[:, :], 0.0)
                    v.drain()
                v.tensor_copy(sct[:, 0:N1T], s1c[:, :])
                v.drain()
                cinc(v.tensor_copy(sct[0:SH2, N1T:N1T + 1], s2c[0:SH2, 0:1]),
                     "sv")
            sv_sr = counts["sv"]

            ccS = CC_SEMS[cc_used[0]]
            cc_used[0] += 1

            @blk.gpsimd
            def _(g):
                W(g, "sv", sv_sr)
                dma(g, "lg", sc_i, sct[:, :])
                g.wait_ge(sems["lg"], counts["lg"])
                counts[ccS] += 1
                g.collective_compute(
                    "AllGather", ALU.bypass, replica_groups=RG,
                    ins=[sc_i], outs=[sc_o]).then_inc(sems[ccS])
                g.wait_ge(sems[ccS], counts[ccS])

            scpool = ["lx", "lk", "lxo", "tw2", "lh", "lh2"]
            sc_tot = {}

            @blk.sync
            def _(sy):
                W(sy, ccS, 1)
                for m in range(NCORES):
                    st_ = scpool[m % len(scpool)]
                    dma(sy, st_, scg[:, m * (N1T + 1):(m + 1) * (N1T + 1)],
                        sc_o[m * P:(m + 1) * P, :], serial=False)
            for st_ in scpool:
                sc_tot[st_] = counts[st_]

            @blk.vector
            def _(v):
                for st_ in scpool:
                    W(v, st_, sc_tot[st_])
                v.drain()
                for m in range(NCORES):
                    v.tensor_copy(swcol1[:, m * N1T:(m + 1) * N1T],
                                  scg[:, m * (N1T + 1):m * (N1T + 1) + N1T])
                    v.drain()
                    if SH2 >= P:
                        v.tensor_copy(swcol2[:, m:m + 1],
                                      scg[:, m * (N1T + 1) + N1T:
                                          (m + 1) * (N1T + 1)])
                        v.drain()
                cinc(v.nop(), "sv")
            sc_sv_box[0] = counts["sv"]

            if SH2 < P:
                @blk.gpsimd
                def _(g):
                    W(g, "sv", sc_sv_box[0])
                    for m in range(NCORES):
                        cc0_ = (m * SH2) // P
                        pa = (m * SH2) % P
                        dma(g, "lg", swcol2[pa:pa + SH2, cc0_:cc0_ + 1],
                            scg[0:SH2, m * (N1T + 1) + N1T:
                                (m + 1) * (N1T + 1)])
            sc_lg_box[0] = counts["lg"]

            if mid_cb is not None:
                mid_cb()

            def rows_and_bcast(invc, ncol, wid, bct, sv_gate):
                @blk.tensor
                def _(t):
                    W(t, "sv", max(sv_gate, sv_id))
                    last = None
                    for i in range(ncol):
                        kk = P if wid // ncol == P else wid // ncol
                        last = t.matmul(p_row[0:1, i * kk:(i + 1) * kk],
                                        invc[0:kk, i:i + 1], idt[0:kk, 0:kk],
                                        start=True, stop=True)
                    cinc(last, "st")
                st_r = counts["st"]

                @blk.vector
                def _(v):
                    W(v, "st", st_r)
                    v.drain()
                    cinc(v.tensor_copy(invr[0:1, 0:wid], p_row[0:1, 0:wid]),
                         "sv")
                sv_r = counts["sv"]

                ws = min(512, wid)
                for sl in range(wid // ws):
                    @blk.tensor
                    def _(t, sl=sl, ws=ws):
                        W(t, "sv", counts["sv"])
                        cinc(t.matmul(p_bct[:, 0:ws], ones_row[:, :],
                                      invr[0:1, sl * ws:(sl + 1) * ws],
                                      start=True, stop=True), "st")
                    st_b = counts["st"]

                    @blk.vector
                    def _(v, sl=sl, ws=ws, bct=bct):
                        W(v, "st", st_b)
                        v.drain()
                        cinc(v.tensor_copy(bct[:, sl * ws:(sl + 1) * ws],
                                           p_bct[:, 0:ws]), "sv")
                return counts["sv"]

            def quantT(nch, width, srcT, dst, bct, qs, qs2, lsa, lsb, sv_bct):
                quant_done = [0, 0]
                spill_done = [0, 0]
                lq_t0 = {n: counts[n] for n in ("lq", "lq2", "lq3", "lq4")}
                for ci in range(nch):
                    b = ci % 2
                    ls = lsa if b == 0 else lsb
                    qsb = qs if b == 0 else qs2

                    @blk.sync
                    def _(sy, ci=ci, b=b, ls=ls):
                        W(sy, "sv", quant_done[b])
                        dma(sy, ls, wTb[b][:, 0:width],
                            srcT[ci * P:(ci + 1) * P, :], serial=False)
                    lw_m = counts[ls]

                    @blk.vector
                    def _(v, ci=ci, b=b, ls=ls, lw_m=lw_m, sv_bct=sv_bct,
                          qsb=qsb, width=width, bct=bct):
                        W(v, ls, lw_m)
                        W(v, "sv", sv_bct)
                        for n in ("lq", "lq2", "lq3", "lq4"):
                            W(v, n, lq_t0[n])
                        W(v, qsb, spill_done[b])
                        v.drain()
                        v.tensor_mul(wTb[b][:, 0:width], wTb[b][:, 0:width],
                                     bct[:, 0:width])
                        v.drain()
                        v.tensor_scalar(wTb[b][:, 0:width], wTb[b][:, 0:width],
                                        MAGIC, MAGIC, ALU.add, ALU.subtract)
                        v.drain()
                        cinc(v.tensor_scalar(qob[b][:, 0:width],
                                             wTb[b][:, 0:width],
                                             127.0, -128.0, ALU.min, ALU.max),
                             "sv")
                    quant_done[b] = counts["sv"]
                    sv_qq = counts["sv"]

                    @blk.sync
                    def _(sy, ci=ci, b=b, sv_qq=sv_qq, qsb=qsb, width=width):
                        W(sy, "sv", sv_qq)
                        dma(sy, qsb, dst[ci * P:(ci + 1) * P, :],
                            qob[b][:, 0:width], serial=False)
                    spill_done[b] = counts[qsb]

            sv_b1c = rows_and_bcast(inv1c, N1T, SH1, bct1,
                                    max(sv_ab1, sv_sr))
            quantT(NCC, SH1, w1Ts_in, w1q_i, bct1, "lq", "lq3", "ll", "ll2",
                   sv_b1c)
            lq_prep_box[0] = (counts["lq"], counts["lq3"])

            ccW1 = CC_SEMS[cc_used[0]]
            cc_used[0] += 1
            ccW1_box[0] = ccW1

            @blk.gpsimd
            def _(g):
                W(g, "lq", counts["lq"])
                W(g, "lq3", counts["lq3"])
                counts[ccW1] += 1
                g.collective_compute(
                    "AllGather", ALU.bypass, replica_groups=RG,
                    ins=[w1q_i], outs=[w1q_o]).then_inc(sems[ccW1])

            sv_b2c = rows_and_bcast(inv2c, 1, SH2, bct2, sv_ab2)
            quantT(NFC, SH2, w2Ts_in, w2q_i, bct2, "lq2", "lq4", "lf", "lf2",
                   sv_b2c)
            lq2_prep_box[0] = (counts["lq2"], counts["lq4"])

            ccW2 = CC_SEMS[cc_used[0]]
            cc_used[0] += 1
            ccW2_box[0] = ccW2
            lq2v, lq4v = counts["lq2"], counts["lq4"]

            def emit_w2_ag():
                @blk.gpsimd
                def _(g):
                    W(g, "lq2", lq2v)
                    W(g, "lq4", lq4v)
                    counts[ccW2] += 1
                    g.collective_compute(
                        "AllGather", ALU.bypass, replica_groups=RG,
                        ins=[w2q_i], outs=[w2q_o]).then_inc(sems[ccW2])
            prep_tail_box[0] = emit_w2_ag

    def ln_stats(want_max, xo=None, stats_done=None):
        nh = max(1, C // 512)
        step = C // nh
        if stats_done is not None:
            pass
        elif xo is None:
            @blk.vector
            def _(v):
                W(v, "lx", lx_p0)
                v.drain()
                for k in range(NTC):
                    src = X[:, k * C:(k + 1) * C]
                    for h in range(nh):
                        v.bn_stats(tmp_col2[:, 6 * h:6 * h + 6],
                                   src[:, h * step:(h + 1) * step])
                    v.drain()
                    v.bn_aggr(mvt[:, 2 * k:2 * k + 2], tmp_col2[:, 0:6 * nh])
                    v.drain()
                    if want_max:
                        v.tensor_reduce(rmax_t[:, k:k + 1], src, AX.X, ALU.max)
                        v.drain()
                cinc(v.nop(), "sv")
        else:
            XO0, XO1, lxo_tot = xo
            with ExitStack() as xctx:
                xot = [[xctx.enter_context(
                    nc.sbuf_tensor(f"xot{j}_{i}", [P, C], F32))
                    for i in range(2)] for j in range(2)]
                add_sv = [0, 0]
                tile_done = [0, 0]
                for k in range(NTC):
                    pb = k % 2

                    @blk.sync
                    def _(sy, k=k, pb=pb):
                        W(sy, "sv", add_sv[pb])
                        if counts["lxo"] > 0:
                            sy.wait_ge(sems["lxo"], counts["lxo"])
                        dma(sy, "lxo", xot[0][pb][:, :],
                            XO0[k * P:(k + 1) * P, :], serial=False)
                        dma(sy, "lxo", xot[1][pb][:, :],
                            XO1[k * P:(k + 1) * P, :], serial=False)
                    tile_done[pb] = counts["lxo"]

                    @blk.vector
                    def _(v, k=k, pb=pb):
                        W(v, "lxo", tile_done[pb])
                        v.drain()
                        src = X[:, k * C:(k + 1) * C]
                        v.tensor_add(src, src, xot[0][pb][:, :])
                        v.drain()
                        add_sv[pb] = cinc(
                            v.tensor_add(src, src, xot[1][pb][:, :]), "sv")
                        for h in range(nh):
                            v.bn_stats(tmp_col2[:, 6 * h:6 * h + 6],
                                       src[:, h * step:(h + 1) * step])
                            v.drain()
                        v.bn_aggr(mvt[:, 2 * k:2 * k + 2], tmp_col2[:, 0:6 * nh])
                        v.drain()
                        if want_max:
                            v.tensor_reduce(rmax_t[:, k:k + 1], src, AX.X, ALU.max)
                            v.drain()
                        cinc(v.nop(), "sv")
        sv_stats = counts["sv"] if stats_done is None else stats_done

        @blk.scalar
        def _(s):
            W(s, "sv", sv_stats)
            s.drain()
            cinc(s.activation(rmax_t[:, :] if not want_max else gm_t[:, :],
                              mvt[:, 1::2], AF.Sqrt, bias=eps_col[:, :]), "ss")
        ss_sd = counts["ss"]

        @blk.vector
        def _(v):
            W(v, "ss", ss_sd)
            v.drain()
            v.reciprocal(rstd_t[:, :], gm_t[:, :] if want_max else rmax_t[:, :])
            v.drain()
            if want_max:
                v.tensor_sub(gm_t[:, :], rmax_t[:, :], mvt[:, 0::2])
                v.drain()
                v.tensor_mul(gm_t[:, :], gm_t[:, :], rstd_t[:, :])
                v.drain()
                v.tensor_reduce(gcol[:, :], gm_t[:, :], AX.X, ALU.max)
                v.drain()
            cinc(v.nop(), "sv")
        return counts["sv"]

    def gmax_collective(sv_ready):
        ccn = CC_SEMS[cc_used[0]]
        cc_used[0] += 1

        @blk.gpsimd
        def _(g):
            W(g, "sv", sv_ready)
            dma(g, "lg", gmax_i, gcol[:, :])
            g.wait_ge(sems["lg"], counts["lg"])
            counts[ccn] += 1
            g.collective_compute(
                "AllGather", ALU.bypass, replica_groups=RG,
                ins=[gmax_i], outs=[gmax_o]).then_inc(sems[ccn])
            g.wait_ge(sems[ccn], counts[ccn])
            dma(g, "lg", grow[:, :], gmax_o.rearrange("(a b) c -> b (a c)", b=1))
        lg_row = counts["lg"]

        @blk.vector
        def _(v):
            W(v, "lg", lg_row)
            v.drain()
            cinc(v.reduce_max(g1[:, :], grow[:, :], AX.X), "sv")
        sv_g1 = counts["sv"]

        with nc.psum_tensor(f"p_bc{cc_used[0]}", [P, 1], F32) as p_bc:
            @blk.tensor
            def _(t):
                W(t, "sv", sv_g1)
                cinc(t.matmul(p_bc[:, :], ones_row[:, :], g1[:, :],
                              start=True, stop=True), "st")
            st_bc = counts["st"]

            @blk.vector
            def _(v):
                W(v, "st", st_bc)
                v.drain()
                v.tensor_scalar(s_col[:, :], p_bc[:, :], epsq_col[:, :], 1.0 / 255.0,
                                ALU.max, ALU.mult)
                v.drain()
                cinc(v.reciprocal(invs_col[:, :], s_col[:, :]), "sv")
        return counts["sv"]

    def bias_quant(sv_scales, sw, bc, n, sp_out, bq_out, halve):
        @blk.vector
        def _(v):
            W(v, "sv", max(sv_scales, sc_sv_box[0]))
            W(v, "lg", sc_lg_box[0])
            v.drain()
            v.tensor_scalar(sp_out[:, 0:n], sw[:, 0:n], s_col[:, :], None, ALU.mult)
            v.drain()
            v.reciprocal(tmp_col[:, 0:n], sp_out[:, 0:n])
            v.drain()
            v.tensor_mul(tmp_col2[:, 0:n], bc[:, 0:n], tmp_col[:, 0:n])
            v.drain()
            v.tensor_scalar(tmp_col2[:, 0:n], tmp_col2[:, 0:n], MAGIC, MAGIC,
                            ALU.add, ALU.subtract)
            v.drain()
            v.tensor_mul(bq_out[:, 0:n], tmp_col2[:, 0:n], sp_out[:, 0:n])
            v.drain()
            if halve:
                v.tensor_scalar(bq_out[:, 0:n], bq_out[:, 0:n], 0.5, None, ALU.mult)
                v.drain()
                v.tensor_scalar(sp_out[:, 0:n], sp_out[:, 0:n], 0.5, None, ALU.mult)
                v.drain()
            cinc(v.nop(), "sv")
        return counts["sv"]

    def ffn_app(app):
        xo = ((XO0_d, XO1_d, xo_info["lxo"])
              if app == 1 and xo_info["lxo"] > 0 else None)
        sfx = f"_a{app}"
        ph_st, ph_sv, ph_ss, ph_sp = (counts["st"], counts["sv"],
                                      counts["ss"], counts["sp"])
        sv0 = ln_stats(want_max=True, xo=xo,
                       stats_done=stats_box[0] if app == 1 else None)
        sv1 = gmax_collective(sv0)

        fctx = ExitStack()
        qt = [fctx.enter_context(nc.sbuf_tensor(f"qt_f{i}{sfx}", [P, C], F32))
              for i in range(2)]
        qbf = [fctx.enter_context(nc.sbuf_tensor(f"qbf_f{i}{sfx}", [P, C], BF16))
               for i in range(2)]
        xqa = fctx.enter_context(
            nc.sbuf_tensor(f"xqa{sfx}", [P, NCC * TPC], BF16))
        xq_box = [0, 0]

        def emit_xq():
            @blk.vector
            def _(v):
                W(v, "sv", sv1)
                v.drain()
                v.tensor_scalar(acol_t[:, :], rstd_t[:, :], invs_col[:, :],
                                None, ALU.mult)
                v.drain()
                v.tensor_mul(bcol_t[:, :], mvt[:, 0::2], acol_t[:, :])
                v.drain()
                cinc(v.tensor_scalar(bcol_t[:, :], bcol_t[:, :], -1.0, MAGIC,
                                     ALU.mult, ALU.add), "sv")
            sv_a = counts["sv"]

            tp_st = [0, 0]
            ev_ss = [0, 0]
            mn_sv = [0, 0]
            rl_ss = [0, 0]
            with nc.psum_tensor(f"ptx{sfx}", [P, 2 * C], BF16) as ptx:
                for k in range(NTC):
                    b = k % 2

                    @blk.scalar
                    def _(s, k=k, b=b):
                        W(s, "sv", max(sv_a, mn_sv[b]))
                        W(s, "ss", ph_ss)
                        s.drain()
                        s.activation(qt[b][:, :], X[:, k * C:(k + 1) * C],
                                     AF.Relu, bias=bcol_t[:, k:k + 1],
                                     scale=acol_t[:, k:k + 1])
                        s.drain()
                        rl_ss[b] = cinc(s.activation(qt[b][:, :], qt[b][:, :],
                                                     AF.Relu,
                                                     bias=nmagic_col[:, :]),
                                        "ss")

                    @blk.vector
                    def _(v, k=k, b=b):
                        W(v, "ss", rl_ss[b])
                        W(v, "st", max(ph_st, tp_st[b]))
                        W(v, "sp", ph_sp)
                        v.drain()
                        mn_sv[b] = cinc(v.tensor_scalar(qbf[b][:, :],
                                                        qt[b][:, :], 255.0,
                                                        None, ALU.min), "sv")

                    @blk.tensor
                    def _(t, k=k, b=b):
                        W(t, "sv", max(mn_sv[b], ev_ss[b]))
                        last = None
                        for c in range(NCC):
                            last = t.transpose(
                                ptx[:, b * C + c * P:b * C + (c + 1) * P],
                                qbf[b][:, c * P:(c + 1) * P], idn[:, :])
                        tp_st[b] = cinc(last, "st")

                    @blk.vector
                    def _(v, k=k, b=b):
                        W(v, "st", tp_st[b])
                        v.drain()
                        ev_ss[b] = cinc(v.tensor_copy(
                            xqa[:, :].rearrange("p (c t) -> p c t", t=TPC)
                                     [:, :, k * P:(k + 1) * P],
                            ptx[:, b * C:(b + 1) * C]
                                .rearrange("p (c w) -> p c w", w=P)), "sv")
            xq_box[0] = counts["sv"]
            xq_box[1] = bias_quant(sv1, swcol1, b1c, NFC, sprod1, bq1f,
                                   halve=False)

        if app == 0:
            emit_prep(mid_cb=emit_xq)
        else:
            emit_xq()
        xq_done = xq_box[0]
        sv_b1 = xq_box[1]

        if True:
            w2slab = [fctx.enter_context(nc.sbuf_tensor(f"w2slab{i}{sfx}", [P, C], BF16))
                      for i in range(NFC)]

            tw2_hold = [0]

            with ExitStack() as m1ctx:
                SLAB = min(256, F)
                NSLAB = SLAB // P
                w1slab2 = [m1ctx.enter_context(
                    nc.sbuf_tensor(f"w1slab{j}{sfx}", [P, NCC * SLAB], BF16))
                    for j in range(2)]
                wv1 = [w1slab2[j][:, :].rearrange("p (c w) -> p c w", w=SLAB)
                       for j in range(2)]
                hbf = [m1ctx.enter_context(nc.sbuf_tensor(f"hbf{i}{sfx}", [P, TPC], BF16))
                       for i in range(2)]

                @blk.sync
                def _(sy):
                    W(sy, ccW2_box[0], 1)
                    W(sy, "st", ph_st)
                    W(sy, "sv", ph_sv)
                    W(sy, "ss", ph_ss)
                    W(sy, "sp", ph_sp)
                    if counts["tw2"] > 0:
                        sy.wait_ge(sems["tw2"], counts["tw2"])
                    for fc in range(NFC):
                        for m in range(NCORES):
                            dma(sy, "tw2",
                                w2slab[fc][:, m * SH2:(m + 1) * SH2],
                                w2q_o[m * F + fc * P:m * F + (fc + 1) * P, :],
                                serial=False)
                tw2_hold[0] = counts["tw2"]

                with ExitStack() as mctx:
                    psumH = [mctx.enter_context(
                        nc.psum_tensor(f"psumH{i}{sfx}", [P, TPC], F32))
                        for i in range(2)]
                    tw_slab = [0, 0]
                    st_slab_done = [0, 0]
                    evac_ss = [0, 0]
                    st_mm = [0, 0]
                    hspill = [("lh", 0), ("lh2", 0)]

                    NSLABS = NFC // NSLAB

                    def slab_load(sg):
                        sbf = sg % 2

                        @blk.scalar
                        def _(s, sg=sg, sbf=sbf):
                            W(s, "st", st_slab_done[sbf])
                            W(s, ccW1_box[0], 1)
                            if sg == 0:
                                W(s, "sv", ph_sv)
                                W(s, "sp", ph_sp)
                            if counts["tw"] > 0:
                                s.wait_ge(sems["tw"], counts["tw"])
                            fa = sg * SLAB
                            fb = fa + SLAB
                            for c in range(NCC):
                                for m in range(fa // SH1, (fb - 1) // SH1 + 1):
                                    la = max(fa, m * SH1) - m * SH1
                                    lb = min(fb, (m + 1) * SH1) - m * SH1
                                    off = m * SH1 + la - fa
                                    dma(s, "tw",
                                        w1slab2[sbf][:, c * SLAB + off:
                                                     c * SLAB + off + lb - la],
                                        w1q_o[m * C + c * P:m * C + (c + 1) * P,
                                              la:lb],
                                        serial=False)
                        tw_slab[sbf] = counts["tw"]

                    slab_load(0)
                    for fc in range(NFC):
                        sg = fc // NSLAB
                        sbf = sg % 2
                        if fc % NSLAB == 0 and sg + 1 < NSLABS:
                            slab_load(sg + 1)

                        b = fc % 2
                        hb = fc % 2

                        @blk.tensor
                        def _(t, fc=fc, b=b, sbf=sbf):
                            W(t, "tw", tw_slab[sbf])
                            W(t, "sv", max(xq_done, sv_b1))
                            W(t, "ss", evac_ss[b])
                            off = (fc % NSLAB) * P
                            last = None
                            for c in range(NCC):
                                for tt in range(NTT):
                                    last = t.matmul(
                                        psumH[b][:, tt * TT:(tt + 1) * TT],
                                        w1slab2[sbf][:, c * SLAB + off:
                                                     c * SLAB + off + P],
                                        xqa[:, c * TPC + tt * TT:
                                            c * TPC + (tt + 1) * TT],
                                        start=(c == 0), stop=(c == NCC - 1))
                            st_mm[b] = cinc(last, "st")
                        if fc % NSLAB == NSLAB - 1:
                            st_slab_done[sbf] = counts["st"]

                        @blk.scalar
                        def _(s, fc=fc, b=b, hb=hb):
                            W(s, "st", st_mm[b])
                            W(s, hspill[hb][0], hspill[hb][1])
                            s.drain()
                            evac_ss[b] = cinc(
                                s.activation(hbf[hb][:, :], psumH[b][:, :], AF.Relu,
                                             bias=bq1f[:, fc:fc + 1],
                                             scale=sprod1[:, fc:fc + 1]), "ss")

                        @blk.vector
                        def _(v, fc=fc, b=b, hb=hb):
                            W(v, "ss", evac_ss[b])
                            v.drain()
                            cinc(v.tensor_reduce(hmcols[:, fc:fc + 1], hbf[hb][:, :],
                                                 AX.X, ALU.max), "sv")
                        sv_hm = counts["sv"]

                        hs = "lh" if fc % 2 == 0 else "lh2"

                        @blk.gpsimd
                        def _(g, fc=fc, hb=hb, hs=hs):
                            W(g, "sv", sv_hm)
                            dma(g, hs, H_d[fc * P:(fc + 1) * P, :], hbf[hb][:, :])
                        hspill[hb] = (hs, counts[hs])
                        if fc == 6 and prep_tail_box[0] is not None:
                            prep_tail_box[0]()
                            prep_tail_box[0] = None

                    @blk.vector
                    def _(v):
                        v.drain()
                        cinc(v.tensor_reduce(gcol[:, :], hmcols[:, :], AX.X, ALU.max),
                             "sv")
                sv_hmax = counts["sv"]
                m1_st = counts["st"]
                m1_ss = counts["ss"]
                m1_lh = counts["lh"]
                m1_lh2 = counts["lh2"]

            sv2 = gmax_collective(sv_hmax)
            sv_b2 = bias_quant(sv2, swcol2, b2c, NCC, sprod2h, bq2fh, halve=True)

            with ExitStack() as m2ctx:
                hin = [xqa]
                hq32 = m2ctx.enter_context(
                    nc.sbuf_tensor(f"hq32{sfx}", [P, 2 * TT], F32))
                f1bf = [m2ctx.enter_context(
                    nc.sbuf_tensor(f"f1bf{i}{sfx}", [P, TT], BF16))
                    for i in range(NCC)]
                f1tm = [m2ctx.enter_context(
                    nc.sbuf_tensor(f"f1tm{i}{sfx}", [P, C], BF16)) for i in range(2)]

                with ExitStack() as mctx:
                    psumC = [mctx.enter_context(
                        nc.psum_tensor(f"psumC{i}{sfx}", [P, TT], F32))
                        for i in range(NCC)]
                    NHALF = min(2, NFC)
                    NH = NFC // NHALF
                    half_free_st = [[0, 0]] + [[0, 0] for _ in range(NTT)]
                    evac_sv2 = [0] * NCC
                    spill_done = [("lf" if c % 2 == 0 else "lf2", 0)
                                  for c in range(NCC)]
                    lf_tile = []
                    hb = 0
                    st_half = [[0, 0] for _ in range(NTT)]
                    sv_hq = [[0, 0] for _ in range(NTT)]

                    def emit_load(tt, half):
                        f0, f1_ = half * NH, (half + 1) * NH
                        th_prev = counts["th"]

                        @blk.scalar
                        def _(s, tt=tt, half=half, f0=f0, f1_=f1_,
                              th_prev=th_prev):
                            W(s, "st", max(half_free_st[tt][half], m1_st))
                            W(s, "ss", m1_ss)
                            W(s, "lh", m1_lh)
                            W(s, "lh2", m1_lh2)
                            W(s, "th", th_prev)
                            nsp = max(1, NH // 8)
                            step = NH // nsp
                            for i in range(nsp):
                                a = f0 + i * step
                                b2_ = f0 + (i + 1) * step
                                dma(s, "th",
                                    hin[hb][:, a * TT:b2_ * TT]
                                        .rearrange("p (f t) -> p f t", t=TT),
                                    H_d[a * P:b2_ * P, tt * TT:(tt + 1) * TT]
                                        .rearrange("(f p) t -> p f t", p=P),
                                    serial=False)
                        return counts["th"]

                    def emit_quant(tt, half, th_half):
                        f0, f1_ = half * NH, (half + 1) * NH
                        GQ = 2
                        for fg in range(f0, f1_, GQ):
                            gn = min(GQ, f1_ - fg)

                            @blk.vector
                            def _(v, fg=fg, gn=gn, th_half=th_half):
                                W(v, "th", th_half)
                                W(v, "sv", sv_b2)
                                W(v, "lh", m1_lh)
                                W(v, "lh2", m1_lh2)
                                W(v, "st", m1_st)
                                v.drain()
                                for i in range(gn):
                                    fc = fg + i
                                    v.tensor_scalar(
                                        hq32[:, i * TT:(i + 1) * TT],
                                        hin[hb][:, fc * TT:(fc + 1) * TT],
                                        invs_col[:, :], MAGIC, ALU.mult, ALU.add)
                                v.drain()
                                for i in range(gn):
                                    fc = fg + i
                                    cinc(v.tensor_scalar(
                                        hin[hb][:, fc * TT:(fc + 1) * TT],
                                        hq32[:, i * TT:(i + 1) * TT], MAGIC,
                                        255.0, ALU.subtract, ALU.min), "sv")
                        sv_hq[tt][half] = counts["sv"]

                    def emit_mm(tt, half):
                        f0, f1_ = half * NH, (half + 1) * NH
                        for c in range(NCC):
                            @blk.tensor
                            def _(t, tt=tt, c=c, half=half, f0=f0, f1_=f1_):
                                W(t, "tw2", tw2_hold[0])
                                W(t, "sv", max(sv_hq[tt][half], evac_sv2[c]))
                                last = None
                                for fc in range(f0, f1_):
                                    last = t.matmul(
                                        psumC[c][:, :],
                                        w2slab[fc][:, c * P:(c + 1) * P],
                                        hin[hb][:, fc * TT:(fc + 1) * TT],
                                        start=(fc == 0), stop=(fc == NFC - 1))
                                cinc(last, "st")
                        st_half[tt][half] = counts["st"]
                        if tt + 1 <= NTT - 1:
                            half_free_st[tt + 1][half] = counts["st"]

                    def emit_evacs(tt):
                        for c in range(NCC):
                            @blk.vector
                            def _(v, c=c, tt=tt):
                                W(v, "st", st_half[tt][NHALF - 1])
                                W(v, spill_done[c][0], spill_done[c][1])
                                v.drain()
                                evac_sv2[c] = cinc(
                                    v.tensor_scalar(f1bf[c][:, :], psumC[c][:, :],
                                                    sprod2h[:, c:c + 1],
                                                    bq2fh[:, c:c + 1],
                                                    ALU.mult, ALU.add), "sv")

                            fs = "lf" if c % 2 == 0 else "lf2"

                            @blk.sync
                            def _(sy, tt=tt, c=c, fs=fs):
                                W(sy, "sv", evac_sv2[c])
                                dma(sy, fs,
                                    F1_d[c * P:(c + 1) * P, tt * TT:(tt + 1) * TT],
                                    f1bf[c][:, :])
                            spill_done[c] = (fs, counts[fs])
                        lf_tile.append((counts["lf"], counts["lf2"]))

                    for tt in range(NTT):
                        th0 = emit_load(tt, 0)
                        emit_quant(tt, 0, th0)
                        if tt > 0:
                            emit_evacs(tt - 1)
                        emit_mm(tt, 0)
                        if NHALF > 1:
                            th1 = emit_load(tt, 1)
                            emit_quant(tt, 1, th1)
                            emit_mm(tt, 1)
                    emit_evacs(NTT - 1)

                add_done = [0, 0]
                for k in range(NTC):
                    b = k % 2
                    lfv, lf2v = lf_tile[min(k // (NTC // NTT), NTT - 1)]

                    @blk.scalar
                    def _(s, k=k, b=b, lfv=lfv, lf2v=lf2v):
                        W(s, "lf", lfv)
                        W(s, "lf2", lf2v)
                        W(s, "sv", add_done[b])
                        dma(s, "tf", f1tm[b][:, :], F1_d[:, k * P:(k + 1) * P],
                            transpose=True)
                    tf_k = counts["tf"]

                    @blk.vector
                    def _(v, k=k, b=b):
                        W(v, "tf", tf_k)
                        v.drain()
                        add_done[b] = cinc(
                            v.tensor_add(X[:, k * C:(k + 1) * C],
                                         X[:, k * C:(k + 1) * C],
                                         f1tm[b][:, :]), "sv")
                    if app == 1:
                        @blk.sync
                        def _(sy, k=k, val=add_done[b]):
                            W(sy, "sv", val)
                            dma(sy, "lxo",
                                out_d.rearrange("(j p) c -> p j c", p=P)
                                     [:, k:k + 1, :],
                                X[:, k * C:(k + 1) * C]
                                    .rearrange("p (j c) -> p j c", c=C))
        fctx.close()
        barrier()
        return counts["sv"]

    def attn():
        sv0 = ln_stats(want_max=False)

        with ExitStack() as mctx:
            ln2bf = [mctx.enter_context(nc.sbuf_tensor(f"ln2bf{i}", [P, C], BF16))
                     for i in range(2)]
            lnfa = mctx.enter_context(
                nc.sbuf_tensor("lnfa", [P, NCC * TPC], BF16))

            @blk.vector
            def _(v):
                W(v, "sv", sv0)
                v.drain()
                v.tensor_mul(bcol_t[:, :], mvt[:, 0::2], rstd_t[:, :])
                v.drain()
                cinc(v.tensor_scalar(bcol_t[:, :], bcol_t[:, :], -1.0, None,
                                     ALU.mult), "sv")
            sv_lb = counts["sv"]

            tp6 = [0, 0]
            ev6 = [0, 0]
            with nc.psum_tensor("ptl", [P, 2 * C], BF16) as ptl:
                for k in range(NTC):
                    b = k % 2

                    @blk.scalar
                    def _(s, k=k, b=b):
                        W(s, "sv", sv_lb)
                        W(s, "st", tp6[b])
                        s.drain()
                        cinc(s.activation(ln2bf[b][:, :], X[:, k * C:(k + 1) * C],
                                          AF.Identity, bias=bcol_t[:, k:k + 1],
                                          scale=rstd_t[:, k:k + 1]), "ss")
                    ss_l = counts["ss"]

                    @blk.tensor
                    def _(t, k=k, b=b, ss_l=ss_l):
                        W(t, "ss", ss_l)
                        W(t, "sv", ev6[b])
                        last = None
                        for c in range(NCC):
                            last = t.transpose(
                                ptl[:, b * C + c * P:b * C + (c + 1) * P],
                                ln2bf[b][:, c * P:(c + 1) * P], idn[:, :])
                        tp6[b] = cinc(last, "st")

                    @blk.vector
                    def _(v, k=k, b=b):
                        W(v, "st", tp6[b])
                        v.drain()
                        ev6[b] = cinc(v.tensor_copy(
                            lnfa[:, :].rearrange("p (c t) -> p c t", t=TPC)
                                      [:, :, k * P:(k + 1) * P],
                            ptl[:, b * C:(b + 1) * C]
                                .rearrange("p (c w) -> p c w", w=P)), "sv")
            ln_done = counts["sv"]

            if True:
                SLAB6 = min(512, 6 * C)
                NSL = SLAB6 // P
                rksl = [[mctx.enter_context(
                    nc.sbuf_tensor(f"rksl{j}_{i}", [P, SLAB6], BF16))
                    for i in range(NCC)] for j in range(2)]
                khold = [mctx.enter_context(nc.sbuf_tensor(f"khold{i}", [P, TPC], BF16))
                         for i in range(NCC)]
                vtile = [mctx.enter_context(nc.sbuf_tensor(f"vtile{i}", [P, TPC], BF16))
                         for i in range(2)]
                iota_a = mctx.enter_context(nc.sbuf_tensor("iota_a", [P, TPC], F32))
                revd = mctx.enter_context(nc.sbuf_tensor("revd", [P, TPC], F32))
                et = mctx.enter_context(nc.sbuf_tensor("et", [P, TPC], F32))
                psum6 = [mctx.enter_context(nc.psum_tensor(f"psum6_{i}", [P, TPC], F32))
                         for i in range(2)]

                @blk.gpsimd
                def _(g):
                    cinc(g.iota(iota_a[:, :], [[1, TPC]], channel_multiplier=0,
                                allow_small_or_imprecise_dtypes=True), "sp")
                sp_iota = counts["sp"]

                NC3 = 3 * NCC
                order = []
                for c in range(NCC):
                    order.append(("r", 0, c, c))
                for c in range(NCC):
                    order.append(("r", 1, c, NC3 + c))
                for c in range(NCC):
                    order.append(("k", 0, c, NCC + c))
                    order.append(("v", 0, c, 2 * NCC + c))
                for c in range(NCC):
                    order.append(("k", 1, c, NC3 + NCC + c))
                    order.append(("v", 1, c, NC3 + 2 * NCC + c))

                mm6 = [0, 0]
                evac6 = [{"ss": 0, "sv": 0}, {"ss": 0, "sv": 0}]

                windows = []
                for idx, (kind, dr, c, d6) in enumerate(order):
                    s_ = d6 // NSL
                    if windows and windows[-1][1] == s_:
                        continue
                    windows.append((idx, s_))
                resident = {}
                wbuf, wload = [], []
                prev_b = 1
                for widx, (start, s_) in enumerate(windows):
                    if s_ in resident:
                        wbuf.append(resident[s_])
                        wload.append(False)
                    else:
                        b_ = 1 - prev_b
                        for sid in [k2 for k2, v2 in resident.items() if v2 == b_]:
                            del resident[sid]
                        resident[s_] = b_
                        wbuf.append(b_)
                        wload.append(True)
                    prev_b = wbuf[-1]
                for widx, (start, s_) in enumerate(windows):
                    if not wload[widx]:
                        assert any(windows[j][1] == s_ and wload[j]
                                   for j in range(widx))
                buf_last_st = [0, 0]
                lg_of_buf = [0, 0]

                def load_slab(widx):
                    s_, b_ = windows[widx][1], wbuf[widx]
                    bufs = rksl[b_]

                    @blk.gpsimd
                    def _(g, s_=s_, bufs=bufs, b_=b_):
                        W(g, "st", buf_last_st[b_])
                        if counts["lg"] > 0:
                            g.wait_ge(sems["lg"], counts["lg"])
                        for cc in range(NCC):
                            dma(g, "lg", bufs[cc][:, :],
                                rkvT_in[cc * P:(cc + 1) * P,
                                        s_ * SLAB6:(s_ + 1) * SLAB6],
                                serial=False)
                    lg_of_buf[b_] = counts["lg"]

                load_slab(0)
                cur_w = -1
                for idx, (kind, dr, c, d6) in enumerate(order):
                    slab_id = d6 // NSL
                    if cur_w + 1 < len(windows) and windows[cur_w + 1][0] == idx:
                        cur_w += 1
                        if cur_w + 1 < len(windows) and wload[cur_w + 1]:
                            load_slab(cur_w + 1)

                    pb = idx % 2
                    my_b = wbuf[cur_w]
                    lg_slab, my_slabs = lg_of_buf[my_b], rksl[my_b]
                    off = (d6 % NSL) * P

                    @blk.tensor
                    def _(t, pb=pb, off=off, my_slabs=my_slabs, lg_slab=lg_slab):
                        W(t, "sv", ln_done)
                        W(t, "lg", lg_slab)
                        W(t, "ss", evac6[pb]["ss"])
                        W(t, "sv", evac6[pb]["sv"])
                        last = None
                        for cc in range(NCC):
                            for tt in range(NTT):
                                last = t.matmul(psum6[pb][:, tt * TT:(tt + 1) * TT],
                                                my_slabs[cc][:, off:off + P],
                                                lnfa[:, cc * TPC + tt * TT:
                                                     cc * TPC + (tt + 1) * TT],
                                                start=(cc == 0), stop=(cc == NCC - 1))
                        mm6[pb] = cinc(last, "st")
                    buf_last_st[my_b] = counts["st"]

                    if kind == "r":
                        @blk.scalar
                        def _(s, pb=pb):
                            W(s, "st", mm6[pb])
                            W(s, "lk", counts["lk"])
                            W(s, "sv", counts["sv"])
                            s.drain()
                            evac6[pb]["ss"] = cinc(
                                s.activation(vtile[pb][:, :], psum6[pb][:, :],
                                             AF.Sigmoid), "ss")
                        ss_e = counts["ss"]

                        @blk.sync
                        def _(sy, d6=d6, pb=pb):
                            W(sy, "ss", ss_e)
                            dma(sy, "lk", KVR_d[d6 * P:(d6 + 1) * P, :], vtile[pb][:, :])

                    elif kind == "k":
                        @blk.vector
                        def _(v, c=c, pb=pb):
                            W(v, "st", mm6[pb])
                            W(v, "lk", counts["lk"])
                            v.drain()
                            evac6[pb]["sv"] = cinc(
                                v.tensor_copy(khold[c][:, :], psum6[pb][:, :]), "sv")
                        sv_e = counts["sv"]

                        @blk.sync
                        def _(sy, d6=d6, c=c):
                            W(sy, "sv", sv_e)
                            dma(sy, "lk", KVR_d[d6 * P:(d6 + 1) * P, :], khold[c][:, :])

                    else:
                        @blk.vector
                        def _(v, pb=pb):
                            W(v, "st", mm6[pb])
                            W(v, "lk", counts["lk"])
                            W(v, "ss", counts["ss"])
                            v.drain()
                            evac6[pb]["sv"] = cinc(
                                v.tensor_copy(vtile[pb][:, :], psum6[pb][:, :]), "sv")
                        sv_e = counts["sv"]

                        @blk.sync
                        def _(sy, d6=d6, pb=pb):
                            W(sy, "sv", sv_e)
                            dma(sy, "lk", KVR_d[d6 * P:(d6 + 1) * P, :], vtile[pb][:, :])

                        @blk.vector
                        def _(v, dr=dr, c=c):
                            W(v, "sp", sp_iota)
                            W(v, "ss", counts["ss"])
                            v.drain()
                            wc = wf_col if dr == 0 else wb_col
                            isrc = iota_a[:, ::-1] if dr == 0 else iota_a[:, :]
                            cinc(v.scalar_tensor_tensor(
                                et[:, :], isrc, wc[:, c:c + 1], khold[c][:, :],
                                ALU.mult, ALU.add), "sv")
                        sv_t1 = counts["sv"]

                        @blk.scalar
                        def _(s, dr=dr, c=c):
                            W(s, "sv", sv_t1)
                            s.drain()
                            cinc(s.activation(et[:, :], et[:, :], AF.Exp,
                                              accum_out=scol[:, (2 * dr + 1) * NCC + c:
                                                              (2 * dr + 1) * NCC + c + 1]),
                                 "ss")
                        ss_et = counts["ss"]

                        @blk.vector
                        def _(v, dr=dr, c=c, pb=pb):
                            W(v, "ss", ss_et)
                            v.drain()
                            cinc(v.scalar_tensor_tensor(
                                revd[:, :], et[:, :], 1.0, vtile[pb][:, :],
                                ALU.mult, ALU.mult,
                                accum_out=scol[:, (2 * dr) * NCC + c:
                                               (2 * dr) * NCC + c + 1]), "sv")

                lk_all = counts["lk"]
                sv_sum = counts["sv"]
                ss_sum = counts["ss"]
                m3_st = counts["st"]

        if True:
            ccn = CC_SEMS[cc_used[0]]
            cc_used[0] += 1

            @blk.gpsimd
            def _(g):
                W(g, "sv", sv_sum)
                W(g, "ss", ss_sum)
                dma(g, "lg", scol_i, scol[:, :])
                g.wait_ge(sems["lg"], counts["lg"])
                counts[ccn] += 1
                g.collective_compute(
                    "AllGather", ALU.bypass, replica_groups=RG,
                    ins=[scol_i], outs=[scol_o]).then_inc(sems[ccn])
                g.wait_ge(sems[ccn], counts[ccn])

            with ExitStack() as sctx:
                Gt = sctx.enter_context(nc.sbuf_tensor("Gt", [P, NCORES * NK], F32))
                coefa = sctx.enter_context(nc.sbuf_tensor("coefa", [P, NCORES * NK], F32))
                kbuf = sctx.enter_context(nc.sbuf_tensor("kbuf", [P, TPC], BF16))
                vbuf = sctx.enter_context(nc.sbuf_tensor("vbuf", [P, TPC], BF16))
                sgbuf = [sctx.enter_context(nc.sbuf_tensor(f"sgbuf{i}", [P, TPC], BF16))
                         for i in range(2)]
                ekt = [sctx.enter_context(nc.sbuf_tensor(f"ekt{i}", [P, TPC], F32))
                       for i in range(2)]
                d1t = [sctx.enter_context(nc.sbuf_tensor(f"d1t{i}", [P, TPC], F32))
                       for i in range(2)]
                Apad = [sctx.enter_context(nc.sbuf_tensor(f"Apad{i}", [P, TPC + 1], F32))
                        for i in range(2)]
                Bpad = [sctx.enter_context(nc.sbuf_tensor(f"Bpad{i}", [P, TPC + 1], F32))
                        for i in range(2)]
                obuf = sctx.enter_context(nc.sbuf_tensor("obuf", [P, NCC * TPC], BF16))
                owt = [sctx.enter_context(nc.sbuf_tensor(f"owt{i}", [P, C], BF16))
                       for i in range(NCC)]
                psum4 = [sctx.enter_context(nc.psum_tensor(f"psum4_{i}", [P, C], F32))
                         for i in range(2)]

                @blk.gpsimd
                def _(g):
                    W(g, "st", m3_st)
                    dma(g, "lg", Gt[:, :].rearrange("p (j k) -> p j k", k=NK),
                        scol_o.rearrange("(j p) k -> p j k", p=P))
                lg_g = counts["lg"]

                @blk.vector
                def _(v):
                    W(v, "lg", lg_g)
                    W(v, "ss", ss_sum)
                    v.drain()
                    for j in range(NCORES):
                        v.tensor_mul(coefa[:, j * NK:(j + 1) * NK], W32[:, :],
                                     pfx_t[:, j * NK:(j + 1) * NK])
                    cinc(v.drain(), "sv")
                sv_pc = counts["sv"]

                @blk.scalar
                def _(s):
                    W(s, "sv", sv_pc)
                    s.drain()
                    cinc(s.activation(coefa[:, :], coefa[:, :], AF.Exp), "ss")
                ss_pc = counts["ss"]

                @blk.vector
                def _(v):
                    W(v, "ss", ss_pc)
                    v.drain()
                    v.tensor_mul(coefa[:, :], coefa[:, :], Gt[:, :])
                    v.drain()
                    for j in range(1, NCORES):
                        v.tensor_add(coefa[:, 0:NK], coefa[:, 0:NK],
                                     coefa[:, j * NK:(j + 1) * NK])
                        v.drain()
                    cinc(v.tensor_copy(Ain[:, :], coefa[:, 0:NK]), "sv")
                sv_ain = counts["sv"]

                ek_ss = {}
                d1_sp = {}
                glob_sv = {}
                glob_ss = {}
                o_sv = {}
                kv_tk = {}
                sg_lk = {}
                Ash_v = {}
                Bsh_v = {}
                dirprev_st = {}
                pend = []
                NCH = 2 * NCC
                dirprev_box = [m3_st]

                def chunk_bases(g):
                    dr_, c_ = divmod(g, NCC)
                    return ((3 * dr_ + 0) * C + c_ * P,
                            (3 * dr_ + 1) * C + c_ * P,
                            (3 * dr_ + 2) * C + c_ * P)

                def emit_kvload(g):
                    if g >= NCH:
                        return
                    _, base_k, base_v = chunk_bases(g)

                    @blk.scalar
                    def _(s, g=g, base_k=base_k, base_v=base_v):
                        W(s, "lk", lk_all)
                        W(s, "ss", ek_ss.get(g - 1, 0))
                        W(s, "sp", d1_sp.get(g - 1, 0))
                        if counts["tk"] > 0:
                            s.wait_ge(sems["tk"], counts["tk"])
                        dma(s, "tk", kbuf[:, :], KVR_d[base_k:base_k + P, :],
                            serial=False)
                        dma(s, "tk", vbuf[:, :], KVR_d[base_v:base_v + P, :],
                            serial=False)
                    kv_tk[g] = counts["tk"]

                def emit_sgload(g):
                    if g >= NCH:
                        return
                    base_r, _, _ = chunk_bases(g)
                    w = g % 2

                    @blk.sync
                    def _(sy, g=g, w=w, base_r=base_r):
                        W(sy, "sv", o_sv.get(g - 2, 0))
                        dma(sy, "lk", sgbuf[w][:, :],
                            KVR_d[base_r:base_r + P, :])
                    sg_lk[g] = counts["lk"]

                def emit_ek(g):
                    w = g % 2

                    @blk.scalar
                    def _(s, g=g, w=w):
                        W(s, "tk", kv_tk[g])
                        W(s, "sv", glob_sv.get(g - 2, 0))
                        s.drain()
                        cinc(s.activation(ekt[w][:, :], kbuf[:, :], AF.Exp),
                             "ss")
                    ek_ss[g] = counts["ss"]

                def emit_d1(g):
                    w = g % 2

                    @blk.gpsimd
                    def _(gp, g=g, w=w):
                        W(gp, "ss", ek_ss[g])
                        W(gp, "sv", glob_sv.get(g - 2, 0))
                        gp.drain()
                        cinc(gp.tensor_mul(d1t[w][:, :], ekt[w][:, :],
                                           vbuf[:, :]), "sp")
                    d1_sp[g] = counts["sp"]

                def emit_block(g):
                    dr_, c_ = divmod(g, NCC)
                    w = g % 2
                    ew = ewf_col if dr_ == 0 else ewb_col
                    eu = euf_col if dr_ == 0 else eub_col

                    @blk.vector
                    def _(v, g=g, w=w, dr=dr_, c=c_, ew=ew, eu=eu):
                        W(v, "sp", d1_sp[g])
                        W(v, "sv", sv_ain)
                        v.drain()
                        kA = (2 * dr) * NCC + c
                        kB = (2 * dr + 1) * NCC + c
                        d0ap = ew[:, c:c + 1].broadcast_to([P, TPC])
                        if dr == 0:
                            v.tensor_copy(Apad[w][:, 0:1], Ain[:, kA:kA + 1])
                            v.tensor_copy(Bpad[w][:, 0:1], Ain[:, kB:kB + 1])
                            v.drain()
                            v.tensor_tensor_scan(Apad[w][:, 1:TPC + 1], d0ap,
                                                 d1t[w][:, :], Ain[:, kA:kA + 1],
                                                 ALU.mult, ALU.add)
                            v.tensor_tensor_scan(Bpad[w][:, 1:TPC + 1], d0ap,
                                                 ekt[w][:, :], Ain[:, kB:kB + 1],
                                                 ALU.mult, ALU.add)
                            Ash = Apad[w][:, 0:TPC]
                            Bsh = Bpad[w][:, 0:TPC]
                        else:
                            v.tensor_copy(Apad[w][:, TPC:TPC + 1],
                                          Ain[:, kA:kA + 1])
                            v.tensor_copy(Bpad[w][:, TPC:TPC + 1],
                                          Ain[:, kB:kB + 1])
                            v.drain()
                            v.tensor_tensor_scan(Apad[w][:, TPC - 1::-1], d0ap,
                                                 d1t[w][:, ::-1],
                                                 Ain[:, kA:kA + 1],
                                                 ALU.mult, ALU.add)
                            v.tensor_tensor_scan(Bpad[w][:, TPC - 1::-1], d0ap,
                                                 ekt[w][:, ::-1],
                                                 Ain[:, kB:kB + 1],
                                                 ALU.mult, ALU.add)
                            Ash = Apad[w][:, 1:TPC + 1]
                            Bsh = Bpad[w][:, 1:TPC + 1]
                        Ash_v[g] = Ash
                        Bsh_v[g] = Bsh
                        v.drain()
                        v.scalar_tensor_tensor(Ash, d1t[w][:, :],
                                               eu[:, c:c + 1], Ash,
                                               ALU.mult, ALU.add)
                        cinc(v.scalar_tensor_tensor(Bsh, ekt[w][:, :],
                                                    eu[:, c:c + 1], Bsh,
                                                    ALU.mult, ALU.add), "sv")
                    glob_sv[g] = counts["sv"]

                def emit_tail(g, w):
                    @blk.scalar
                    def _(s, g=g, w=w):
                        W(s, "sv", glob_sv[g])
                        s.drain()
                        s.activation(Bsh_v[g], Bsh_v[g], AF.Ln)
                        s.drain()
                        cinc(s.activation(Bsh_v[g], Bsh_v[g], AF.Exp,
                                          scale=-1.0), "ss")
                    glob_ss[g] = counts["ss"]

                    @blk.vector
                    def _(v, g=g, w=w):
                        W(v, "ss", glob_ss[g])
                        W(v, "lk", sg_lk[g])
                        W(v, "st", dirprev_st[g])
                        v.drain()
                        v.tensor_mul(Ash_v[g], Ash_v[g], Bsh_v[g])
                        v.drain()
                        cinc(v.tensor_mul(
                            obuf[:, (g % NCC) * TPC:(g % NCC + 1) * TPC],
                            Ash_v[g], sgbuf[w][:, :]), "sv")
                    o_sv[g] = counts["sv"]

                m4_lxo = [0, 0]
                for dr in range(2):
                    if dr == 0:
                        emit_kvload(0)
                        emit_sgload(0)
                        emit_sgload(1)

                    for c in range(NCC):
                        g_ = dr * NCC + c
                        dirprev_st[g_] = dirprev_box[0]
                        emit_ek(g_)
                        emit_d1(g_)
                        emit_kvload(g_ + 1)
                        emit_block(g_)
                        pend.append((g_, g_ % 2))
                        if len(pend) > 1:
                            emit_tail(*pend.pop(0))
                            emit_sgload(g_ + 1)
                    while pend:
                        gl, wl = pend.pop(0)
                        emit_tail(gl, wl)
                        emit_sgload(gl + 2)

                    @blk.gpsimd
                    def _(g, dr=dr):
                        W(g, "st", dirprev_box[0])
                        W(g, "ss", ss_sum)
                        W(g, "sv", sv_ain)
                        if counts["lg"] > 0:
                            g.wait_ge(sems["lg"], counts["lg"])
                        for cc in range(NCC):
                            r0 = dr * C + cc * P
                            dma(g, "lg", owt[cc][:, :], owT_in[r0:r0 + P, :],
                                serial=False)
                    lg_ow = counts["lg"]

                    m4_evac = [0, 0]
                    for k in range(NTC):
                        pb = k % 2

                        @blk.tensor
                        def _(t, dr=dr, k=k, pb=pb, lg_ow=lg_ow):
                            W(t, "sv", max(o_sv[dr * NCC + NCC - 1],
                                           m4_evac[pb]))
                            W(t, "lg", lg_ow)
                            last = None
                            w4 = min(TT, C)
                            for cc in range(NCC):
                                lhs = obuf[:, cc * TPC + k * P:
                                           cc * TPC + (k + 1) * P]
                                for h in range(C // w4):
                                    last = t.matmul(
                                        psum4[pb][:, h * w4:(h + 1) * w4],
                                        lhs, owt[cc][:, h * w4:(h + 1) * w4],
                                        start=(cc == 0), stop=(cc == NCC - 1))
                            cinc(last, "st")
                        st_m4 = counts["st"]

                        @blk.vector
                        def _(v, dr=dr, k=k, pb=pb, st_m4=st_m4):
                            W(v, "st", st_m4)
                            v.drain()
                            src_ = X[:, k * C:(k + 1) * C]
                            m4_evac[pb] = cinc(
                                v.tensor_add(src_, src_, psum4[pb][:, :]), "sv")
                            if dr == 1:
                                nh = max(1, C // 512)
                                step = C // nh
                                for h in range(nh):
                                    v.bn_stats(tmp_col2[:, 6 * h:6 * h + 6],
                                               src_[:, h * step:(h + 1) * step])
                                v.drain()
                                v.bn_aggr(mvt[:, 2 * k:2 * k + 2],
                                          tmp_col2[:, 0:6 * nh])
                                v.drain()
                                v.tensor_reduce(rmax_t[:, k:k + 1], src_,
                                                AX.X, ALU.max)
                                cinc(v.drain(), "sv")
                    dirprev_box[0] = counts["st"]
        stats_box[0] = counts["sv"]

        barrier()
        return counts["sv"]

    if 'f' in phases:
        ffn_app(0)
    if 'a' in phases:
        attn()
    if phases == 'faf':
        ffn_app(1)

    sv_final = counts["sv"]

    if phases == 'faf':
        lxo_out = counts["lxo"]

        @blk.gpsimd
        def _(g):
            g.wait_ge(sems["lxo"], lxo_out)
    else:
        @blk.sync
        def _(sy):
            W(sy, "sv", sv_final)
            xr = out_d.rearrange("(k p) c -> p k c", p=P)
            g = max(1, NTC // 4)
            for i in range(0, NTC, g):
                j = min(NTC, i + g)
                dma(sy, "lx", xr[:, i:j, :],
                    X[:, i * C:j * C].rearrange("p (k c) -> p k c", c=C))
        lx_out = counts["lx"]

        @blk.gpsimd
        def _(g):
            g.wait_ge(sems["lx"], lx_out)

    return nc


def _prep_inputs(cfg, L0_attn_x, L0_attn_rkv_w, L0_attn_out_w, L0_attn_time_decay,
                 L0_attn_time_first, L0_attn_time_decay_rev, L0_attn_time_first_rev,
                 L0_ffn_w1, L0_ffn_b1, L0_ffn_w2, L0_ffn_b2):
    TPC, C, NCC = cfg.TPC, cfg.C, cfg.NCC
    NK = 4 * NCC
    B, T, _ = L0_attn_x.shape
    cpb = NCORES // B
    assert T == cpb * TPC

    xf = np.ascontiguousarray(np.asarray(L0_attn_x, np.float32).reshape(B * T, C))
    rkvT = np.ascontiguousarray(np.asarray(L0_attn_rkv_w, np.float32).T)
    owT = np.ascontiguousarray(np.asarray(L0_attn_out_w, np.float32).T)

    def colform(a, n):
        return np.ascontiguousarray(np.asarray(a, np.float32).reshape(n, P).T)

    tdec = colform(L0_attn_time_decay, NCC)
    tfir = colform(L0_attn_time_first, NCC)
    tdecr = colform(L0_attn_time_decay_rev, NCC)
    tfirr = colform(L0_attn_time_first_rev, NCC)
    b1c = colform(L0_ffn_b1, cfg.NFC)
    b2c = colform(L0_ffn_b2, NCC)
    w1 = np.ascontiguousarray(np.asarray(L0_ffn_w1, np.float32))
    w2 = np.ascontiguousarray(np.asarray(L0_ffn_w2, np.float32))
    w1T = np.ascontiguousarray(w1.T)
    w2T = np.ascontiguousarray(w2.T)
    F = 4 * C
    SH1 = F // NCORES
    SH2 = C // NCORES

    in_maps = []
    for m in range(NCORES):
        mb = m % cpb
        bix = m // cpb
        pfx = np.full((P, NCORES * NK), 1e35, np.float32)
        for j in range(NCORES):
            jb = j // cpb
            jmb = j % cpb
            if jb != bix:
                continue
            if jmb < mb:
                val = float(TPC * (mb - 1 - jmb))
                pfx[:, j * NK + 0 * NCC:j * NK + 2 * NCC] = val
            if jmb > mb:
                val = float(TPC * (jmb - mb - 1))
                pfx[:, j * NK + 2 * NCC:j * NK + 4 * NCC] = val
        in_maps.append(dict(
            x=np.ascontiguousarray(xf[m * TPC:(m + 1) * TPC]),
            w1s=np.ascontiguousarray(w1[m * SH1:(m + 1) * SH1, :]),
            w2s=np.ascontiguousarray(w2[m * SH2:(m + 1) * SH2, :]),
            w1Ts=np.ascontiguousarray(w1T[:, m * SH1:(m + 1) * SH1]),
            w2Ts=np.ascontiguousarray(w2T[:, m * SH2:(m + 1) * SH2]),
            b1=b1c, b2=b2c, rkvT=rkvT, owT=owT,
            tdec=tdec, tfir=tfir, tdecr=tdecr, tfirr=tfirr, pfx=pfx,
        ))
    return in_maps


def kernel(L0_attn_x, L0_attn_rkv_w, L0_attn_out_w, L0_attn_time_decay,
           L0_attn_time_first, L0_attn_time_decay_rev, L0_attn_time_first_rev,
           L0_ffn_w1, L0_ffn_b1, L0_ffn_w2, L0_ffn_b2):
    cfg = CFG_REAL
    in_maps = _prep_inputs(cfg, L0_attn_x, L0_attn_rkv_w, L0_attn_out_w,
                           L0_attn_time_decay, L0_attn_time_first,
                           L0_attn_time_decay_rev, L0_attn_time_first_rev,
                           L0_ffn_w1, L0_ffn_b1, L0_ffn_w2, L0_ffn_b2)
    nc = build_nc(cfg)
    res = run_bass_kernel_spmd(nc, in_maps, core_ids=list(range(NCORES)))
    B, T, C = L0_attn_x.shape
    out = np.concatenate([res.results[m]["out"] for m in range(NCORES)], axis=0)
    return out.reshape(B, T, C).astype(np.float32)
